# version 01
# baseline (speedup 1.0000x reference)
import numpy as np
import ml_dtypes
from contextlib import ExitStack
import concourse.bass as bass
import concourse.mybir as mybir
from concourse.bass_utils import run_bass_kernel_spmd

F32 = mybir.dt.float32
BF16 = mybir.dt.bfloat16
AF = mybir.ActivationFunctionType
ALU = mybir.AluOpType
AX = mybir.AxisListType
BFNP = ml_dtypes.bfloat16
MAGIC = 12582912.0

B = 2
T = 2048
D = 1024
H = 16
HD = 64
NT = B * T
E_EXP = 8
NCORES = 8


class Ctr:
    def __init__(self, ctx, nc, name):
        self.h = ctx.enter_context(nc.semaphore(name))
        self.v = 0
    def take(self, n):
        base = self.v
        self.v += n
        return base


def mkop(eng, sem, cursor):
    def op(fn, *a, **k):
        eng.wait_ge(sem.h, cursor[0])
        inst = getattr(eng, fn)(*a, **k)
        inst.then_inc(sem.h, 1)
        cursor[0] += 1
        return inst
    return op



def build_l0attn():
    nc = bass.Bass("TRN2", target_bir_lowering=False, debug=False)
    x0 = nc.dram_tensor("x0", [NT, D], F32, kind="ExternalInput").ap()
    wg = nc.dram_tensor("wg", [D, E_EXP], F32, kind="ExternalInput").ap()
    x1 = nc.dram_tensor("x1", [NT, D], F32, kind="ExternalOutput").ap()
    meta = nc.dram_tensor("meta", [NT, 4], F32, kind="ExternalOutput").ap()

    ident = nc.inline_tensor(np.eye(128, dtype=BFNP), name="identbf")
    e2np = np.zeros((128, 2), dtype=BFNP)
    e2np[0:64, 0] = 1
    e2np[64:128, 1] = 1
    E2c = nc.inline_tensor(e2np, name="E2")
    iota8 = nc.inline_tensor(np.tile(np.arange(8, dtype=np.float32), (128, 1)),
                             name="iota8")
    epsc = nc.inline_tensor(np.full((128, 1), 1e-5, dtype=np.float32), name="epsc")

    with ExitStack() as ctx:
        En = ctx.enter_context
        XT = En(nc.sbuf_tensor("XT", [128, 8 * T], BF16))
        XBF = En(nc.sbuf_tensor("XBF", [128, 16 * 1024], BF16))
        BUFA = En(nc.sbuf_tensor("BUFA", [128, 16 * 1024], BF16))
        ATTN = En(nc.sbuf_tensor("ATTN", [128, 16 * 1024], BF16))
        GH = En(nc.sbuf_tensor("GH", [128, 1024], BF16))
        Z = En(nc.sbuf_tensor("Z", [128, 1024], BF16))
        NEGU = En(nc.sbuf_tensor("NEGU", [128, 256], F32))
        NEGUBF = En(nc.sbuf_tensor("NEGUBF", [128, 256], BF16))
        NEGUFM = En(nc.sbuf_tensor("NEGUFM", [16, T], F32))
        NEGC = En(nc.sbuf_tensor("NEGC", [16, 1], F32))
        SUFM = En(nc.sbuf_tensor("SUFM", [16, T], BF16))
        SUTM = En(nc.sbuf_tensor("SUTM", [128, 256], BF16))
        SUREP = En(nc.sbuf_tensor("SUREP", [128, 1024], BF16))
        ACCT = En(nc.sbuf_tensor("ACCT", [128, 128], F32))
        FS = En(nc.sbuf_tensor("FS", [128, 32], F32))
        FST = En(nc.sbuf_tensor("FST", [128, 32], F32))
        XRES = En(nc.sbuf_tensor("XRES", [128, 2 * 1024], F32))
        X1O = En(nc.sbuf_tensor("X1O", [128, 2 * 1024], F32))
        SQS = En(nc.sbuf_tensor("SQS", [128, 1024], BF16))
        LNS = En(nc.sbuf_tensor("LNS", [128, 8], F32))
        WGB = En(nc.sbuf_tensor("WGB", [128, 64], BF16))
        PRB = En(nc.sbuf_tensor("PRB", [128, 8], F32))
        GSM = En(nc.sbuf_tensor("GSM", [128, 24], F32))
        MT = En(nc.sbuf_tensor("MT", [128, 4], F32))
        IDS = En(nc.sbuf_tensor("IDS", [128, 128], BF16))
        E2S = En(nc.sbuf_tensor("E2S", [128, 2], BF16))
        EPS = En(nc.sbuf_tensor("EPS", [128, 1], F32))
        IO8 = En(nc.sbuf_tensor("IO8", [128, 8], F32))

        psQKa = En(nc.psum_tensor("psQKa", [128, 512], F32))
        psQKb = En(nc.psum_tensor("psQKb", [128, 512], F32))
        psAV0 = En(nc.psum_tensor("psAV0", [128, 1024], F32))
        psAV1 = En(nc.psum_tensor("psAV1", [128, 1024], F32))
        psT = En(nc.psum_tensor("psT", [128, 1024], BF16))
        psU = psT[:, 0:32].bitcast(F32)
        psF = psAV0[0:16, :].bitcast(BF16)
        psSU = psT[:, 0:256]
        psGate = psQKa[:, 0:8]

        s_ld = Ctr(ctx, nc, "s_ld")
        s_gd = Ctr(ctx, nc, "s_gd")
        s_pe = Ctr(ctx, nc, "s_pe")
        s_sc = Ctr(ctx, nc, "s_sc")
        s_dv = Ctr(ctx, nc, "s_dv")
        s_st = Ctr(ctx, nc, "s_st")
        s_x = [Ctr(ctx, nc, "s_x0"), Ctr(ctx, nc, "s_x1")]
        s_wg = Ctr(ctx, nc, "s_wg")
        s_y = [Ctr(ctx, nc, "s_y0"), Ctr(ctx, nc, "s_y1")]
        block = En(nc.Block())

        ld0 = s_ld.take(64)

        @block.sync
        def _(sync):
            sync.dma_start(out=IDS[:], in_=ident.ap()).then_inc(s_ld.h, 16)
            sync.dma_start(out=E2S[:], in_=E2c.ap()).then_inc(s_ld.h, 16)
            sync.dma_start(out=IO8[:], in_=iota8.ap()).then_inc(s_ld.h, 16)
            sync.dma_start(out=EPS[:], in_=epsc.ap()).then_inc(s_ld.h, 16)

        gd_wg = s_wg.take(16)

        @block.gpsimd
        def _(g):
            g.dma_start(
                out=WGB[:].rearrange("p (i e) -> p i e", i=8),
                in_=bass.AP(wg.tensor, 0, [[8, 128], [1024, 8], [1, 8]])
            ).then_inc(s_wg.h, 16)

        def emit_transpose_tm_to_fm(src_dram_off_rows, gd_wait=None):
            raise NotImplementedError

        for b in range(B):
            pe_before_p0 = s_pe.v
            gd_x = s_gd.take(16)

            @block.gpsimd
            def _(g, b=b, pe_before_p0=pe_before_p0):
                g.wait_ge(s_pe.h, pe_before_p0)
                src = bass.AP(x0.tensor, b * 16 * 128 * 1024,
                              [[1024, 128], [128 * 1024, 16], [1, 1024]])
                g.dma_start(out=XBF[:].rearrange("p (c d) -> p c d", c=16),
                            in_=src).then_inc(s_gd.h, 16)

            pe1 = s_pe.take(16)
            dv1 = s_dv.take(16)

            @block.tensor
            def _(t, gd_x=gd_x, pe1=pe1, dv1=dv1):
                t.wait_ge(s_gd.h, gd_x + 16)
                t.wait_ge(s_ld.h, ld0 + 64)
                for ci in range(16):
                    if ci > 0:
                        t.wait_ge(s_dv.h, dv1 + ci)
                    for ch in range(8):
                        last = t.transpose(
                            psT[:, 128 * ch:128 * (ch + 1)],
                            XBF[:, ci * 1024 + 128 * ch:ci * 1024 + 128 * (ch + 1)],
                            IDS[:])
                    last.then_inc(s_pe.h, 1)

            @block.vector
            def _(v, pe1=pe1, dv1=dv1):
                cur = [dv1]
                op = mkop(v, s_dv, cur)
                for ci in range(16):
                    v.wait_ge(s_pe.h, pe1 + ci + 1)
                    out_ap = bass.AP(XT[:].tensor, XT[:].offset + 128 * ci,
                                     [[8 * T, 128], [T, 8], [1, 128]])
                    op("tensor_copy", out_ap,
                       psT[:].rearrange("p (c f) -> p c f", c=8))

            dv2 = s_dv.take(24)

            @block.vector
            def _(v, dv2=dv2):
                cur = [dv2]
                op = mkop(v, s_dv, cur)
                for ch in range(8):
                    sl = XT[:, ch * T:(ch + 1) * T]
                    op("reduce_sum", LNS[:, 0:1], sl, axis=AX.X, negate=True)
                    op("tensor_scalar", LNS[:, 1:2], LNS[:, 0:1], 1.0 / T, None,
                       ALU.mult)
                    op("tensor_scalar", sl, sl, LNS[:, 1:2], None, ALU.add)

            sc3 = s_sc.take(8)

            @block.scalar
            def _(s, dv2=dv2):
                s.wait_ge(s_dv.h, dv2 + 24)
                for ch in range(8):
                    s.activation(BUFA[:, ch * T:(ch + 1) * T],
                                 XT[:, ch * T:(ch + 1) * T], AF.Square
                                 ).then_inc(s_sc.h, 1)

            pe4 = s_pe.take(16)
            dv4 = s_dv.take(16)

            @block.tensor
            def _(t, sc3=sc3, pe4=pe4, dv4=dv4):
                t.wait_ge(s_sc.h, sc3 + 8)
                for tb in range(16):
                    if tb > 0:
                        t.wait_ge(s_dv.h, dv4 + tb)
                    for ch in range(8):
                        last = t.matmul(
                            psU[:, 2 * ch:2 * ch + 2],
                            BUFA[:, ch * T + 128 * tb:ch * T + 128 * (tb + 1)],
                            E2S[:], start=(ch == 0), stop=(ch == 7))
                    last.then_inc(s_pe.h, 1)

            @block.vector
            def _(v, pe4=pe4, dv4=dv4):
                cur = [dv4]
                op = mkop(v, s_dv, cur)
                for tb in range(16):
                    v.wait_ge(s_pe.h, pe4 + tb + 1)
                    op("tensor_scalar", NEGU[:, 16 * tb:16 * (tb + 1)], psU,
                       -0.5, None, ALU.mult)

            dv5a = s_dv.take(1)

            @block.vector
            def _(v, dv5a=dv5a):
                cur = [dv5a]
                op = mkop(v, s_dv, cur)
                op("tensor_copy", NEGUBF[:], NEGU[:])

            pe5 = s_pe.take(1)

            @block.tensor
            def _(t, dv5a=dv5a):
                t.wait_ge(s_dv.h, dv5a + 1)
                for tb in range(16):
                    last = t.transpose(psF[:, 128 * tb:128 * (tb + 1)],
                                       NEGUBF[:, 16 * tb:16 * (tb + 1)], IDS[:])
                last.then_inc(s_pe.h, 1)

            dv5b = s_dv.take(2)

            @block.vector
            def _(v, pe5=pe5, dv5b=dv5b):
                cur = [dv5b]
                op = mkop(v, s_dv, cur)
                v.wait_ge(s_pe.h, pe5 + 1)
                op("tensor_copy", NEGUFM[:], psF)
                op("tensor_reduce", NEGC[:], NEGUFM[:], axis=AX.X, op=ALU.min)

            sc5 = s_sc.take(1)

            @block.scalar
            def _(s, dv5b=dv5b):
                s.wait_ge(s_dv.h, dv5b + 2)
                s.activation(SUFM[:], NEGUFM[:], AF.Exp, scale=-1.0,
                             bias=NEGC[:, 0:1]).then_inc(s_sc.h, 1)

            pe5b = s_pe.take(1)

            @block.tensor
            def _(t, sc5=sc5):
                t.wait_ge(s_sc.h, sc5 + 1)
                for tb in range(16):
                    last = t.transpose(psSU[:, 16 * tb:16 * (tb + 1)],
                                       SUFM[:, 128 * tb:128 * (tb + 1)],
                                       IDS[0:16, 0:16])
                last.then_inc(s_pe.h, 1)

            dv5c = s_dv.take(1)

            @block.vector
            def _(v, pe5b=pe5b, dv5c=dv5c):
                cur = [dv5c]
                op = mkop(v, s_dv, cur)
                v.wait_ge(s_pe.h, pe5b + 1)
                op("tensor_copy", SUTM[:], psSU)

            pe_before_p6 = s_pe.v
            dv6 = s_dv.take(32)

            @block.vector
            def _(v, pe_before_p6=pe_before_p6, dv6=dv6):
                cur = [dv6]
                op = mkop(v, s_dv, cur)
                v.wait_ge(s_pe.h, pe_before_p6)
                for ci in range(16):
                    su_b = bass.AP(SUTM[:].tensor, SUTM[:].offset + 16 * ci,
                                   [[256, 128], [1, 16], [0, 64]])
                    op("tensor_copy", SUREP[:].rearrange("p (h j) -> p h j", h=16),
                       su_b)
                    op("tensor_tensor", BUFA[:, ci * 1024:(ci + 1) * 1024],
                       XBF[:, ci * 1024:(ci + 1) * 1024], SUREP[:], ALU.mult)

            dv_prev_hp = s_dv.v
            for hp in range(8):
                pe_qk = s_pe.take(3 * 64)
                sc_ex = s_sc.take(2 * 64)

                @block.tensor
                def _(t, hp=hp, pe_qk=pe_qk, sc_ex=sc_ex, dv_prev_hp=dv_prev_hp):
                    t.wait_ge(s_dv.h, dv_prev_hp)
                    for q in range(4):
                        ps_av = psAV0 if q < 2 else psAV1
                        avcol = 512 * (q % 2)
                        for r in range(16):
                            i = 16 * q + r
                            stat = XT[:, hp * T + 128 * r: hp * T + 128 * (r + 1)]
                            mov = XT[:, hp * T + 512 * q: hp * T + 512 * (q + 1)]
                            if i > 0:
                                t.wait_ge(s_sc.h, sc_ex + 2 * i)
                            t.matmul(psQKa[:], stat[0:64, :], mov[0:64, :],
                                     start=True, stop=True,
                                     tile_position=(0, 0)).then_inc(s_pe.h, 1)
                            t.matmul(psQKb[:], stat[64:128, :], mov[64:128, :],
                                     start=True, stop=True,
                                     tile_position=(64, 0)).then_inc(s_pe.h, 1)
                            t.wait_ge(s_sc.h, sc_ex + 2 * i + 2)
                            wx0 = BUFA[:, r * 1024 + 128 * hp:
                                       r * 1024 + 128 * hp + 64]
                            wx1 = BUFA[:, r * 1024 + 128 * hp + 64:
                                       r * 1024 + 128 * hp + 128]
                            t.matmul(ps_av[0:64, avcol:avcol + 512], wx0,
                                     GH[:, 0:512], start=(r == 0), stop=(r == 15),
                                     tile_position=(0, 0), skip_group_check=True)
                            t.matmul(ps_av[64:128, avcol:avcol + 512], wx1,
                                     GH[:, 512:1024], start=(r == 0), stop=(r == 15),
                                     tile_position=(0, 64), skip_group_check=True
                                     ).then_inc(s_pe.h, 1)

                @block.scalar
                def _(s, hp=hp, pe_qk=pe_qk, sc_ex=sc_ex, dv_prev_hp=dv_prev_hp):
                    s.wait_ge(s_dv.h, dv_prev_hp)
                    for q in range(4):
                        for r in range(16):
                            i = 16 * q + r
                            s.wait_ge(s_pe.h, pe_qk + 3 * i + 2)
                            s.activation(
                                GH[:, 0:512], psQKa[:], AF.Exp,
                                bias=NEGU[:, 16 * r + 2 * hp:16 * r + 2 * hp + 1],
                                accum_out=ACCT[:, r * 8 + q:r * 8 + q + 1]
                            ).then_inc(s_sc.h, 1)
                            s.activation(
                                GH[:, 512:1024], psQKb[:], AF.Exp,
                                bias=NEGU[:, 16 * r + 2 * hp + 1:16 * r + 2 * hp + 2],
                                accum_out=ACCT[:, r * 8 + 4 + q:r * 8 + 4 + q + 1]
                            ).then_inc(s_sc.h, 1)

                dv_fs = s_dv.take(10)

                @block.vector
                def _(v, hp=hp, sc_ex=sc_ex, dv_fs=dv_fs):
                    cur = [dv_fs]
                    op = mkop(v, s_dv, cur)
                    v.wait_ge(s_sc.h, sc_ex + 128)
                    for hl in range(2):
                        off = 4 * hl
                        def acc_ap(k):
                            return bass.AP(ACCT[:].tensor,
                                           ACCT[:].offset + off + k,
                                           [[128, 128], [8, 16]])
                        sl = slice(16 * hl, 16 * hl + 16)
                        op("tensor_tensor", FST[:, sl], acc_ap(0), acc_ap(1), ALU.add)
                        op("tensor_tensor", FS[:, sl], acc_ap(2), acc_ap(3), ALU.add)
                        op("tensor_tensor", FST[:, sl], FST[:, sl], FS[:, sl], ALU.add)
                        su_ap = bass.AP(SUTM[:].tensor,
                                        SUTM[:].offset + 2 * hp + hl,
                                        [[256, 128], [16, 16]])
                        op("tensor_tensor", FST[:, sl], FST[:, sl], su_ap, ALU.mult)
                        op("reciprocal", FS[:, sl], FST[:, sl])

                pe_after_qk = s_pe.v
                for half in range(2):
                    dv_z = s_dv.take(1)

                    @block.vector
                    def _(v, half=half, pe_after_qk=pe_after_qk, dv_z=dv_z):
                        cur = [dv_z]
                        op = mkop(v, s_dv, cur)
                        v.wait_ge(s_pe.h, pe_after_qk + half)
                        ps = psAV0 if half == 0 else psAV1
                        op("tensor_copy", Z[:], ps[:])

                    pe_z = s_pe.take(1)

                    @block.tensor
                    def _(t, dv_z=dv_z):
                        t.wait_ge(s_dv.h, dv_z + 1)
                        for j in range(8):
                            last = t.transpose(psT[:, 128 * j:128 * (j + 1)],
                                               Z[:, 128 * j:128 * (j + 1)], IDS[:])
                        last.then_inc(s_pe.h, 1)

                    dv_w = s_dv.take(16)

                    @block.vector
                    def _(v, hp=hp, half=half, pe_z=pe_z, dv_w=dv_w):
                        cur = [dv_w]
                        op = mkop(v, s_dv, cur)
                        v.wait_ge(s_pe.h, pe_z + 1)
                        for j in range(8):
                            ci = half * 8 + j
                            for hl in range(2):
                                op("tensor_scalar",
                                   ATTN[:, ci * 1024 + 128 * hp + 64 * hl:
                                        ci * 1024 + 128 * hp + 64 * (hl + 1)],
                                   psT[:, 128 * j + 64 * hl:
                                       128 * j + 64 * (hl + 1)],
                                   FS[:, 16 * hl + ci:16 * hl + ci + 1], None,
                                   ALU.mult)
                dv_prev_hp = s_dv.v

            x8 = [s_x[0].take(16 * 8), s_x[1].take(16 * 8)]
            y8 = [s_y[0].take(16 * 8), s_y[1].take(16 * 8)]
            st8 = s_st.take(0)
            dv8 = s_dv.take(16 * 7)
            sc8 = s_sc.take(16 * 2)

            @block.sync
            def _(sync, b=b, dv8=dv8, x8=x8):
                for ci in range(16):
                    sync.wait_ge(s_dv.h, dv8 + 7 * (ci - 2) + 7 if ci >= 2 else dv8)
                    src = bass.AP(x0.tensor, (b * 16 + ci) * 128 * 1024,
                                  [[1024, 128], [1, 1024]])
                    sync.dma_start(out=XRES[:, 1024 * (ci % 2):1024 * (ci % 2 + 1)],
                                   in_=src).then_inc(s_x[ci % 2].h, 16)

            @block.vector
            def _(v, x8=x8, y8=y8, sc8=sc8, dv8=dv8):
                cur = [dv8]
                op = mkop(v, s_dv, cur)
                for ci in range(16):
                    xs = XRES[:, 1024 * (ci % 2):1024 * (ci % 2 + 1)]
                    v.wait_ge(s_x[ci % 2].h, x8[ci % 2] + 16 * (ci // 2 + 1))
                    op("tensor_tensor", xs, xs,
                       ATTN[:, ci * 1024:(ci + 1) * 1024], ALU.add)
                    op("reduce_sum", LNS[:, 0:1], xs, axis=AX.X, negate=True)
                    op("tensor_scalar", LNS[:, 1:2], LNS[:, 0:1], 1.0 / D, None,
                       ALU.mult)
                    op("tensor_tensor", LNS[:, 6:7], LNS[:, 1:2], LNS[:, 1:2],
                       ALU.mult)
                    v.wait_ge(s_sc.h, sc8 + 2 * ci + 1)
                    op("scalar_tensor_tensor", LNS[:, 2:3], LNS[:, 3:4], 1.0 / D,
                       LNS[:, 6:7], ALU.mult, ALU.subtract)
                    v.wait_ge(s_sc.h, sc8 + 2 * ci + 2)
                    op("reciprocal", LNS[:, 5:6], LNS[:, 4:5])
                    ob = X1O[:, 1024 * (ci % 2):1024 * (ci % 2 + 1)]
                    v.wait_ge(s_y[ci % 2].h,
                              y8[ci % 2] + 16 * (ci // 2) if ci >= 2 else y8[ci % 2])
                    op("tensor_scalar", ob, xs, LNS[:, 1:2], LNS[:, 5:6],
                       ALU.add, ALU.mult)

            @block.scalar
            def _(s, dv8=dv8, sc8=sc8):
                for ci in range(16):
                    xs = XRES[:, 1024 * (ci % 2):1024 * (ci % 2 + 1)]
                    s.wait_ge(s_dv.h, dv8 + 7 * ci + 1)
                    s.activation(SQS[:], xs, AF.Square,
                                 accum_out=LNS[:, 3:4]).then_inc(s_sc.h, 1)
                    s.wait_ge(s_dv.h, dv8 + 7 * ci + 5)
                    s.activation(LNS[:, 4:5], LNS[:, 2:3], AF.Sqrt,
                                 bias=EPS[:, 0:1]).then_inc(s_sc.h, 1)

            @block.gpsimd
            def _(g, b=b, dv8=dv8, y8=y8):
                for ci in range(16):
                    g.wait_ge(s_dv.h, dv8 + 7 * (ci + 1))
                    dst = bass.AP(x1.tensor, (b * 16 + ci) * 128 * 1024,
                                  [[1024, 128], [1, 1024]])
                    g.dma_start(
                        out=dst, in_=X1O[:, 1024 * (ci % 2):1024 * (ci % 2 + 1)]
                    ).then_inc(s_y[ci % 2].h, 16)

            pe_before_p9 = s_pe.v
            gd9 = s_gd.take(16)

            @block.gpsimd
            def _(g, y0t=s_y[0].v, y1t=s_y[1].v, pe_before_p9=pe_before_p9, b=b):
                g.wait_ge(s_y[0].h, y0t)
                g.wait_ge(s_y[1].h, y1t)
                g.wait_ge(s_pe.h, pe_before_p9)
                src = bass.AP(x1.tensor, b * 16 * 128 * 1024,
                              [[1024, 128], [128 * 1024, 16], [1, 1024]])
                g.dma_start(out=XBF[:].rearrange("p (c d) -> p c d", c=16),
                            in_=src).then_inc(s_gd.h, 16)

            dv_before_p9 = s_dv.v
            pe9 = s_pe.take(16)
            dv9 = s_dv.take(16)

            @block.tensor
            def _(t, gd9=gd9, pe9=pe9, dv9=dv9, dv_before_p9=dv_before_p9):
                t.wait_ge(s_gd.h, gd9 + 16)
                t.wait_ge(s_dv.h, dv_before_p9)
                for ci in range(16):
                    if ci > 0:
                        t.wait_ge(s_dv.h, dv9 + ci)
                    for ch in range(8):
                        last = t.transpose(
                            psT[:, 128 * ch:128 * (ch + 1)],
                            XBF[:, ci * 1024 + 128 * ch:ci * 1024 + 128 * (ch + 1)],
                            IDS[:])
                    last.then_inc(s_pe.h, 1)

            @block.vector
            def _(v, pe9=pe9, dv9=dv9):
                cur = [dv9]
                op = mkop(v, s_dv, cur)
                for ci in range(16):
                    v.wait_ge(s_pe.h, pe9 + ci + 1)
                    out_ap = bass.AP(XT[:].tensor, XT[:].offset + 128 * ci,
                                     [[8 * T, 128], [T, 8], [1, 128]])
                    op("tensor_copy", out_ap,
                       psT[:].rearrange("p (c f) -> p c f", c=8))

            pe9b = s_pe.take(16)
            sc9 = s_sc.take(16)
            dv9b = s_dv.take(16 * 9)
            st9 = s_st.take(16 * 16)

            @block.tensor
            def _(t, dv9=dv9, pe9b=pe9b, sc9=sc9):
                t.wait_ge(s_dv.h, dv9 + 16)
                t.wait_ge(s_wg.h, gd_wg + 16)
                for tb in range(16):
                    if tb > 0:
                        t.wait_ge(s_sc.h, sc9 + tb)
                    for ich in range(8):
                        last = t.matmul(
                            psGate[:],
                            XT[:, ich * T + 128 * tb:ich * T + 128 * (tb + 1)],
                            WGB[:, ich * 8:(ich + 1) * 8],
                            start=(ich == 0), stop=(ich == 7))
                    last.then_inc(s_pe.h, 1)

            @block.scalar
            def _(s, pe9b=pe9b, dv9b=dv9b, sc9=sc9):
                for tb in range(16):
                    s.wait_ge(s_pe.h, pe9b + tb + 1)
                    if tb > 0:
                        s.wait_ge(s_dv.h, dv9b + 9 * tb)
                    s.activation(PRB[:], psGate, AF.Exp,
                                 accum_out=GSM[:, 0:1]).then_inc(s_sc.h, 1)

            @block.vector
            def _(v, sc9=sc9, dv9b=dv9b, st9=st9):
                cur = [dv9b]
                op = mkop(v, s_dv, cur)
                for tb in range(16):
                    v.wait_ge(s_sc.h, sc9 + tb + 1)
                    v.wait_ge(s_st.h, st9 + 16 * tb)
                    op("reciprocal", GSM[:, 1:2], GSM[:, 0:1])
                    op("tensor_scalar", PRB[:], PRB[:], GSM[:, 1:2], None, ALU.mult)
                    op("reduce_max", MT[:, 2:3], PRB[:], axis=AX.X)
                    op("tensor_scalar", GSM[:, 2:10], PRB[:], MT[:, 2:3], None,
                       ALU.is_ge)
                    op("scalar_tensor_tensor", GSM[:, 16:24], GSM[:, 2:10], 1.0,
                       IO8[:], ALU.mult, ALU.mult, accum_out=MT[:, 0:1])
                    op("scalar_tensor_tensor", GSM[:, 2:10], GSM[:, 2:10], -1e30,
                       PRB[:], ALU.mult, ALU.add)
                    op("reduce_max", MT[:, 3:4], GSM[:, 2:10], axis=AX.X)
                    op("tensor_scalar", GSM[:, 2:10], GSM[:, 2:10], MT[:, 3:4],
                       None, ALU.is_ge)
                    op("scalar_tensor_tensor", GSM[:, 16:24], GSM[:, 2:10], 1.0,
                       IO8[:], ALU.mult, ALU.mult, accum_out=MT[:, 1:2])

            @block.sync
            def _(sync, b=b, dv9b=dv9b, st9=st9):
                for tb in range(16):
                    sync.wait_ge(s_dv.h, dv9b + 9 * (tb + 1))
                    dst = bass.AP(meta.tensor, (b * 16 + tb) * 128 * 4,
                                  [[4, 128], [1, 4]])
                    sync.dma_start(out=dst, in_=MT[:]).then_inc(s_st.h, 16)

        @block.gpsimd
        def _(g):
            g.wait_ge(s_st.h, s_st.v)
            g.wait_ge(s_y[0].h, s_y[0].v)
            g.wait_ge(s_y[1].h, s_y[1].v)

    return nc



def build_moe():
    NTOK = 16 * T
    NCQ = NTOK // 128
    nc = bass.Bass("TRN2", target_bir_lowering=False, debug=False)
    x1o = nc.dram_tensor("x1o", [NT, D], F32, kind="ExternalInput").ap()
    metaF = nc.dram_tensor("metaF", [NTOK, 4], F32, kind="ExternalInput").ap()
    cfc = nc.dram_tensor("cfc", [E_EXP, D, 3072], F32, kind="ExternalInput").ap()
    cpj = nc.dram_tensor("cpj", [E_EXP, 1536, D], F32, kind="ExternalInput").ap()
    x2 = nc.dram_tensor("x2", [NT, D], F32, kind="ExternalOutput").ap()
    gscr = nc.dram_tensor("gscr", [128, NCQ, 8], F32, kind="Internal").ap()

    ident = nc.inline_tensor(np.eye(128, dtype=BFNP), name="identbf")
    ltri = np.zeros((128, 128), dtype=BFNP)
    for k in range(128):
        ltri[k, k + 1:] = 1
    LTc = nc.inline_tensor(ltri, name="LT")
    iota8 = nc.inline_tensor(np.tile(np.arange(8, dtype=np.float32), (128, 1)),
                             name="iota8")
    epsc = nc.inline_tensor(np.full((128, 1), 1e-5, dtype=np.float32), name="epsc")
    onesr = nc.inline_tensor(np.ones((1, 128), dtype=np.float32), name="onesr")
    onesc = nc.inline_tensor(np.ones((128, 1), dtype=BFNP), name="onesc")

    with ExitStack() as ctx:
        En = ctx.enter_context
        XT = En(nc.sbuf_tensor("XT", [128, 8 * T], BF16))
        XBF = En(nc.sbuf_tensor("XBF", [128, 16 * 1024], BF16))
        ACC = XBF
        CFA = En(nc.sbuf_tensor("CFA", [128, 8 * 1536], BF16))
        CFG = En(nc.sbuf_tensor("CFG", [128, 8 * 1536], BF16))
        CPJ = En(nc.sbuf_tensor("CPJ", [128, 12 * 1024], BF16))
        HH1 = En(nc.sbuf_tensor("HH1", [128, 1536], BF16))
        HH = En(nc.sbuf_tensor("HH", [128, 1536], BF16))
        HHT = En(nc.sbuf_tensor("HHT", [128, 1536], BF16))
        MF = XT[:, 0:2048].bitcast(F32)
        OH0 = XT[:, 2048:4096]
        OH1 = XT[:, 4096:6144]
        OHP = XT[:, 6144:8192]
        GALL = XT[:, 8192:12288].bitcast(F32)
        GOWN = En(nc.sbuf_tensor("GOWN", [128, 256], F32))
        RS = En(nc.sbuf_tensor("RS", [128, 40], F32))
        BcSB = En(nc.sbuf_tensor("BcSB", [1, 8], F32))
        XRES = En(nc.sbuf_tensor("XRES", [128, 2 * 1024], F32))
        X2O = En(nc.sbuf_tensor("X2O", [128, 2 * 1024], F32))
        SQS = En(nc.sbuf_tensor("SQS", [128, 1024], BF16))
        LNS = En(nc.sbuf_tensor("LNS", [128, 8], F32))
        IDS = En(nc.sbuf_tensor("IDS", [128, 128], BF16))
        LTS = En(nc.sbuf_tensor("LTS", [128, 128], BF16))
        IO8 = En(nc.sbuf_tensor("IO8", [128, 8], F32))
        EPS = En(nc.sbuf_tensor("EPS", [128, 1], F32))
        ON1 = En(nc.sbuf_tensor("ON1", [1, 128], F32))
        ONC = En(nc.sbuf_tensor("ONC", [128, 1], BF16))

        psH = [En(nc.psum_tensor(f"psH{i}", [128, 512], F32)) for i in range(3)]
        psO = [En(nc.psum_tensor(f"psO{i}", [128, 512], F32)) for i in range(2)]
        psT = En(nc.psum_tensor("psT", [128, 1024], BF16))
        psP = psH[0][:, 0:8]
        psB = psH[1][0:1, 0:8]

        s_ld = Ctr(ctx, nc, "s_ld")
        s_gd = Ctr(ctx, nc, "s_gd")
        s_pe = Ctr(ctx, nc, "s_pe")
        s_sc = Ctr(ctx, nc, "s_sc")
        s_dv = Ctr(ctx, nc, "s_dv")
        s_wa = Ctr(ctx, nc, "s_wa")
        s_wg2 = Ctr(ctx, nc, "s_wg2")
        s_wp = Ctr(ctx, nc, "s_wp")
        s_x = [Ctr(ctx, nc, "s_x0"), Ctr(ctx, nc, "s_x1")]
        s_y = [Ctr(ctx, nc, "s_y0"), Ctr(ctx, nc, "s_y1")]
        s_gs = Ctr(ctx, nc, "s_gs")
        s_go = Ctr(ctx, nc, "s_go")
        block = En(nc.Block())

        ld0 = s_ld.take(16 * 7)

        @block.sync
        def _(sync):
            sync.dma_start(out=IDS[:], in_=ident.ap()).then_inc(s_ld.h, 16)
            sync.dma_start(out=LTS[:], in_=LTc.ap()).then_inc(s_ld.h, 16)
            sync.dma_start(out=IO8[:], in_=iota8.ap()).then_inc(s_ld.h, 16)
            sync.dma_start(out=EPS[:], in_=epsc.ap()).then_inc(s_ld.h, 16)
            sync.dma_start(out=ON1[:], in_=onesr.ap()).then_inc(s_ld.h, 16)
            sync.dma_start(out=ONC[:], in_=onesc.ap()).then_inc(s_ld.h, 16)
            sync.dma_start(
                out=MF.rearrange("p (c f) -> p c f", c=NCQ),
                in_=bass.AP(metaF.tensor, 0, [[4, 128], [512, NCQ], [1, 4]])
            ).then_inc(s_ld.h, 16)

        dvR1 = s_dv.take(NCQ * 7)

        @block.vector
        def _(v, dvR1=dvR1):
            cur = [dvR1]
            op = mkop(v, s_dv, cur)
            v.wait_ge(s_ld.h, ld0 + 16 * 7)
            for cq in range(NCQ):
                e0c = MF[:, 4 * cq:4 * cq + 1]
                e1c = MF[:, 4 * cq + 1:4 * cq + 2]
                op("tensor_scalar", RS[:, 0:8], IO8[:], e0c, None, ALU.subtract)
                op("tensor_tensor", RS[:, 8:16], RS[:, 0:8], RS[:, 0:8], ALU.mult)
                op("tensor_scalar", OH0[:, 8 * cq:8 * cq + 8], RS[:, 8:16], 0.25,
                   None, ALU.is_le)
                op("tensor_scalar", RS[:, 0:8], IO8[:], e1c, None, ALU.subtract)
                op("tensor_tensor", RS[:, 8:16], RS[:, 0:8], RS[:, 0:8], ALU.mult)
                op("tensor_scalar", OH1[:, 8 * cq:8 * cq + 8], RS[:, 8:16], 0.25,
                   None, ALU.is_le)
                op("tensor_tensor", OHP[:, 8 * cq:8 * cq + 8],
                   OH0[:, 8 * cq:8 * cq + 8], OH1[:, 8 * cq:8 * cq + 8], ALU.add)

        peR = s_pe.take(NCQ)
        dvR2 = s_dv.take(NCQ * 10)

        @block.tensor
        def _(t, peR=peR, dvR2=dvR2, dvR1=dvR1):
            t.wait_ge(s_dv.h, dvR1 + NCQ * 7)
            for cq in range(NCQ):
                t.wait_ge(s_dv.h, dvR2 + 10 * cq + 1)
                ohp = OHP[:, 8 * cq:8 * cq + 8]
                t.matmul(psP, LTS[:], ohp, start=True, stop=False)
                t.matmul(psP, ON1[:], BcSB[:], start=False, stop=True)
                t.matmul(psB, ONC[:], ohp, start=(cq == 0), stop=(cq == NCQ - 1),
                         skip_group_check=True).then_inc(s_pe.h, 1)

        @block.vector
        def _(v, peR=peR, dvR2=dvR2):
            cur = [dvR2]
            op = mkop(v, s_dv, cur)
            for cq in range(NCQ):
                if cq > 0:
                    v.wait_ge(s_pe.h, peR + cq)
                    op("tensor_copy", BcSB[:], psB)
                else:
                    op("memset", BcSB[:], 0.0)
                v.wait_ge(s_pe.h, peR + cq + 1)
                op("scalar_tensor_tensor", RS[:, 16:24], psP, 1.0,
                   OH0[:, 8 * cq:8 * cq + 8], ALU.mult, ALU.mult,
                   accum_out=RS[:, 32:33])
                op("scalar_tensor_tensor", RS[:, 24:32], psP, 1.0,
                   OH1[:, 8 * cq:8 * cq + 8], ALU.mult, ALU.mult,
                   accum_out=RS[:, 33:34])
                op("tensor_scalar", RS[:, 34:35], RS[:, 32:33], 4095.5, None,
                   ALU.is_le)
                op("tensor_scalar", RS[:, 35:36], RS[:, 33:34], 4095.5, None,
                   ALU.is_le)
                op("tensor_tensor", RS[:, 36:37], MF[:, 4 * cq + 2:4 * cq + 3],
                   RS[:, 34:35], ALU.mult)
                op("tensor_tensor", RS[:, 37:38], MF[:, 4 * cq + 3:4 * cq + 4],
                   RS[:, 35:36], ALU.mult)
                op("tensor_scalar", RS[:, 16:24], OH0[:, 8 * cq:8 * cq + 8],
                   RS[:, 36:37], None, ALU.mult)
                op("tensor_scalar", RS[:, 24:32], OH1[:, 8 * cq:8 * cq + 8],
                   RS[:, 37:38], None, ALU.mult)
                op("tensor_tensor", GALL[:, 8 * cq:8 * cq + 8], RS[:, 16:24],
                   RS[:, 24:32], ALU.add)

        gdR = s_gd.take(16)
        dv_after_R = s_dv.v

        @block.gpsimd
        def _(g, dv_after_R=dv_after_R, gdR=gdR):
            g.wait_ge(s_dv.h, dv_after_R)
            g.dma_start(
                out=bass.AP(gscr.tensor, 0, [[NCQ * 8, 128], [8, NCQ], [1, 8]]),
                in_=GALL.rearrange("p (c f) -> p c f", c=NCQ)
            ).then_inc(s_gd.h, 16)
            g.wait_ge(s_gd.h, gdR + 16)
            pid = g.partition_id()
            src = bass.AP(gscr.tensor, pid * (32 * 8),
                          [[NCQ * 8, 128], [8, 32], [1, 8]])
            g.dma_start(out=GOWN[:].rearrange("p (c f) -> p c f", c=32),
                        in_=src).then_inc(s_go.h, 16)

        pe_a_done = {}
        pe_g_done = {}
        pe_p_done = {}

        for b in range(B):
            pe_before_x = s_pe.v
            dv_before_x = s_dv.v
            gd_x = s_gd.take(16)

            @block.gpsimd
            def _(g, b=b, pe_before_x=pe_before_x, dv_before_x=dv_before_x):
                g.wait_ge(s_pe.h, pe_before_x)
                g.wait_ge(s_dv.h, dv_before_x)
                src = bass.AP(x1o.tensor, b * 16 * 128 * 1024,
                              [[1024, 128], [128 * 1024, 16], [1, 1024]])
                g.dma_start(out=XBF[:].rearrange("p (c d) -> p c d", c=16),
                            in_=src).then_inc(s_gd.h, 16)

            pe1 = s_pe.take(16)
            dv1 = s_dv.take(17)

            @block.tensor
            def _(t, gd_x=gd_x, pe1=pe1, dv1=dv1, gdR=gdR):
                t.wait_ge(s_gd.h, gd_x + 16)
                t.wait_ge(s_go.h, 16)
                for ci in range(16):
                    if ci > 0:
                        t.wait_ge(s_dv.h, dv1 + ci)
                    for ch in range(8):
                        last = t.transpose(
                            psT[:, 128 * ch:128 * (ch + 1)],
                            XBF[:, ci * 1024 + 128 * ch:ci * 1024 + 128 * (ch + 1)],
                            IDS[:])
                    last.then_inc(s_pe.h, 1)

            @block.vector
            def _(v, pe1=pe1, dv1=dv1):
                cur = [dv1]
                op = mkop(v, s_dv, cur)
                for ci in range(16):
                    v.wait_ge(s_pe.h, pe1 + ci + 1)
                    out_ap = bass.AP(XT[:].tensor, XT[:].offset + 128 * ci,
                                     [[8 * T, 128], [T, 8], [1, 128]])
                    op("tensor_copy", out_ap,
                       psT[:].rearrange("p (c f) -> p c f", c=8))
                op("memset", ACC[:], 0.0)

            for e in range(E_EXP):
                wa = s_wa.take(16 * 8)
                wg2 = s_wg2.take(16 * 8)
                wp = s_wp.take(16 * 12)

                @block.gpsimd
                def _(g, b=b, e=e, wa=wa, wg2=wg2, wp=wp):
                    if (b, e) != (0, 0):
                        pb, pe_ = (b, e - 1) if e > 0 else (b - 1, 7)
                        g.wait_ge(s_pe.h, pe_a_done[(pb, pe_)])
                    for ich in range(8):
                        src = bass.AP(cfc.tensor,
                                      e * D * 3072 + 128 * ich * 3072,
                                      [[3072, 128], [1, 1536]])
                        g.dma_start(out=CFA[:, 1536 * ich:1536 * (ich + 1)],
                                    in_=src).then_inc(s_wa.h, 16)
                    if (b, e) != (0, 0):
                        pb, pe_ = (b, e - 1) if e > 0 else (b - 1, 7)
                        g.wait_ge(s_pe.h, pe_g_done[(pb, pe_)])
                    for ich in range(8):
                        src = bass.AP(cfc.tensor,
                                      e * D * 3072 + 128 * ich * 3072 + 1536,
                                      [[3072, 128], [1, 1536]])
                        g.dma_start(out=CFG[:, 1536 * ich:1536 * (ich + 1)],
                                    in_=src).then_inc(s_wg2.h, 16)
                    if (b, e) != (0, 0):
                        pb, pe_ = (b, e - 1) if e > 0 else (b - 1, 7)
                        g.wait_ge(s_pe.h, pe_p_done[(pb, pe_)])
                    for fc in range(12):
                        src = bass.AP(cpj.tensor,
                                      e * 1536 * D + 128 * fc * D,
                                      [[1024, 128], [1, 1024]])
                        g.dma_start(out=CPJ[:, 1024 * fc:1024 * (fc + 1)],
                                    in_=src).then_inc(s_wp.h, 16)

                pe_e = s_pe.take(16 * 5)
                sc_e = s_sc.take(16 * 3)
                dv_e = s_dv.take(16 * 7)
                dv_xt_done = dv1 + 16

                @block.tensor
                def _(t, b=b, e=e, wa=wa, wg2=wg2, wp=wp, pe_e=pe_e,
                      sc_e=sc_e, dv_e=dv_e, dv_xt_done=dv_xt_done):
                    t.wait_ge(s_dv.h, dv_xt_done + 1)
                    t.wait_ge(s_wa.h, wa + 128)
                    for tb in range(16):
                        base5 = pe_e + 5 * tb
                        based = dv_e + 7 * tb
                        if tb > 0 or (b, e) != (0, 0):
                            t.wait_ge(s_dv.h, based - 7 + 3 if tb > 0 else dv_e)
                        for ich in range(8):
                            stat = XT[:, ich * T + 128 * tb:ich * T + 128 * (tb + 1)]
                            for fb in range(3):
                                last = t.matmul(
                                    psH[fb][:], stat,
                                    CFA[:, 1536 * ich + 512 * fb:
                                        1536 * ich + 512 * (fb + 1)],
                                    start=(ich == 0), stop=(ich == 7))
                        last.then_inc(s_pe.h, 1)
                        t.wait_ge(s_sc.h, sc_e + 3 * tb + 3)
                        if tb == 0:
                            t.wait_ge(s_wg2.h, wg2 + 128)
                        for ich in range(8):
                            stat = XT[:, ich * T + 128 * tb:ich * T + 128 * (tb + 1)]
                            for fb in range(3):
                                last = t.matmul(
                                    psH[fb][:], stat,
                                    CFG[:, 1536 * ich + 512 * fb:
                                        1536 * ich + 512 * (fb + 1)],
                                    start=(ich == 0), stop=(ich == 7))
                        last.then_inc(s_pe.h, 1)
                        t.wait_ge(s_dv.h, based + 3)
                        for j in range(8):
                            last = t.transpose(psT[:, 128 * j:128 * (j + 1)],
                                               HH[:, 128 * j:128 * (j + 1)], IDS[:])
                        last.then_inc(s_pe.h, 1)
                        t.wait_ge(s_dv.h, based + 4)
                        for j in range(4):
                            last = t.transpose(psT[:, 128 * j:128 * (j + 1)],
                                               HH[:, 1024 + 128 * j:1024 + 128 * (j + 1)],
                                               IDS[:])
                        last.then_inc(s_pe.h, 1)
                        t.wait_ge(s_dv.h, based + 5)
                        if tb == 0:
                            t.wait_ge(s_wp.h, wp + 192)
                        for fc in range(12):
                            stat = HHT[:, 128 * fc:128 * (fc + 1)]
                            t.matmul(psO[0][:], stat,
                                     CPJ[:, 1024 * fc:1024 * fc + 512],
                                     start=(fc == 0), stop=(fc == 11))
                            last = t.matmul(psO[1][:], stat,
                                            CPJ[:, 1024 * fc + 512:1024 * (fc + 1)],
                                            start=(fc == 0), stop=(fc == 11))
                        last.then_inc(s_pe.h, 1)
                    pe_a_done[(b, e)] = pe_e + 5 * 15 + 1
                    pe_g_done[(b, e)] = pe_e + 5 * 15 + 2
                    pe_p_done[(b, e)] = pe_e + 5 * 16

                @block.scalar
                def _(s, b=b, e=e, pe_e=pe_e, sc_e=sc_e):
                    for tb in range(16):
                        s.wait_ge(s_pe.h, pe_e + 5 * tb + 1)
                        for fb in range(3):
                            s.activation(HH1[:, 512 * fb:512 * (fb + 1)],
                                         psH[fb][:], AF.Silu).then_inc(s_sc.h, 1)

                @block.vector
                def _(v, b=b, e=e, pe_e=pe_e, dv_e=dv_e, sc_e=sc_e):
                    cur = [dv_e]
                    op = mkop(v, s_dv, cur)
                    for tb in range(16):
                        v.wait_ge(s_pe.h, pe_e + 5 * tb + 2)
                        for fb in range(3):
                            op("tensor_tensor", HH[:, 512 * fb:512 * (fb + 1)],
                               HH1[:, 512 * fb:512 * (fb + 1)], psH[fb][:],
                               ALU.mult)
                        v.wait_ge(s_pe.h, pe_e + 5 * tb + 3)
                        op("tensor_copy", HHT[:, 0:1024], psT[:])
                        v.wait_ge(s_pe.h, pe_e + 5 * tb + 4)
                        op("tensor_copy", HHT[:, 1024:1536], psT[:, 0:512])
                        v.wait_ge(s_pe.h, pe_e + 5 * tb + 5)
                        gcol = GOWN[:, (b * 16 + tb) * 8 + e:
                                    (b * 16 + tb) * 8 + e + 1]
                        op("scalar_tensor_tensor",
                           ACC[:, 1024 * tb:1024 * tb + 512], psO[0][:], gcol,
                           ACC[:, 1024 * tb:1024 * tb + 512], ALU.mult, ALU.add)
                        op("scalar_tensor_tensor",
                           ACC[:, 1024 * tb + 512:1024 * (tb + 1)], psO[1][:],
                           gcol, ACC[:, 1024 * tb + 512:1024 * (tb + 1)],
                           ALU.mult, ALU.add)

            x8 = [s_x[0].take(16 * 8), s_x[1].take(16 * 8)]
            y8 = [s_y[0].take(16 * 8), s_y[1].take(16 * 8)]
            dv8 = s_dv.take(16 * 7)
            sc8 = s_sc.take(16 * 2)

            @block.sync
            def _(sync, b=b, dv8=dv8):
                for ci in range(16):
                    sync.wait_ge(s_dv.h, dv8 + 7 * (ci - 2) + 7 if ci >= 2 else dv8)
                    src = bass.AP(x1o.tensor, (b * 16 + ci) * 128 * 1024,
                                  [[1024, 128], [1, 1024]])
                    sync.dma_start(out=XRES[:, 1024 * (ci % 2):1024 * (ci % 2 + 1)],
                                   in_=src).then_inc(s_x[ci % 2].h, 16)

            @block.vector
            def _(v, b=b, x8=x8, y8=y8, sc8=sc8, dv8=dv8):
                cur = [dv8]
                op = mkop(v, s_dv, cur)
                for ci in range(16):
                    xs = XRES[:, 1024 * (ci % 2):1024 * (ci % 2 + 1)]
                    v.wait_ge(s_x[ci % 2].h, x8[ci % 2] + 16 * (ci // 2 + 1))
                    op("tensor_tensor", xs, xs,
                       ACC[:, ci * 1024:(ci + 1) * 1024], ALU.add)
                    op("reduce_sum", LNS[:, 0:1], xs, axis=AX.X, negate=True)
                    op("tensor_scalar", LNS[:, 1:2], LNS[:, 0:1], 1.0 / D, None,
                       ALU.mult)
                    op("tensor_tensor", LNS[:, 6:7], LNS[:, 1:2], LNS[:, 1:2],
                       ALU.mult)
                    v.wait_ge(s_sc.h, sc8 + 2 * ci + 1)
                    op("scalar_tensor_tensor", LNS[:, 2:3], LNS[:, 3:4], 1.0 / D,
                       LNS[:, 6:7], ALU.mult, ALU.subtract)
                    v.wait_ge(s_sc.h, sc8 + 2 * ci + 2)
                    op("reciprocal", LNS[:, 5:6], LNS[:, 4:5])
                    ob = X2O[:, 1024 * (ci % 2):1024 * (ci % 2 + 1)]
                    v.wait_ge(s_y[ci % 2].h,
                              y8[ci % 2] + 16 * (ci // 2) if ci >= 2 else y8[ci % 2])
                    op("tensor_scalar", ob, xs, LNS[:, 1:2], LNS[:, 5:6],
                       ALU.add, ALU.mult)

            @block.scalar
            def _(s, b=b, dv8=dv8, sc8=sc8):
                for ci in range(16):
                    xs = XRES[:, 1024 * (ci % 2):1024 * (ci % 2 + 1)]
                    s.wait_ge(s_dv.h, dv8 + 7 * ci + 1)
                    s.activation(SQS[:], xs, AF.Square,
                                 accum_out=LNS[:, 3:4]).then_inc(s_sc.h, 1)
                    s.wait_ge(s_dv.h, dv8 + 7 * ci + 5)
                    s.activation(LNS[:, 4:5], LNS[:, 2:3], AF.Sqrt,
                                 bias=EPS[:, 0:1]).then_inc(s_sc.h, 1)

            @block.gpsimd
            def _(g, b=b, dv8=dv8, y8=y8):
                for ci in range(16):
                    g.wait_ge(s_dv.h, dv8 + 7 * (ci + 1))
                    dst = bass.AP(x2.tensor, (b * 16 + ci) * 128 * 1024,
                                  [[1024, 128], [1, 1024]])
                    g.dma_start(
                        out=dst, in_=X2O[:, 1024 * (ci % 2):1024 * (ci % 2 + 1)]
                    ).then_inc(s_y[ci % 2].h, 16)

        @block.gpsimd
        def _(g):
            g.wait_ge(s_y[0].h, s_y[0].v)
            g.wait_ge(s_y[1].h, s_y[1].v)

    return nc



def _rope_tables():
    j = np.arange(64) % 32
    inv = (10000.0 ** (-2.0 * j / 64.0))[:, None]
    tpos = np.arange(T, dtype=np.float64)[None, :]
    ang = inv * tpos
    cos = np.cos(ang)
    sin = np.sin(ang)
    cos2 = np.tile(cos, (2, 1)).astype(BFNP)
    sin2 = np.tile(sin, (2, 1)).astype(BFNP)
    return cos2, sin2


def build_l1attn():
    nc = bass.Bass("TRN2", target_bir_lowering=False, debug=False)
    x2i = nc.dram_tensor("x2i", [NT, D], F32, kind="ExternalInput").ap()
    wqT = nc.dram_tensor("wqT", [D, D], F32, kind="ExternalInput").ap()
    wkT = nc.dram_tensor("wkT", [D, D], F32, kind="ExternalInput").ap()
    wvT = nc.dram_tensor("wvT", [D, D], F32, kind="ExternalInput").ap()
    woT = nc.dram_tensor("woT", [D, D], F32, kind="ExternalInput").ap()
    x3 = nc.dram_tensor("x3", [NT, D], F32, kind="ExternalOutput").ap()

    ident = nc.inline_tensor(np.eye(128, dtype=BFNP), name="identbf")
    epsc = nc.inline_tensor(np.full((128, 1), 1e-5, dtype=np.float32), name="epsc")
    cos2, sin2 = _rope_tables()
    cosc = nc.inline_tensor(cos2, name="cosc")
    sinc = nc.inline_tensor(sin2, name="sinc")
    utri = np.triu(np.ones((128, 128))).astype(BFNP)
    utric = nc.inline_tensor(utri, name="utri")
    onesc = nc.inline_tensor(np.ones((128, 1), dtype=BFNP), name="onesc")

    with ExitStack() as ctx:
        En = ctx.enter_context
        XT = En(nc.sbuf_tensor("XT", [128, 8 * T], BF16))
        ATTNT = XT
        XBF = En(nc.sbuf_tensor("XBF", [128, 16 * 1024], BF16))
        ATTN = XBF
        QR = En(nc.sbuf_tensor("QR", [128, T], BF16))
        KR = En(nc.sbuf_tensor("KR", [128, T], BF16))
        VTF = En(nc.sbuf_tensor("VTF", [128, T], BF16))
        VTM = En(nc.sbuf_tensor("VTM", [128, T], BF16))
        RTMP = En(nc.sbuf_tensor("RTMP", [128, T], BF16))
        TT2 = En(nc.sbuf_tensor("TT2", [128, 1024], BF16))
        UST = En(nc.sbuf_tensor("UST", [128, 16 * 1024], BF16))
        WQP = En(nc.sbuf_tensor("WQP", [128, 1024], BF16))
        WKP = En(nc.sbuf_tensor("WKP", [128, 1024], BF16))
        WVP = En(nc.sbuf_tensor("WVP", [128, 1024], BF16))
        WQRP = En(nc.sbuf_tensor("WQRP", [128, 1024], BF16))
        WKRP = En(nc.sbuf_tensor("WKRP", [128, 1024], BF16))
        WOS = En(nc.sbuf_tensor("WOS", [128, 8 * 1024], BF16))
        COS = En(nc.sbuf_tensor("COS", [128, T], BF16))
        SIN = En(nc.sbuf_tensor("SIN", [128, T], BF16))
        UTS = En(nc.sbuf_tensor("UTS", [128, 128], BF16))
        XRES = En(nc.sbuf_tensor("XRES", [128, 2 * 1024], F32))
        X3O = En(nc.sbuf_tensor("X3O", [128, 2 * 1024], F32))
        SQS = En(nc.sbuf_tensor("SQS", [128, 1024], BF16))
        LNS = En(nc.sbuf_tensor("LNS", [128, 16], F32))
        IDS = En(nc.sbuf_tensor("IDS", [128, 128], BF16))
        EPS = En(nc.sbuf_tensor("EPS", [128, 1], F32))
        ONC = En(nc.sbuf_tensor("ONC", [128, 1], BF16))

        psPF = En(nc.psum_tensor("psPF", [128, 1024], F32))
        psQKa = En(nc.psum_tensor("psQKa", [128, 512], F32))
        psQKb = En(nc.psum_tensor("psQKb", [128, 512], F32))
        psAV2 = [En(nc.psum_tensor(f"psAV{i}", [128, 65], F32)) for i in range(2)]
        psT = En(nc.psum_tensor("psT", [128, 1024], BF16))

        s_ld = Ctr(ctx, nc, "s_ld")
        s_gd = Ctr(ctx, nc, "s_gd")
        s_pe = Ctr(ctx, nc, "s_pe")
        s_sc = Ctr(ctx, nc, "s_sc")
        s_dv = Ctr(ctx, nc, "s_dv")
        s_w = Ctr(ctx, nc, "s_w")
        s_wo = Ctr(ctx, nc, "s_wo")
        s_x = [Ctr(ctx, nc, "s_x0"), Ctr(ctx, nc, "s_x1")]
        s_y = [Ctr(ctx, nc, "s_y0"), Ctr(ctx, nc, "s_y1")]
        block = En(nc.Block())

        ld0 = s_ld.take(16 * 6)

        @block.sync
        def _(sync):
            sync.dma_start(out=IDS[:], in_=ident.ap()).then_inc(s_ld.h, 16)
            sync.dma_start(out=EPS[:], in_=epsc.ap()).then_inc(s_ld.h, 16)
            sync.dma_start(out=COS[:], in_=cosc.ap()).then_inc(s_ld.h, 16)
            sync.dma_start(out=SIN[:], in_=sinc.ap()).then_inc(s_ld.h, 16)
            sync.dma_start(out=UTS[:], in_=utric.ap()).then_inc(s_ld.h, 16)
            sync.dma_start(out=ONC[:], in_=onesc.ap()).then_inc(s_ld.h, 16)

        gd_wo = s_wo.take(16)

        @block.gpsimd
        def _(g):
            src = bass.AP(woT.tensor, 0, [[1024, 128], [131072, 8], [1, 1024]])
            g.dma_start(out=WOS[:].rearrange("p (i o) -> p i o", i=8),
                        in_=src).then_inc(s_wo.h, 16)

        for b in range(B):
            pe_before_x = s_pe.v
            dv_before_x = s_dv.v
            gd_x = s_gd.take(16)

            @block.gpsimd
            def _(g, b=b, pe_before_x=pe_before_x, dv_before_x=dv_before_x):
                g.wait_ge(s_pe.h, pe_before_x)
                g.wait_ge(s_dv.h, dv_before_x)
                src = bass.AP(x2i.tensor, b * 16 * 128 * 1024,
                              [[1024, 128], [128 * 1024, 16], [1, 1024]])
                g.dma_start(out=XBF[:].rearrange("p (c d) -> p c d", c=16),
                            in_=src).then_inc(s_gd.h, 16)

            pe1 = s_pe.take(16)
            dv1 = s_dv.take(16)

            @block.tensor
            def _(t, gd_x=gd_x, pe1=pe1, dv1=dv1):
                t.wait_ge(s_gd.h, gd_x + 16)
                t.wait_ge(s_ld.h, ld0 + 96)
                for ci in range(16):
                    if ci > 0:
                        t.wait_ge(s_dv.h, dv1 + ci)
                    for ch in range(8):
                        last = t.transpose(
                            psT[:, 128 * ch:128 * (ch + 1)],
                            XBF[:, ci * 1024 + 128 * ch:ci * 1024 + 128 * (ch + 1)],
                            IDS[:])
                    last.then_inc(s_pe.h, 1)

            @block.vector
            def _(v, pe1=pe1, dv1=dv1):
                cur = [dv1]
                op = mkop(v, s_dv, cur)
                for ci in range(16):
                    v.wait_ge(s_pe.h, pe1 + ci + 1)
                    out_ap = bass.AP(XT[:].tensor, XT[:].offset + 128 * ci,
                                     [[8 * T, 128], [T, 8], [1, 128]])
                    op("tensor_copy", out_ap,
                       psT[:].rearrange("p (c f) -> p c f", c=8))

            for hp in range(8):
                wld = s_w.take(16 * 11)
                pe_prev_pair = s_pe.v
                dv_prev_pair = s_dv.v

                @block.gpsimd
                def _(g, hp=hp, pe_prev_pair=pe_prev_pair):
                    g.wait_ge(s_pe.h, pe_prev_pair)
                    for wdram, wsb in ((wqT, WQP), (wkT, WKP), (wvT, WVP)):
                        src = bass.AP(wdram.tensor, 128 * hp,
                                      [[1024, 128], [131072, 8], [1, 128]])
                        g.dma_start(out=wsb[:].rearrange("p (i o) -> p i o", i=8),
                                    in_=src).then_inc(s_w.h, 16)
                    for wdram, wsb in ((wqT, WQRP), (wkT, WKRP)):
                        for half in range(2):
                            for hl in range(2):
                                src = bass.AP(
                                    wdram.tensor,
                                    128 * hp + 64 * hl + (1 - half) * 32,
                                    [[1024, 128], [131072, 8], [1, 32]])
                                dst = bass.AP(
                                    wsb[:].tensor,
                                    wsb[:].offset + 64 * hl + half * 32,
                                    [[1024, 128], [128, 8], [1, 32]])
                                g.dma_start(out=dst, in_=src).then_inc(s_w.h, 16)
                dv_neg = s_dv.take(2)

                @block.vector
                def _(v, wld=wld, dv_neg=dv_neg):
                    cur = [dv_neg]
                    op = mkop(v, s_dv, cur)
                    v.wait_ge(s_w.h, wld + 16 * 11)
                    for wsb in (WQRP, WKRP):
                        neg = bass.AP(wsb[:].tensor, wsb[:].offset,
                                      [[1024, 128], [64, 16], [1, 32]])
                        op("tensor_scalar", neg, neg, -1.0, None, ALU.mult)

                projs = [("qrot", WQRP, None), ("q", WQP, QR),
                         ("krot", WKRP, None), ("k", WKP, KR),
                         ("v", WVP, VTF)]
                pe_pf = s_pe.take(5 * 2)
                dv_pf = s_dv.take(2 * (1 + 3 + 1 + 3 + 1))

                @block.tensor
                def _(t, b=b, hp=hp, wld=wld, pe_pf=pe_pf, dv_pf=dv_pf,
                      dv_neg=dv_neg, dv1=dv1):
                    t.wait_ge(s_dv.h, dv_neg + 2)
                    t.wait_ge(s_dv.h, dv1 + 16)
                    k = 0
                    for pi, (nm, wsb, dst) in enumerate(projs):
                        for th in range(2):
                            t.wait_ge(s_dv.h, dv_pf + _PF_EVAC_BEFORE[k])
                            for ich in range(8):
                                for half in range(2):
                                    last = t.matmul(
                                        psPF[:, 512 * (th % 1) + 512 * half:
                                             512 * (th % 1) + 512 * (half + 1)]
                                        if False else
                                        psPF[:, 512 * half:512 * (half + 1)],
                                        wsb[:, 128 * ich:128 * (ich + 1)],
                                        XT[:, ich * T + 1024 * th + 512 * half:
                                           ich * T + 1024 * th + 512 * (half + 1)],
                                        start=(ich == 0), stop=(ich == 7))
                            last.then_inc(s_pe.h, 1)
                            k += 1

                @block.vector
                def _(v, hp=hp, pe_pf=pe_pf, dv_pf=dv_pf):
                    cur = [dv_pf]
                    op = mkop(v, s_dv, cur)
                    k = 0
                    for pi, (nm, wsb, dst) in enumerate(projs):
                        for th in range(2):
                            v.wait_ge(s_pe.h, pe_pf + k + 1)
                            sl = slice(1024 * th, 1024 * (th + 1))
                            if nm in ("qrot", "krot"):
                                op("tensor_copy", RTMP[:, sl], psPF[:])
                            elif nm == "v":
                                op("tensor_copy", VTF[:, sl], psPF[:])
                            else:
                                op("tensor_tensor", dst[:, sl], psPF[:],
                                   COS[:, sl], ALU.mult)
                                op("tensor_tensor", TT2[:], RTMP[:, sl],
                                   SIN[:, sl], ALU.mult)
                                op("tensor_tensor", dst[:, sl], dst[:, sl],
                                   TT2[:], ALU.add)
                            k += 1

                pe_vt = s_pe.take(2)
                dv_vt = s_dv.take(2)

                @block.tensor
                def _(t, pe_vt=pe_vt, dv_pf=dv_pf, dv_vt=dv_vt):
                    t.wait_ge(s_dv.h, dv_pf + 18)
                    for half in range(2):
                        if half == 1:
                            t.wait_ge(s_dv.h, dv_vt + 1)
                        for j in range(8):
                            last = t.transpose(
                                psT[:, 128 * j:128 * (j + 1)],
                                VTF[:, 1024 * half + 128 * j:
                                    1024 * half + 128 * (j + 1)], IDS[:])
                        last.then_inc(s_pe.h, 1)

                @block.vector
                def _(v, pe_vt=pe_vt, dv_vt=dv_vt):
                    cur = [dv_vt]
                    op = mkop(v, s_dv, cur)
                    for half in range(2):
                        v.wait_ge(s_pe.h, pe_vt + half + 1)
                        op("tensor_copy", VTM[:, 1024 * half:1024 * (half + 1)],
                           psT[:])

                for tq in range(4):
                    nr = 4 * tq + 4
                    pe_qk = s_pe.take(nr)
                    sc_ex = s_sc.take(2 * nr)
                    dv_msk = s_dv.take(2 * 4)

                    @block.tensor
                    def _(t, hp=hp, tq=tq, nr=nr, pe_qk=pe_qk, sc_ex=sc_ex,
                          dv_vt=dv_vt):
                        t.wait_ge(s_dv.h, dv_vt + 2)
                        for r in range(nr):
                            if r > 0:
                                t.wait_ge(s_sc.h, sc_ex + 2 * r)
                            t.matmul(psQKa[:], KR[0:64, 128 * r:128 * (r + 1)],
                                     QR[0:64, 512 * tq:512 * (tq + 1)],
                                     start=True, stop=True,
                                     tile_position=(0, 0))
                            t.matmul(psQKb[:], KR[64:128, 128 * r:128 * (r + 1)],
                                     QR[64:128, 512 * tq:512 * (tq + 1)],
                                     start=True, stop=True,
                                     tile_position=(64, 0)).then_inc(s_pe.h, 1)

                    @block.scalar
                    def _(s, hp=hp, tq=tq, nr=nr, pe_qk=pe_qk, sc_ex=sc_ex,
                          dv_msk=dv_msk):
                        for r in range(nr):
                            s.wait_ge(s_pe.h, pe_qk + r + 1)
                            if r == 0:
                                pass
                            s.activation(UST[:, 1024 * r:1024 * r + 512],
                                         psQKa[:], AF.Exp, scale=0.125
                                         ).then_inc(s_sc.h, 1)
                            s.activation(UST[:, 1024 * r + 512:1024 * (r + 1)],
                                         psQKb[:], AF.Exp, scale=0.125
                                         ).then_inc(s_sc.h, 1)

                    @block.vector
                    def _(v, tq=tq, nr=nr, sc_ex=sc_ex, dv_msk=dv_msk):
                        cur = [dv_msk]
                        op = mkop(v, s_dv, cur)
                        for tbl in range(4):
                            r = 4 * tq + tbl
                            v.wait_ge(s_sc.h, sc_ex + 2 * (r + 1))
                            for hl in range(2):
                                sl = slice(1024 * r + 512 * hl + 128 * tbl,
                                           1024 * r + 512 * hl + 128 * (tbl + 1))
                                op("tensor_tensor", UST[:, sl], UST[:, sl],
                                   UTS[:], ALU.mult)

                    pe_av = s_pe.take(8)
                    dv_av = s_dv.take(8 * 3)

                    @block.tensor
                    def _(t, b=b, hp=hp, tq=tq, nr=nr, sc_ex=sc_ex,
                          dv_msk=dv_msk, pe_av=pe_av, dv_av=dv_av):
                        t.wait_ge(s_sc.h, sc_ex + 2 * nr)
                        t.wait_ge(s_dv.h, dv_msk + 8)
                        for tbl in range(4):
                            tb = 4 * tq + tbl
                            for hl in range(2):
                                if tbl > 0:
                                    t.wait_ge(s_dv.h,
                                              dv_av + 3 * (2 * (tbl - 1) + hl + 1))
                                psv = psAV2[hl]
                                for r in range(tb + 1):
                                    t.matmul(psv[:, 0:64],
                                             UST[:, 1024 * r + 512 * hl + 128 * tbl:
                                                 1024 * r + 512 * hl + 128 * (tbl + 1)],
                                             VTM[:, 128 * r + 64 * hl:
                                                 128 * r + 64 * (hl + 1)],
                                             start=(r == 0), stop=False,
                                             skip_group_check=True)
                                    last = t.matmul(
                                        psv[:, 64:65],
                                        UST[:, 1024 * r + 512 * hl + 128 * tbl:
                                            1024 * r + 512 * hl + 128 * (tbl + 1)],
                                        ONC[:], start=False, stop=(r == tb),
                                        skip_group_check=True)
                                last.then_inc(s_pe.h, 1)

                    @block.vector
                    def _(v, b=b, hp=hp, tq=tq, pe_av=pe_av, dv_av=dv_av):
                        cur = [dv_av]
                        op = mkop(v, s_dv, cur)
                        for tbl in range(4):
                            tb = 4 * tq + tbl
                            for hl in range(2):
                                v.wait_ge(s_pe.h, pe_av + 2 * tbl + hl + 1)
                                psv = psAV2[hl]
                                op("tensor_copy", LNS[:, 8 + hl:9 + hl],
                                   psv[:, 64:65])
                                op("reciprocal", LNS[:, 10 + hl:11 + hl],
                                   LNS[:, 8 + hl:9 + hl])
                                op("tensor_scalar",
                                   ATTN[:, tb * 1024 + 128 * hp + 64 * hl:
                                        tb * 1024 + 128 * hp + 64 * (hl + 1)],
                                   psv[:, 0:64],
                                   LNS[:, 10 + hl:11 + hl], None, ALU.mult)

            pe_at = s_pe.take(16)
            dv_at = s_dv.take(16)
            dv_all_attn = s_dv.v - 16

            @block.tensor
            def _(t, b=b, pe_at=pe_at, dv_at=dv_at, dv_all=s_dv.v - 16):
                t.wait_ge(s_dv.h, dv_all)
                for ci in range(16):
                    if ci > 0:
                        t.wait_ge(s_dv.h, dv_at + ci)
                    for ch in range(8):
                        last = t.transpose(
                            psT[:, 128 * ch:128 * (ch + 1)],
                            ATTN[:, ci * 1024 + 128 * ch:ci * 1024 + 128 * (ch + 1)],
                            IDS[:])
                    last.then_inc(s_pe.h, 1)

            @block.vector
            def _(v, pe_at=pe_at, dv_at=dv_at):
                cur = [dv_at]
                op = mkop(v, s_dv, cur)
                for ci in range(16):
                    v.wait_ge(s_pe.h, pe_at + ci + 1)
                    out_ap = bass.AP(ATTNT[:].tensor, ATTNT[:].offset + 128 * ci,
                                     [[8 * T, 128], [T, 8], [1, 128]])
                    op("tensor_copy", out_ap,
                       psT[:].rearrange("p (c f) -> p c f", c=8))

            pe_op = s_pe.take(16)
            x8 = [s_x[0].take(16 * 8), s_x[1].take(16 * 8)]
            y8 = [s_y[0].take(16 * 8), s_y[1].take(16 * 8)]
            dv8 = s_dv.take(16 * 7)
            sc8 = s_sc.take(16 * 2)

            @block.tensor
            def _(t, b=b, dv_at=dv_at, pe_op=pe_op, dv8=dv8, gd_wo=gd_wo):
                t.wait_ge(s_dv.h, dv_at + 16)
                t.wait_ge(s_wo.h, gd_wo + 16)
                for tb in range(16):
                    if tb > 0:
                        t.wait_ge(s_dv.h, dv8 + 7 * (tb - 1) + 1)
                    for ich in range(8):
                        stat = ATTNT[:, ich * T + 128 * tb:ich * T + 128 * (tb + 1)]
                        for half in range(2):
                            last = t.matmul(
                                psPF[:, 512 * half:512 * (half + 1)], stat,
                                WOS[:, 1024 * ich + 512 * half:
                                    1024 * ich + 512 * (half + 1)],
                                start=(ich == 0), stop=(ich == 7))
                    last.then_inc(s_pe.h, 1)

            @block.sync
            def _(sync, b=b, dv8=dv8):
                for ci in range(16):
                    sync.wait_ge(s_dv.h, dv8 + 7 * (ci - 2) + 7 if ci >= 2 else dv8)
                    src = bass.AP(x2i.tensor, (b * 16 + ci) * 128 * 1024,
                                  [[1024, 128], [1, 1024]])
                    sync.dma_start(out=XRES[:, 1024 * (ci % 2):1024 * (ci % 2 + 1)],
                                   in_=src).then_inc(s_x[ci % 2].h, 16)

            @block.vector
            def _(v, b=b, x8=x8, y8=y8, sc8=sc8, dv8=dv8, pe_op=pe_op):
                cur = [dv8]
                op = mkop(v, s_dv, cur)
                for ci in range(16):
                    xs = XRES[:, 1024 * (ci % 2):1024 * (ci % 2 + 1)]
                    v.wait_ge(s_x[ci % 2].h, x8[ci % 2] + 16 * (ci // 2 + 1))
                    v.wait_ge(s_pe.h, pe_op + ci + 1)
                    op("tensor_tensor", xs, xs, psPF[:], ALU.add)
                    op("reduce_sum", LNS[:, 0:1], xs, axis=AX.X, negate=True)
                    op("tensor_scalar", LNS[:, 1:2], LNS[:, 0:1], 1.0 / D, None,
                       ALU.mult)
                    op("tensor_tensor", LNS[:, 6:7], LNS[:, 1:2], LNS[:, 1:2],
                       ALU.mult)
                    v.wait_ge(s_sc.h, sc8 + 2 * ci + 1)
                    op("scalar_tensor_tensor", LNS[:, 2:3], LNS[:, 3:4], 1.0 / D,
                       LNS[:, 6:7], ALU.mult, ALU.subtract)
                    v.wait_ge(s_sc.h, sc8 + 2 * ci + 2)
                    op("reciprocal", LNS[:, 5:6], LNS[:, 4:5])
                    ob = X3O[:, 1024 * (ci % 2):1024 * (ci % 2 + 1)]
                    v.wait_ge(s_y[ci % 2].h,
                              y8[ci % 2] + 16 * (ci // 2) if ci >= 2 else y8[ci % 2])
                    op("tensor_scalar", ob, xs, LNS[:, 1:2], LNS[:, 5:6],
                       ALU.add, ALU.mult)

            @block.scalar
            def _(s, b=b, dv8=dv8, sc8=sc8):
                for ci in range(16):
                    xs = XRES[:, 1024 * (ci % 2):1024 * (ci % 2 + 1)]
                    s.wait_ge(s_dv.h, dv8 + 7 * ci + 1)
                    s.activation(SQS[:], xs, AF.Square,
                                 accum_out=LNS[:, 3:4]).then_inc(s_sc.h, 1)
                    s.wait_ge(s_dv.h, dv8 + 7 * ci + 5)
                    s.activation(LNS[:, 4:5], LNS[:, 2:3], AF.Sqrt,
                                 bias=EPS[:, 0:1]).then_inc(s_sc.h, 1)

            @block.gpsimd
            def _(g, b=b, dv8=dv8, y8=y8):
                for ci in range(16):
                    g.wait_ge(s_dv.h, dv8 + 7 * (ci + 1))
                    dst = bass.AP(x3.tensor, (b * 16 + ci) * 128 * 1024,
                                  [[1024, 128], [1, 1024]])
                    g.dma_start(
                        out=dst, in_=X3O[:, 1024 * (ci % 2):1024 * (ci % 2 + 1)]
                    ).then_inc(s_y[ci % 2].h, 16)

        @block.gpsimd
        def _(g):
            g.wait_ge(s_y[0].h, s_y[0].v)
            g.wait_ge(s_y[1].h, s_y[1].v)

    return nc


_PF_EVAC_BEFORE = {}
_k = 0
_dv_count = 0
for _pi, _nm in enumerate(["qrot", "q", "krot", "k", "v"]):
    for _th in range(2):
        _PF_EVAC_BEFORE[_k] = _dv_count
        _dv_count += 1 if _nm in ("qrot", "krot", "v") else 3
        _k += 1



def build_l1ffn():
    DH = 4096
    nc = bass.Bass("TRN2", target_bir_lowering=False, debug=False)
    x3i = nc.dram_tensor("x3i", [NT, D], F32, kind="ExternalInput").ap()
    w1T = nc.dram_tensor("w1T", [D, DH], F32, kind="ExternalInput").ap()
    w2T = nc.dram_tensor("w2T", [DH, D], F32, kind="ExternalInput").ap()
    cwL = nc.dram_tensor("cwL", [128, 96], F32, kind="ExternalInput").ap()
    cbL = nc.dram_tensor("cbL", [128, 32], F32, kind="ExternalInput").ap()
    out = nc.dram_tensor("out", [NT, D], F32, kind="ExternalOutput").ap()
    t1d = nc.dram_tensor("t1d", [D, DH], BF16, kind="Internal").ap()
    t2d = nc.dram_tensor("t2d", [DH, D], BF16, kind="Internal").ap()

    ident = nc.inline_tensor(np.eye(128, dtype=BFNP), name="identbf")
    epsc = nc.inline_tensor(np.full((128, 1), 1e-5, dtype=np.float32), name="epsc")
    onesf = nc.inline_tensor(np.ones((128, 1), dtype=np.float32), name="onesf")
    onesr = nc.inline_tensor(np.ones((1, 128), dtype=np.float32), name="onesr")

    NW = 514

    with ExitStack() as ctx:
        En = ctx.enter_context
        XQT = En(nc.sbuf_tensor("XQT", [128, 8 * T], BF16))
        XQD = En(nc.sbuf_tensor("XQD", [128, 16 * 1024], BF16))
        HFC = En(nc.sbuf_tensor("HFC", [128, 32 * 516], BF16))
        SFM = En(nc.sbuf_tensor("SFM", [128, 32 * 512], BF16))
        XQ2D = XQD
        XQ2F = En(nc.sbuf_tensor("XQ2F", [128, 32 * 512], BF16))
        W1S = XQT[:, 0:8192].bitcast(F32)
        TRS = XQT[:, 8192:16384].bitcast(F32)
        TRN = XQD[:, 0:4096]
        TS1 = En(nc.sbuf_tensor("TS1", [128, 1024], BF16))
        TS2 = En(nc.sbuf_tensor("TS2", [128, 1024], BF16))
        CW = En(nc.sbuf_tensor("CW", [128, 96], F32))
        CB = En(nc.sbuf_tensor("CB", [128, 32], F32))
        AB = En(nc.sbuf_tensor("AB", [128, 40], F32))
        SC = En(nc.sbuf_tensor("SC", [128, 8], F32))
        SCT = En(nc.sbuf_tensor("SCT", [1, 8], F32))
        DQ = En(nc.sbuf_tensor("DQ", [128, 64], F32))
        DQ2 = En(nc.sbuf_tensor("DQ2", [128, 8], F32))
        XRES = En(nc.sbuf_tensor("XRES", [128, 2 * 1024], F32))
        XOO = En(nc.sbuf_tensor("XOO", [128, 2 * 1024], F32))
        SQS = En(nc.sbuf_tensor("SQS", [128, 1024], BF16))
        LNS = En(nc.sbuf_tensor("LNS", [128, 8], F32))
        IDS = En(nc.sbuf_tensor("IDS", [128, 128], BF16))
        EPS = En(nc.sbuf_tensor("EPS", [128, 1], F32))
        ONF = En(nc.sbuf_tensor("ONF", [128, 1], F32))
        ON1 = En(nc.sbuf_tensor("ON1", [1, 128], F32))

        psHa = En(nc.psum_tensor("psHa", [128, 258], F32))
        psHb = En(nc.psum_tensor("psHb", [128, 258], F32))
        psT = En(nc.psum_tensor("psT", [128, 1024], BF16))
        psO = [En(nc.psum_tensor(f"psO{i}", [128, 1024], F32)) for i in range(2)]
        psS = psHa[0:1, 0:1]
        psBC = psHb[:, 0:1]

        s_ld = Ctr(ctx, nc, "s_ld")
        s_gd = Ctr(ctx, nc, "s_gd")
        s_pe = Ctr(ctx, nc, "s_pe")
        s_sc = Ctr(ctx, nc, "s_sc")
        s_dv = Ctr(ctx, nc, "s_dv")
        s_w1 = Ctr(ctx, nc, "s_w1")
        s_w2 = Ctr(ctx, nc, "s_w2")
        s_ts = Ctr(ctx, nc, "s_ts")
        s_t2 = Ctr(ctx, nc, "s_t2")
        s_x = [Ctr(ctx, nc, "s_x0"), Ctr(ctx, nc, "s_x1")]
        s_q = [Ctr(ctx, nc, "s_q0"), Ctr(ctx, nc, "s_q1")]
        s_y = [Ctr(ctx, nc, "s_y0"), Ctr(ctx, nc, "s_y1")]
        block = En(nc.Block())

        ld0 = s_ld.take(16 * 6)

        @block.sync
        def _(sync):
            sync.dma_start(out=IDS[:], in_=ident.ap()).then_inc(s_ld.h, 16)
            sync.dma_start(out=EPS[:], in_=epsc.ap()).then_inc(s_ld.h, 16)
            sync.dma_start(out=ONF[:], in_=onesf.ap()).then_inc(s_ld.h, 16)
            sync.dma_start(out=ON1[:], in_=onesr.ap()).then_inc(s_ld.h, 16)
            sync.dma_start(out=CW[:], in_=cwL).then_inc(s_ld.h, 16)
            sync.dma_start(out=CB[:], in_=cbL).then_inc(s_ld.h, 16)

        for wi, (wdram, nt, tdram, scol, mcol) in enumerate(
                [(w1T, 8, t1d, 0, 1), (w2T, 32, t2d, 2, 3)]):
            wid = DH if wi == 0 else 1024
            w1l = s_w1.take(16 * nt)
            dvA = s_dv.take(nt + 3)
            peA = s_pe.take(1)
            dvB = s_dv.take(6)
            scA = s_sc.take(0)

            @block.sync
            def _(sync, wdram=wdram, nt=nt, wid=wid, w1l=w1l, dvA=dvA):
                sync.wait_ge(s_dv.h, dvA)
                for i in range(nt):
                    if i >= 1:
                        sync.wait_ge(s_dv.h, dvA + i)
                    src = bass.AP(wdram.tensor, i * 128 * wid,
                                  [[wid, 128], [1, wid]])
                    sync.dma_start(out=W1S[:, 0:wid], in_=src
                                   ).then_inc(s_w1.h, 16)

            @block.vector
            def _(v, nt=nt, wid=wid, w1l=w1l, dvA=dvA):
                cur = [dvA]
                op = mkop(v, s_dv, cur)
                for i in range(nt):
                    v.wait_ge(s_w1.h, w1l + 16 * (i + 1))
                    op("tensor_reduce", AB[:, i:i + 1], W1S[:, 0:wid],
                       axis=AX.X, op=ALU.add, apply_absolute_value=True)
                op("tensor_reduce", AB[:, 36:37], AB[:, 0:nt], axis=AX.X,
                   op=ALU.add)
                op("tensor_copy", AB[:, 37:38], AB[:, 36:37])
                op("tensor_copy", AB[:, 38:39], AB[:, 36:37])

            @block.tensor
            def _(t, dvA=dvA, peA=peA, nt=nt):
                t.wait_ge(s_dv.h, dvA + nt + 3)
                t.wait_ge(s_ld.h, ld0 + 96)
                t.matmul(psS, ONF[:], AB[:, 36:37], start=True, stop=True
                         ).then_inc(s_pe.h, 1)

            @block.vector
            def _(v, peA=peA, dvB=dvB, wi=wi, scol=scol, mcol=mcol, wid=wid,
                  nt=nt):
                cur = [dvB]
                op = mkop(v, s_dv, cur)
                v.wait_ge(s_pe.h, peA + 1)
                op("tensor_copy", SCT[:, 0:1], psS)
                op("tensor_scalar", SCT[:, 1:2], SCT[:, 0:1],
                   1.0 / (nt * 128 * wid), None, ALU.mult)
                op("tensor_scalar", SCT[:, 1:2], SCT[:, 1:2], 1e-5, None,
                   ALU.max)
                op("reciprocal", SCT[:, 2:3], SCT[:, 1:2])
                op("tensor_copy", SCT[:, 4:5], SCT[:, 2:3])
                op("tensor_copy", SCT[:, 5:6], SCT[:, 1:2])

            peB = s_pe.take(1)
            dvC = s_dv.take(1)

            @block.tensor
            def _(t, dvB=dvB, peB=peB):
                t.wait_ge(s_dv.h, dvB + 6)
                t.matmul(psBC.tensor_or_self() if False else psHb[:, 0:2],
                         ON1[:], SCT[:, 4:6], start=True, stop=True
                         ).then_inc(s_pe.h, 1)

            @block.vector
            def _(v, peB=peB, dvC=dvC, scol=scol):
                cur = [dvC]
                op = mkop(v, s_dv, cur)
                v.wait_ge(s_pe.h, peB + 1)
                op("tensor_copy", SC[:, scol:scol + 2], psHb[:, 0:2])

            w2l = s_w1.take(16 * nt)
            dvD = s_dv.take(nt * 3)
            gdT = s_gd.take(16 * nt)

            @block.sync
            def _(sync, wdram=wdram, nt=nt, wid=wid, w2l=w2l, dvD=dvD,
                  dvC=dvC):
                for i in range(nt):
                    sync.wait_ge(s_dv.h, dvD + 3 * i if i >= 1 else dvC + 1)
                    src = bass.AP(wdram.tensor, i * 128 * wid,
                                  [[wid, 128], [1, wid]])
                    sync.dma_start(out=W1S[:, 0:wid], in_=src
                                   ).then_inc(s_w1.h, 16)

            @block.vector
            def _(v, nt=nt, wid=wid, w2l=w2l, dvD=dvD, scol=scol, gdT=gdT):
                cur = [dvD]
                op = mkop(v, s_dv, cur)
                v.wait_ge(s_gd.h, gdT)
                for i in range(nt):
                    v.wait_ge(s_w1.h, w2l + 16 * (i + 1))
                    if i >= 1:
                        v.wait_ge(s_gd.h, gdT + 16 * i)
                    op("tensor_scalar", TRS[:, 0:wid], W1S[:, 0:wid],
                       SC[:, scol:scol + 1], MAGIC, ALU.mult, ALU.add)
                    op("tensor_scalar", TRS[:, 0:wid], TRS[:, 0:wid],
                       -MAGIC, 1.0, ALU.add, ALU.min)
                    op("tensor_scalar", TRN[:, 0:wid], TRS[:, 0:wid],
                       -1.0, None, ALU.max)

            @block.gpsimd
            def _(g, nt=nt, wid=wid, tdram=tdram, dvD=dvD, gdT=gdT):
                for i in range(nt):
                    g.wait_ge(s_dv.h, dvD + 3 * (i + 1))
                    dst = bass.AP(tdram.tensor, i * 128 * wid,
                                  [[wid, 128], [1, wid]])
                    g.dma_start(out=dst, in_=TRN[:, 0:wid]).then_inc(s_gd.h, 16)

        dvCW = s_dv.take(2)

        @block.vector
        def _(v, dvCW=dvCW):
            cur = [dvCW]
            op = mkop(v, s_dv, cur)
            v.wait_ge(s_ld.h, ld0 + 96)
            op("tensor_scalar", CW[:], CW[:], SC[:, 1:2], None, ALU.mult)
            op("tensor_copy", AB[:, 39:40], SC[:, 3:4])

        for b in range(B):
            xq = [s_q[0].take(16 * 8), s_q[1].take(16 * 8)]
            dvQ = s_dv.take(16 * 7)

            @block.sync
            def _(sync, b=b, xq=xq, dvQ=dvQ):
                for ci in range(16):
                    sync.wait_ge(s_dv.h, dvQ + 7 * (ci - 2) + 7 if ci >= 2 else dvQ)
                    src = bass.AP(x3i.tensor, (b * 16 + ci) * 128 * 1024,
                                  [[1024, 128], [1, 1024]])
                    sync.dma_start(out=XRES[:, 1024 * (ci % 2):1024 * (ci % 2 + 1)],
                                   in_=src).then_inc(s_q[ci % 2].h, 16)

            @block.vector
            def _(v, b=b, xq=xq, dvQ=dvQ, ypv0=s_y[0].v, ypv1=s_y[1].v,
                  pepv=s_pe.v, gdpv=s_gd.v):
                cur = [dvQ]
                op = mkop(v, s_dv, cur)
                v.wait_ge(s_y[0].h, ypv0)
                v.wait_ge(s_y[1].h, ypv1)
                v.wait_ge(s_pe.h, pepv)
                v.wait_ge(s_gd.h, gdpv)
                for ci in range(16):
                    xs = XRES[:, 1024 * (ci % 2):1024 * (ci % 2 + 1)]
                    v.wait_ge(s_q[ci % 2].h, xq[ci % 2] + 16 * (ci // 2 + 1))
                    op("reduce_max", DQ[:, 4 * ci:4 * ci + 1], xs, axis=AX.X,
                       apply_absolute_value=True)
                    op("tensor_scalar", DQ[:, 4 * ci:4 * ci + 1],
                       DQ[:, 4 * ci:4 * ci + 1], 1e-5, None, ALU.max)
                    op("reciprocal", DQ[:, 4 * ci + 1:4 * ci + 2],
                       DQ[:, 4 * ci:4 * ci + 1])
                    op("tensor_scalar", DQ[:, 4 * ci + 1:4 * ci + 2],
                       DQ[:, 4 * ci + 1:4 * ci + 2], 127.0, None, ALU.mult)
                    op("tensor_scalar", DQ[:, 4 * ci + 2:4 * ci + 3],
                       DQ[:, 4 * ci:4 * ci + 1], 1.0 / 127.0, None, ALU.mult)
                    op("tensor_scalar", XOO[:, 0:1024], xs,
                       DQ[:, 4 * ci + 1:4 * ci + 2], MAGIC, ALU.mult, ALU.add)
                    op("tensor_scalar", XQD[:, ci * 1024:(ci + 1) * 1024],
                       XOO[:, 0:1024],
                       -MAGIC, DQ[:, 4 * ci + 2:4 * ci + 3], ALU.add, ALU.mult)

            pe1 = s_pe.take(16)
            dv1 = s_dv.take(16)

            @block.tensor
            def _(t, b=b, dvQ=dvQ, pe1=pe1, dv1=dv1):
                t.wait_ge(s_dv.h, dvQ + 112)
                for ci in range(16):
                    if ci > 0:
                        t.wait_ge(s_dv.h, dv1 + ci)
                    for ch in range(8):
                        last = t.transpose(
                            psT[:, 128 * ch:128 * (ch + 1)],
                            XQD[:, ci * 1024 + 128 * ch:ci * 1024 + 128 * (ch + 1)],
                            IDS[:])
                    last.then_inc(s_pe.h, 1)

            @block.vector
            def _(v, pe1=pe1, dv1=dv1):
                cur = [dv1]
                op = mkop(v, s_dv, cur)
                for ci in range(16):
                    v.wait_ge(s_pe.h, pe1 + ci + 1)
                    out_ap = bass.AP(XQT[:].tensor, XQT[:].offset + 128 * ci,
                                     [[8 * T, 128], [T, 8], [1, 128]])
                    op("tensor_copy", out_ap,
                       psT[:].rearrange("p (c f) -> p c f", c=8))

            for tt in range(4):
                t0 = 512 * tt
                lo = max(t0 - 1, 0)
                hi = min(t0 + 513, 2048)
                wa = lo
                wb = t0 + 255
                na = 258 if tt > 0 else 257
                nb = 258 if tt < 3 else 257
                offa = 0 if tt > 0 else 1
                offb = 256

                tl1 = s_ts.take(16 * 32)
                pe_m1 = s_pe.take(32 * 2)
                dv_m1 = s_dv.take(32 * (2 + (1 if tt == 0 or tt == 3 else 0) + 3))
                sc_m1 = s_sc.take(32)
                nEv = 2 + (1 if tt in (0, 3) else 0) + 3

                @block.gpsimd
                def _(g, b=b, tt=tt, tl1=tl1, pe_m1=pe_m1):
                    for och in range(32):
                        if och >= 1:
                            g.wait_ge(s_pe.h, pe_m1 + 2 * och)
                        src = bass.AP(t1d.tensor, 128 * och,
                                      [[DH, 128], [128 * DH, 8], [1, 128]])
                        g.dma_start(
                            out=TS1[:].rearrange("p (i c) -> p i c", i=8),
                            in_=src).then_inc(s_ts.h, 16)

                @block.tensor
                def _(t, b=b, tt=tt, tl1=tl1, pe_m1=pe_m1, dv_m1=dv_m1,
                      dv1=dv1, wa=wa, wb=wb, na=na, nb=nb, nEv=nEv):
                    t.wait_ge(s_dv.h, dv1 + 16)
                    for och in range(32):
                        t.wait_ge(s_ts.h, tl1 + 16 * (och + 1))
                        if och > 0:
                            t.wait_ge(s_dv.h, dv_m1 + nEv * (och - 1) + 2)
                        for ich in range(8):
                            stat = TS1[:, 128 * ich:128 * (ich + 1)]
                            t.matmul(psHa[:, 0:na], stat,
                                     XQT[:, ich * T + wa:ich * T + wa + na],
                                     start=(ich == 0), stop=(ich == 7))
                            last = t.matmul(psHb[:, 0:nb], stat,
                                            XQT[:, ich * T + wb:ich * T + wb + nb],
                                            start=(ich == 0), stop=(ich == 7))
                        last.then_inc(s_pe.h, 2)

                @block.vector
                def _(v, b=b, tt=tt, pe_m1=pe_m1, dv_m1=dv_m1, sc_m1=sc_m1,
                      na=na, nb=nb, offa=offa, nEv=nEv):
                    cur = [dv_m1]
                    op = mkop(v, s_dv, cur)
                    for och in range(32):
                        v.wait_ge(s_pe.h, pe_m1 + 2 * (och + 1))
                        base = 516 * och
                        op("tensor_copy", HFC[:, base + offa:base + offa + na],
                           psHa[:, 0:na])
                        op("tensor_copy", HFC[:, base + 256:base + 256 + nb],
                           psHb[:, 0:nb])
                        if tt == 0:
                            op("memset", HFC[:, base:base + 1], 0.0)
                        elif tt == 3:
                            op("memset", HFC[:, base + 513:base + 514], 0.0)
                        op("tensor_scalar", SFM[:, 512 * och:512 * (och + 1)],
                           HFC[:, base:base + 512], CW[:, 3 * och:3 * och + 1],
                           None, ALU.mult)
                        op("scalar_tensor_tensor",
                           SFM[:, 512 * och:512 * (och + 1)],
                           HFC[:, base + 1:base + 513],
                           CW[:, 3 * och + 1:3 * och + 2],
                           SFM[:, 512 * och:512 * (och + 1)], ALU.mult, ALU.add)
                        op("scalar_tensor_tensor",
                           SFM[:, 512 * och:512 * (och + 1)],
                           HFC[:, base + 2:base + 514],
                           CW[:, 3 * och + 2:3 * och + 3],
                           SFM[:, 512 * och:512 * (och + 1)], ALU.mult, ALU.add)

                @block.scalar
                def _(s, b=b, tt=tt, dv_m1=dv_m1, sc_m1=sc_m1, nEv=nEv):
                    for och in range(32):
                        s.wait_ge(s_dv.h, dv_m1 + nEv * (och + 1))
                        s.activation(SFM[:, 512 * och:512 * (och + 1)],
                                     SFM[:, 512 * och:512 * (och + 1)], AF.Silu,
                                     bias=CB[:, och:och + 1]).then_inc(s_sc.h, 1)

                pe_t1 = s_pe.take(16)
                dv_t1 = s_dv.take(16)

                @block.tensor
                def _(t, sc_m1=sc_m1, pe_t1=pe_t1, dv_t1=dv_t1):
                    t.wait_ge(s_sc.h, sc_m1 + 32)
                    k = 0
                    for tb4 in range(4):
                        for part in range(4):
                            if k > 0:
                                t.wait_ge(s_dv.h, dv_t1 + k)
                            for j in range(8):
                                och = part * 8 + j
                                last = t.transpose(
                                    psT[:, 128 * j:128 * (j + 1)],
                                    SFM[:, 512 * och + 128 * tb4:
                                        512 * och + 128 * (tb4 + 1)], IDS[:])
                            last.then_inc(s_pe.h, 1)
                            k += 1

                @block.vector
                def _(v, pe_t1=pe_t1, dv_t1=dv_t1):
                    cur = [dv_t1]
                    op = mkop(v, s_dv, cur)
                    k = 0
                    for tb4 in range(4):
                        for part in range(4):
                            v.wait_ge(s_pe.h, pe_t1 + k + 1)
                            op("tensor_copy",
                               XQ2D[:, 4096 * tb4 + 1024 * part:
                                    4096 * tb4 + 1024 * (part + 1)], psT[:])
                            k += 1

                dv_q2 = s_dv.take(4 * 9)

                @block.vector
                def _(v, dv_q2=dv_q2, dv_t1=dv_t1):
                    cur = [dv_q2]
                    op = mkop(v, s_dv, cur)
                    v.wait_ge(s_dv.h, dv_t1 + 16)
                    for tb4 in range(4):
                        xs = XQ2D[:, 4096 * tb4:4096 * (tb4 + 1)]
                        op("reduce_max", DQ2[:, 0:1], xs, axis=AX.X,
                           apply_absolute_value=True)
                        op("tensor_scalar", DQ2[:, 0:1], DQ2[:, 0:1], 1e-5,
                           None, ALU.max)
                        op("reciprocal", DQ2[:, 1:2], DQ2[:, 0:1])
                        op("tensor_scalar", DQ2[:, 1:2], DQ2[:, 1:2], 127.0,
                           None, ALU.mult)
                        op("tensor_scalar", DQ2[:, 2 + tb4:3 + tb4], DQ2[:, 0:1],
                           1.0 / 127.0, None, ALU.mult)
                        for hf in range(2):
                            xh = XQ2D[:, 4096 * tb4 + 2048 * hf:
                                      4096 * tb4 + 2048 * (hf + 1)]
                            op("tensor_scalar", XRES[:], xh, DQ2[:, 1:2],
                               MAGIC, ALU.mult, ALU.add)
                            op("tensor_scalar", xh, XRES[:], -MAGIC, None,
                               ALU.add)

                pe_t2 = s_pe.take(16)
                dv_t2 = s_dv.take(16)

                @block.tensor
                def _(t, dv_q2=dv_q2, pe_t2=pe_t2, dv_t2=dv_t2):
                    t.wait_ge(s_dv.h, dv_q2 + 36)
                    k = 0
                    for part in range(4):
                        for tb4 in range(4):
                            if k > 0:
                                t.wait_ge(s_dv.h, dv_t2 + k)
                            for j in range(8):
                                hch = part * 8 + j
                                last = t.transpose(
                                    psT[:, 128 * j:128 * (j + 1)],
                                    XQ2D[:, 4096 * tb4 + 128 * hch:
                                        4096 * tb4 + 128 * (hch + 1)], IDS[:])
                            last.then_inc(s_pe.h, 1)
                            k += 1

                @block.vector
                def _(v, pe_t2=pe_t2, dv_t2=dv_t2):
                    cur = [dv_t2]
                    op = mkop(v, s_dv, cur)
                    k = 0
                    for part in range(4):
                        for tb4 in range(4):
                            v.wait_ge(s_pe.h, pe_t2 + k + 1)
                            out_ap = bass.AP(
                                XQ2F[:].tensor,
                                XQ2F[:].offset + 512 * 8 * part + 128 * tb4,
                                [[32 * 512, 128], [512, 8], [1, 128]])
                            op("tensor_copy", out_ap,
                               psT[:].rearrange("p (c f) -> p c f", c=8))
                            k += 1

                for tt2 in range(2):
                    tl2 = s_ts2 = s_t2.take(16 * 32)
                    pe_m2 = s_pe.take(32)
                    x8 = [s_x[0].take(16), s_x[1].take(16)]
                    y8 = [s_y[0].take(16), s_y[1].take(16)]
                    dv8 = s_dv.take(2 * 8)
                    sc8 = s_sc.take(2 * 2)

                    @block.gpsimd
                    def _(g, b=b, tt=tt, tt2=tt2, tl2=tl2, pe_m2=pe_m2):
                        for hch in range(32):
                            if hch >= 1:
                                g.wait_ge(s_pe.h, pe_m2 + hch)
                            src = bass.AP(t2d.tensor, 128 * hch * 1024,
                                          [[1024, 128], [1, 1024]])
                            g.dma_start(out=TS2[:], in_=src).then_inc(s_t2.h, 16)

                    @block.tensor
                    def _(t, b=b, tt=tt, tt2=tt2, tl2=tl2, pe_m2=pe_m2,
                          dv_t2=dv_t2, dv8=dv8):
                        t.wait_ge(s_dv.h, dv_t2 + 16)
                        if True:
                            t.wait_ge(s_dv.h, dv8)
                        for hch in range(32):
                            t.wait_ge(s_t2.h, tl2 + 16 * (hch + 1))
                            for tbl in range(2):
                                stat = XQ2F[:, 512 * hch + 256 * tt2 + 128 * tbl:
                                            512 * hch + 256 * tt2 + 128 * (tbl + 1)]
                                t.matmul(psO[tbl][:, 0:512], stat, TS2[:, 0:512],
                                         start=(hch == 0), stop=(hch == 31))
                                last = t.matmul(psO[tbl][:, 512:1024], stat,
                                                TS2[:, 512:1024],
                                                start=(hch == 0), stop=(hch == 31))
                            last.then_inc(s_pe.h, 1)

                    @block.sync
                    def _(sync, b=b, tt=tt, tt2=tt2, x8=x8, dv8=dv8):
                        for tbl in range(2):
                            ci = 4 * tt + 2 * tt2 + tbl
                            sync.wait_ge(s_dv.h, dv8)
                            src = bass.AP(x3i.tensor,
                                          (b * 16 + ci) * 128 * 1024,
                                          [[1024, 128], [1, 1024]])
                            sync.dma_start(
                                out=XRES[:, 1024 * tbl:1024 * (tbl + 1)],
                                in_=src).then_inc(s_x[tbl].h, 16)

                    @block.vector
                    def _(v, b=b, tt=tt, tt2=tt2, x8=x8, y8=y8, dv8=dv8,
                          sc8=sc8, pe_m2=pe_m2):
                        cur = [dv8]
                        op = mkop(v, s_dv, cur)
                        for tbl in range(2):
                            ci = 4 * tt + 2 * tt2 + tbl
                            tb4 = 2 * tt2 + tbl
                            xs = XRES[:, 1024 * tbl:1024 * (tbl + 1)]
                            v.wait_ge(s_x[tbl].h, x8[tbl] + 16)
                            v.wait_ge(s_pe.h, pe_m2 + 32)
                            op("tensor_tensor", LNS[:, 7:8],
                               DQ2[:, 2 + tb4:3 + tb4], SC[:, 3:4], ALU.mult)
                            op("scalar_tensor_tensor", xs, psO[tbl][:],
                               LNS[:, 7:8], xs, ALU.mult, ALU.add)
                            op("reduce_sum", LNS[:, 0:1], xs, axis=AX.X,
                               negate=True)
                            op("tensor_scalar", LNS[:, 1:2], LNS[:, 0:1],
                               1.0 / D, None, ALU.mult)
                            op("tensor_tensor", LNS[:, 6:7], LNS[:, 1:2],
                               LNS[:, 1:2], ALU.mult)
                            v.wait_ge(s_sc.h, sc8 + 2 * tbl + 1)
                            op("scalar_tensor_tensor", LNS[:, 2:3], LNS[:, 3:4],
                               1.0 / D, LNS[:, 6:7], ALU.mult, ALU.subtract)
                            v.wait_ge(s_sc.h, sc8 + 2 * tbl + 2)
                            op("reciprocal", LNS[:, 5:6], LNS[:, 4:5])
                            ob = XOO[:, 1024 * tbl:1024 * (tbl + 1)]
                            v.wait_ge(s_y[tbl].h, y8[tbl])
                            op("tensor_scalar", ob, xs, LNS[:, 1:2],
                               LNS[:, 5:6], ALU.add, ALU.mult)

                    @block.scalar
                    def _(s, b=b, dv8=dv8, sc8=sc8):
                        for tbl in range(2):
                            xs = XRES[:, 1024 * tbl:1024 * (tbl + 1)]
                            s.wait_ge(s_dv.h, dv8 + 8 * tbl + 2)
                            s.activation(SQS[:], xs, AF.Square,
                                         accum_out=LNS[:, 3:4]).then_inc(s_sc.h, 1)
                            s.wait_ge(s_dv.h, dv8 + 8 * tbl + 6)
                            s.activation(LNS[:, 4:5], LNS[:, 2:3], AF.Sqrt,
                                         bias=EPS[:, 0:1]).then_inc(s_sc.h, 1)

                    @block.gpsimd
                    def _(g, b=b, tt=tt, tt2=tt2, dv8=dv8, y8=y8):
                        for tbl in range(2):
                            ci = 4 * tt + 2 * tt2 + tbl
                            g.wait_ge(s_dv.h, dv8 + 8 * (tbl + 1))
                            dst = bass.AP(out.tensor, (b * 16 + ci) * 128 * 1024,
                                          [[1024, 128], [1, 1024]])
                            g.dma_start(
                                out=dst,
                                in_=XOO[:, 1024 * tbl:1024 * (tbl + 1)]
                            ).then_inc(s_y[tbl].h, 16)

        @block.gpsimd
        def _(g):
            g.wait_ge(s_y[0].h, s_y[0].v)
            g.wait_ge(s_y[1].h, s_y[1].v)

    return nc



_NC_CACHE = {}


def _get_nc(name, builder):
    if name not in _NC_CACHE:
        _NC_CACHE[name] = builder()
    return _NC_CACHE[name]


def kernel(**inputs):
    cores = list(range(NCORES))
    x0_full = np.ascontiguousarray(
        inputs["L0_attn_x"].reshape(16 * T, D).astype(np.float32, copy=False))
    wg = np.ascontiguousarray(inputs["L0_ffn_w_gate"].astype(np.float32, copy=False))
    cfc = np.ascontiguousarray(inputs["L0_ffn_c_fc"].astype(np.float32, copy=False))
    cpj = np.ascontiguousarray(inputs["L0_ffn_c_proj"].astype(np.float32, copy=False))
    wqT = np.ascontiguousarray(inputs["L1_attn_Wq"].astype(np.float32, copy=False).T)
    wkT = np.ascontiguousarray(inputs["L1_attn_Wk"].astype(np.float32, copy=False).T)
    wvT = np.ascontiguousarray(inputs["L1_attn_Wv"].astype(np.float32, copy=False).T)
    woT = np.ascontiguousarray(inputs["L1_attn_Wo"].astype(np.float32, copy=False).T)
    w1T = np.ascontiguousarray(inputs["L1_ffn_w1"].astype(np.float32, copy=False).T)
    w2T = np.ascontiguousarray(inputs["L1_ffn_w2"].astype(np.float32, copy=False).T)
    cw = inputs["L1_ffn_conv_w"].astype(np.float32, copy=False).reshape(4096, 3)
    cwL = np.ascontiguousarray(cw.reshape(32, 128, 3).transpose(1, 0, 2).reshape(128, 96))
    cbL = np.ascontiguousarray(
        inputs["L1_ffn_conv_b"].astype(np.float32, copy=False).reshape(32, 128).T)

    ncA = _get_nc("A", build_l0attn)
    resA = run_bass_kernel_spmd(
        ncA, [{"x0": np.ascontiguousarray(x0_full[c * NT:(c + 1) * NT]),
               "wg": wg} for c in cores], core_ids=cores)
    metaF = np.concatenate([resA.results[c]["meta"] for c in cores], axis=0)
    metaF = np.ascontiguousarray(metaF)
    x1d = [np.ascontiguousarray(resA.results[c]["x1"]) for c in cores]

    ncC = _get_nc("C", build_moe)
    resC = run_bass_kernel_spmd(
        ncC, [{"x1o": x1d[c], "metaF": metaF, "cfc": cfc, "cpj": cpj}
              for c in cores], core_ids=cores)
    x2d = [np.ascontiguousarray(resC.results[c]["x2"]) for c in cores]

    ncD = _get_nc("D", build_l1attn)
    resD = run_bass_kernel_spmd(
        ncD, [{"x2i": x2d[c], "wqT": wqT, "wkT": wkT, "wvT": wvT, "woT": woT}
              for c in cores], core_ids=cores)
    x3d = [np.ascontiguousarray(resD.results[c]["x3"]) for c in cores]

    ncE = _get_nc("E", build_l1ffn)
    resE = run_bass_kernel_spmd(
        ncE, [{"x3i": x3d[c], "w1T": w1T, "w2T": w2T, "cwL": cwL, "cbL": cbL}
              for c in cores], core_ids=cores)
    out = np.concatenate([resE.results[c]["out"] for c in cores], axis=0)
    return out.reshape(16, T, D).astype(np.float32, copy=False)


# version 02
# speedup vs baseline: 1.1288x; 1.1288x over previous
import numpy as np
import ml_dtypes
import contextlib
from contextlib import ExitStack
import concourse.bass as bass
import concourse.mybir as mybir
from concourse.bass_utils import run_bass_kernel_spmd

F32 = mybir.dt.float32
BF16 = mybir.dt.bfloat16
AF = mybir.ActivationFunctionType
ALU = mybir.AluOpType
AX = mybir.AxisListType
BFNP = ml_dtypes.bfloat16
MAGIC = 12582912.0

B = 2
T = 2048
D = 1024
H = 16
HD = 64
NT = B * T
E_EXP = 8
NCORES = 8


class Ctr:
    def __init__(self, ctx, nc, name):
        self.h = ctx.enter_context(nc.semaphore(name))
        self.v = 0
    def take(self, n):
        base = self.v
        self.v += n
        return base


def mkop(eng, sem, cursor):
    def op(fn, *a, **k):
        eng.wait_ge(sem.h, cursor[0])
        inst = getattr(eng, fn)(*a, **k)
        inst.then_inc(sem.h, 1)
        cursor[0] += 1
        return inst
    return op



def build_l0attn():
    nc = bass.Bass("TRN2", target_bir_lowering=False, debug=False)
    x0 = nc.dram_tensor("x0", [NT, D], F32, kind="ExternalInput").ap()
    wg = nc.dram_tensor("wg", [D, E_EXP], F32, kind="ExternalInput").ap()
    x1 = nc.dram_tensor("x1", [NT, D], F32, kind="ExternalOutput").ap()
    meta = nc.dram_tensor("meta", [NT, 4], F32, kind="ExternalOutput").ap()

    ident = nc.inline_tensor(np.eye(128, dtype=BFNP), name="identbf")
    e2np = np.zeros((128, 2), dtype=BFNP)
    e2np[0:64, 0] = 1
    e2np[64:128, 1] = 1
    E2c = nc.inline_tensor(e2np, name="E2")
    iota8 = nc.inline_tensor(np.tile(np.arange(8, dtype=np.float32), (128, 1)),
                             name="iota8")
    epsc = nc.inline_tensor(np.full((128, 1), 1e-5, dtype=np.float32), name="epsc")

    with ExitStack() as ctx:
        En = ctx.enter_context
        XT = En(nc.sbuf_tensor("XT", [128, 8 * T], BF16))
        XBF = En(nc.sbuf_tensor("XBF", [128, 16 * 1024], BF16))
        BUFA = En(nc.sbuf_tensor("BUFA", [128, 16 * 1024], BF16))
        ATTN = En(nc.sbuf_tensor("ATTN", [128, 16 * 1024], BF16))
        GH = En(nc.sbuf_tensor("GH", [128, 1024], BF16))
        Z = En(nc.sbuf_tensor("Z", [128, 1024], BF16))
        NEGU = En(nc.sbuf_tensor("NEGU", [128, 256], F32))
        NEGUBF = En(nc.sbuf_tensor("NEGUBF", [128, 256], BF16))
        NEGUFM = En(nc.sbuf_tensor("NEGUFM", [16, T], F32))
        NEGC = En(nc.sbuf_tensor("NEGC", [16, 1], F32))
        SUFM = En(nc.sbuf_tensor("SUFM", [16, T], BF16))
        SUTM = En(nc.sbuf_tensor("SUTM", [128, 256], BF16))
        SUREP = En(nc.sbuf_tensor("SUREP", [128, 1024], BF16))
        ACCT = En(nc.sbuf_tensor("ACCT", [128, 128], F32))
        FS = En(nc.sbuf_tensor("FS", [128, 32], F32))
        FST = En(nc.sbuf_tensor("FST", [128, 32], F32))
        XRES = En(nc.sbuf_tensor("XRES", [128, 2 * 1024], F32))
        X1O = En(nc.sbuf_tensor("X1O", [128, 2 * 1024], F32))
        SQS = En(nc.sbuf_tensor("SQS", [128, 1024], BF16))
        LNS = En(nc.sbuf_tensor("LNS", [128, 8], F32))
        WGB = En(nc.sbuf_tensor("WGB", [128, 64], BF16))
        PRB = En(nc.sbuf_tensor("PRB", [128, 8], F32))
        GSM = En(nc.sbuf_tensor("GSM", [128, 24], F32))
        MT = En(nc.sbuf_tensor("MT", [128, 4], F32))
        IDS = En(nc.sbuf_tensor("IDS", [128, 128], BF16))
        E2S = En(nc.sbuf_tensor("E2S", [128, 2], BF16))
        EPS = En(nc.sbuf_tensor("EPS", [128, 1], F32))
        IO8 = En(nc.sbuf_tensor("IO8", [128, 8], F32))

        psQKa = En(nc.psum_tensor("psQKa", [128, 512], F32))
        psQKb = En(nc.psum_tensor("psQKb", [128, 512], F32))
        psAV0 = En(nc.psum_tensor("psAV0", [128, 1024], F32))
        psAV1 = En(nc.psum_tensor("psAV1", [128, 1024], F32))
        psT = En(nc.psum_tensor("psT", [128, 1024], BF16))
        psU = psT[:, 0:32].bitcast(F32)
        psF = psAV0[0:16, :].bitcast(BF16)
        psSU = psT[:, 0:256]
        psGate = psQKa[:, 0:8]

        s_ld = Ctr(ctx, nc, "s_ld")
        s_gd = Ctr(ctx, nc, "s_gd")
        s_pe = Ctr(ctx, nc, "s_pe")
        s_sc = Ctr(ctx, nc, "s_sc")
        s_dv = Ctr(ctx, nc, "s_dv")
        s_st = Ctr(ctx, nc, "s_st")
        s_x = [Ctr(ctx, nc, "s_x0"), Ctr(ctx, nc, "s_x1")]
        s_wg = Ctr(ctx, nc, "s_wg")
        s_y = [Ctr(ctx, nc, "s_y0"), Ctr(ctx, nc, "s_y1")]
        block = En(nc.Block())

        ld0 = s_ld.take(64)

        @block.sync
        def _(sync):
            sync.dma_start(out=IDS[:], in_=ident.ap()).then_inc(s_ld.h, 16)
            sync.dma_start(out=E2S[:], in_=E2c.ap()).then_inc(s_ld.h, 16)
            sync.dma_start(out=IO8[:], in_=iota8.ap()).then_inc(s_ld.h, 16)
            sync.dma_start(out=EPS[:], in_=epsc.ap()).then_inc(s_ld.h, 16)

        gd_wg = s_wg.take(16)

        @block.gpsimd
        def _(g):
            g.dma_start(
                out=WGB[:].rearrange("p (i e) -> p i e", i=8),
                in_=bass.AP(wg.tensor, 0, [[8, 128], [1024, 8], [1, 8]])
            ).then_inc(s_wg.h, 16)

        def emit_transpose_tm_to_fm(src_dram_off_rows, gd_wait=None):
            raise NotImplementedError

        for b in range(B):
            pe_before_p0 = s_pe.v
            gd_x = s_gd.take(16)

            @block.gpsimd
            def _(g, b=b, pe_before_p0=pe_before_p0):
                g.wait_ge(s_pe.h, pe_before_p0)
                src = bass.AP(x0.tensor, b * 16 * 128 * 1024,
                              [[1024, 128], [128 * 1024, 16], [1, 1024]])
                g.dma_start(out=XBF[:].rearrange("p (c d) -> p c d", c=16),
                            in_=src).then_inc(s_gd.h, 16)

            pe1 = s_pe.take(16)
            dv1 = s_dv.take(16)

            @block.tensor
            def _(t, gd_x=gd_x, pe1=pe1, dv1=dv1):
                t.wait_ge(s_gd.h, gd_x + 16)
                t.wait_ge(s_ld.h, ld0 + 64)
                for ci in range(16):
                    if ci > 0:
                        t.wait_ge(s_dv.h, dv1 + ci)
                    for ch in range(8):
                        last = t.transpose(
                            psT[:, 128 * ch:128 * (ch + 1)],
                            XBF[:, ci * 1024 + 128 * ch:ci * 1024 + 128 * (ch + 1)],
                            IDS[:])
                    last.then_inc(s_pe.h, 1)

            @block.vector
            def _(v, pe1=pe1, dv1=dv1):
                cur = [dv1]
                op = mkop(v, s_dv, cur)
                for ci in range(16):
                    v.wait_ge(s_pe.h, pe1 + ci + 1)
                    out_ap = bass.AP(XT[:].tensor, XT[:].offset + 128 * ci,
                                     [[8 * T, 128], [T, 8], [1, 128]])
                    op("tensor_copy", out_ap,
                       psT[:].rearrange("p (c f) -> p c f", c=8))

            dv2 = s_dv.take(24)

            @block.vector
            def _(v, dv2=dv2):
                cur = [dv2]
                op = mkop(v, s_dv, cur)
                for ch in range(8):
                    sl = XT[:, ch * T:(ch + 1) * T]
                    op("reduce_sum", LNS[:, 0:1], sl, axis=AX.X, negate=True)
                    op("tensor_scalar", LNS[:, 1:2], LNS[:, 0:1], 1.0 / T, None,
                       ALU.mult)
                    op("tensor_scalar", sl, sl, LNS[:, 1:2], None, ALU.add)

            sc3 = s_sc.take(8)

            @block.scalar
            def _(s, dv2=dv2):
                s.wait_ge(s_dv.h, dv2 + 24)
                for ch in range(8):
                    s.activation(BUFA[:, ch * T:(ch + 1) * T],
                                 XT[:, ch * T:(ch + 1) * T], AF.Square
                                 ).then_inc(s_sc.h, 1)

            pe4 = s_pe.take(16)
            dv4 = s_dv.take(16)

            @block.tensor
            def _(t, sc3=sc3, pe4=pe4, dv4=dv4):
                t.wait_ge(s_sc.h, sc3 + 8)
                for tb in range(16):
                    if tb > 0:
                        t.wait_ge(s_dv.h, dv4 + tb)
                    for ch in range(8):
                        last = t.matmul(
                            psU[:, 2 * ch:2 * ch + 2],
                            BUFA[:, ch * T + 128 * tb:ch * T + 128 * (tb + 1)],
                            E2S[:], start=(ch == 0), stop=(ch == 7))
                    last.then_inc(s_pe.h, 1)

            @block.vector
            def _(v, pe4=pe4, dv4=dv4):
                cur = [dv4]
                op = mkop(v, s_dv, cur)
                for tb in range(16):
                    v.wait_ge(s_pe.h, pe4 + tb + 1)
                    op("tensor_scalar", NEGU[:, 16 * tb:16 * (tb + 1)], psU,
                       -0.5, None, ALU.mult)

            dv5a = s_dv.take(1)

            @block.vector
            def _(v, dv5a=dv5a):
                cur = [dv5a]
                op = mkop(v, s_dv, cur)
                op("tensor_copy", NEGUBF[:], NEGU[:])

            pe5 = s_pe.take(1)

            @block.tensor
            def _(t, dv5a=dv5a):
                t.wait_ge(s_dv.h, dv5a + 1)
                for tb in range(16):
                    last = t.transpose(psF[:, 128 * tb:128 * (tb + 1)],
                                       NEGUBF[:, 16 * tb:16 * (tb + 1)], IDS[:])
                last.then_inc(s_pe.h, 1)

            dv5b = s_dv.take(2)

            @block.vector
            def _(v, pe5=pe5, dv5b=dv5b):
                cur = [dv5b]
                op = mkop(v, s_dv, cur)
                v.wait_ge(s_pe.h, pe5 + 1)
                op("tensor_copy", NEGUFM[:], psF)
                op("tensor_reduce", NEGC[:], NEGUFM[:], axis=AX.X, op=ALU.min)

            sc5 = s_sc.take(1)

            @block.scalar
            def _(s, dv5b=dv5b):
                s.wait_ge(s_dv.h, dv5b + 2)
                s.activation(SUFM[:], NEGUFM[:], AF.Exp, scale=-1.0,
                             bias=NEGC[:, 0:1]).then_inc(s_sc.h, 1)

            pe5b = s_pe.take(1)

            @block.tensor
            def _(t, sc5=sc5):
                t.wait_ge(s_sc.h, sc5 + 1)
                for tb in range(16):
                    last = t.transpose(psSU[:, 16 * tb:16 * (tb + 1)],
                                       SUFM[:, 128 * tb:128 * (tb + 1)],
                                       IDS[0:16, 0:16])
                last.then_inc(s_pe.h, 1)

            dv5c = s_dv.take(1)

            @block.vector
            def _(v, pe5b=pe5b, dv5c=dv5c):
                cur = [dv5c]
                op = mkop(v, s_dv, cur)
                v.wait_ge(s_pe.h, pe5b + 1)
                op("tensor_copy", SUTM[:], psSU)

            pe_before_p6 = s_pe.v
            dv6 = s_dv.take(32)

            @block.vector
            def _(v, pe_before_p6=pe_before_p6, dv6=dv6):
                cur = [dv6]
                op = mkop(v, s_dv, cur)
                v.wait_ge(s_pe.h, pe_before_p6)
                for ci in range(16):
                    su_b = bass.AP(SUTM[:].tensor, SUTM[:].offset + 16 * ci,
                                   [[256, 128], [1, 16], [0, 64]])
                    op("tensor_copy", SUREP[:].rearrange("p (h j) -> p h j", h=16),
                       su_b)
                    op("tensor_tensor", BUFA[:, ci * 1024:(ci + 1) * 1024],
                       XBF[:, ci * 1024:(ci + 1) * 1024], SUREP[:], ALU.mult)

            dv_prev_hp = s_dv.v
            for hp in range(8):
                pe_qk = s_pe.take(3 * 64)
                sc_ex = s_sc.take(2 * 64)

                @block.tensor
                def _(t, hp=hp, pe_qk=pe_qk, sc_ex=sc_ex, dv_prev_hp=dv_prev_hp):
                    t.wait_ge(s_dv.h, dv_prev_hp)
                    for q in range(4):
                        ps_av = psAV0 if q < 2 else psAV1
                        avcol = 512 * (q % 2)
                        for r in range(16):
                            i = 16 * q + r
                            stat = XT[:, hp * T + 128 * r: hp * T + 128 * (r + 1)]
                            mov = XT[:, hp * T + 512 * q: hp * T + 512 * (q + 1)]
                            if i > 0:
                                t.wait_ge(s_sc.h, sc_ex + 2 * i)
                            t.matmul(psQKa[:], stat[0:64, :], mov[0:64, :],
                                     start=True, stop=True,
                                     tile_position=(0, 0)).then_inc(s_pe.h, 1)
                            t.matmul(psQKb[:], stat[64:128, :], mov[64:128, :],
                                     start=True, stop=True,
                                     tile_position=(64, 0)).then_inc(s_pe.h, 1)
                            t.wait_ge(s_sc.h, sc_ex + 2 * i + 2)
                            wx0 = BUFA[:, r * 1024 + 128 * hp:
                                       r * 1024 + 128 * hp + 64]
                            wx1 = BUFA[:, r * 1024 + 128 * hp + 64:
                                       r * 1024 + 128 * hp + 128]
                            t.matmul(ps_av[0:64, avcol:avcol + 512], wx0,
                                     GH[:, 0:512], start=(r == 0), stop=(r == 15),
                                     tile_position=(0, 0), skip_group_check=True)
                            t.matmul(ps_av[64:128, avcol:avcol + 512], wx1,
                                     GH[:, 512:1024], start=(r == 0), stop=(r == 15),
                                     tile_position=(0, 64), skip_group_check=True
                                     ).then_inc(s_pe.h, 1)

                @block.scalar
                def _(s, hp=hp, pe_qk=pe_qk, sc_ex=sc_ex, dv_prev_hp=dv_prev_hp):
                    s.wait_ge(s_dv.h, dv_prev_hp)
                    for q in range(4):
                        for r in range(16):
                            i = 16 * q + r
                            s.wait_ge(s_pe.h, pe_qk + 3 * i + 2)
                            s.activation(
                                GH[:, 0:512], psQKa[:], AF.Exp,
                                bias=NEGU[:, 16 * r + 2 * hp:16 * r + 2 * hp + 1],
                                accum_out=ACCT[:, r * 8 + q:r * 8 + q + 1]
                            ).then_inc(s_sc.h, 1)
                            s.activation(
                                GH[:, 512:1024], psQKb[:], AF.Exp,
                                bias=NEGU[:, 16 * r + 2 * hp + 1:16 * r + 2 * hp + 2],
                                accum_out=ACCT[:, r * 8 + 4 + q:r * 8 + 4 + q + 1]
                            ).then_inc(s_sc.h, 1)

                dv_fs = s_dv.take(10)

                @block.vector
                def _(v, hp=hp, sc_ex=sc_ex, dv_fs=dv_fs):
                    cur = [dv_fs]
                    op = mkop(v, s_dv, cur)
                    v.wait_ge(s_sc.h, sc_ex + 128)
                    for hl in range(2):
                        off = 4 * hl
                        def acc_ap(k):
                            return bass.AP(ACCT[:].tensor,
                                           ACCT[:].offset + off + k,
                                           [[128, 128], [8, 16]])
                        sl = slice(16 * hl, 16 * hl + 16)
                        op("tensor_tensor", FST[:, sl], acc_ap(0), acc_ap(1), ALU.add)
                        op("tensor_tensor", FS[:, sl], acc_ap(2), acc_ap(3), ALU.add)
                        op("tensor_tensor", FST[:, sl], FST[:, sl], FS[:, sl], ALU.add)
                        su_ap = bass.AP(SUTM[:].tensor,
                                        SUTM[:].offset + 2 * hp + hl,
                                        [[256, 128], [16, 16]])
                        op("tensor_tensor", FST[:, sl], FST[:, sl], su_ap, ALU.mult)
                        op("reciprocal", FS[:, sl], FST[:, sl])

                pe_after_qk = s_pe.v
                for half in range(2):
                    dv_z = s_dv.take(1)

                    @block.vector
                    def _(v, half=half, pe_after_qk=pe_after_qk, dv_z=dv_z):
                        cur = [dv_z]
                        op = mkop(v, s_dv, cur)
                        v.wait_ge(s_pe.h, pe_after_qk + half)
                        ps = psAV0 if half == 0 else psAV1
                        op("tensor_copy", Z[:], ps[:])

                    pe_z = s_pe.take(1)

                    @block.tensor
                    def _(t, dv_z=dv_z):
                        t.wait_ge(s_dv.h, dv_z + 1)
                        for j in range(8):
                            last = t.transpose(psT[:, 128 * j:128 * (j + 1)],
                                               Z[:, 128 * j:128 * (j + 1)], IDS[:])
                        last.then_inc(s_pe.h, 1)

                    dv_w = s_dv.take(16)

                    @block.vector
                    def _(v, hp=hp, half=half, pe_z=pe_z, dv_w=dv_w):
                        cur = [dv_w]
                        op = mkop(v, s_dv, cur)
                        v.wait_ge(s_pe.h, pe_z + 1)
                        for j in range(8):
                            ci = half * 8 + j
                            for hl in range(2):
                                op("tensor_scalar",
                                   ATTN[:, ci * 1024 + 128 * hp + 64 * hl:
                                        ci * 1024 + 128 * hp + 64 * (hl + 1)],
                                   psT[:, 128 * j + 64 * hl:
                                       128 * j + 64 * (hl + 1)],
                                   FS[:, 16 * hl + ci:16 * hl + ci + 1], None,
                                   ALU.mult)
                dv_prev_hp = s_dv.v

            x8 = [s_x[0].take(16 * 8), s_x[1].take(16 * 8)]
            y8 = [s_y[0].take(16 * 8), s_y[1].take(16 * 8)]
            st8 = s_st.take(0)
            dv8 = s_dv.take(16 * 7)
            sc8 = s_sc.take(16 * 2)

            @block.sync
            def _(sync, b=b, dv8=dv8, x8=x8):
                for ci in range(16):
                    sync.wait_ge(s_dv.h, dv8 + 7 * (ci - 2) + 7 if ci >= 2 else dv8)
                    src = bass.AP(x0.tensor, (b * 16 + ci) * 128 * 1024,
                                  [[1024, 128], [1, 1024]])
                    sync.dma_start(out=XRES[:, 1024 * (ci % 2):1024 * (ci % 2 + 1)],
                                   in_=src).then_inc(s_x[ci % 2].h, 16)

            @block.vector
            def _(v, x8=x8, y8=y8, sc8=sc8, dv8=dv8):
                cur = [dv8]
                op = mkop(v, s_dv, cur)
                for ci in range(16):
                    xs = XRES[:, 1024 * (ci % 2):1024 * (ci % 2 + 1)]
                    v.wait_ge(s_x[ci % 2].h, x8[ci % 2] + 16 * (ci // 2 + 1))
                    op("tensor_tensor", xs, xs,
                       ATTN[:, ci * 1024:(ci + 1) * 1024], ALU.add)
                    op("reduce_sum", LNS[:, 0:1], xs, axis=AX.X, negate=True)
                    op("tensor_scalar", LNS[:, 1:2], LNS[:, 0:1], 1.0 / D, None,
                       ALU.mult)
                    op("tensor_tensor", LNS[:, 6:7], LNS[:, 1:2], LNS[:, 1:2],
                       ALU.mult)
                    v.wait_ge(s_sc.h, sc8 + 2 * ci + 1)
                    op("scalar_tensor_tensor", LNS[:, 2:3], LNS[:, 3:4], 1.0 / D,
                       LNS[:, 6:7], ALU.mult, ALU.subtract)
                    v.wait_ge(s_sc.h, sc8 + 2 * ci + 2)
                    op("reciprocal", LNS[:, 5:6], LNS[:, 4:5])
                    ob = X1O[:, 1024 * (ci % 2):1024 * (ci % 2 + 1)]
                    v.wait_ge(s_y[ci % 2].h,
                              y8[ci % 2] + 16 * (ci // 2) if ci >= 2 else y8[ci % 2])
                    op("tensor_scalar", ob, xs, LNS[:, 1:2], LNS[:, 5:6],
                       ALU.add, ALU.mult)

            @block.scalar
            def _(s, dv8=dv8, sc8=sc8):
                for ci in range(16):
                    xs = XRES[:, 1024 * (ci % 2):1024 * (ci % 2 + 1)]
                    s.wait_ge(s_dv.h, dv8 + 7 * ci + 1)
                    s.activation(SQS[:], xs, AF.Square,
                                 accum_out=LNS[:, 3:4]).then_inc(s_sc.h, 1)
                    s.wait_ge(s_dv.h, dv8 + 7 * ci + 5)
                    s.activation(LNS[:, 4:5], LNS[:, 2:3], AF.Sqrt,
                                 bias=EPS[:, 0:1]).then_inc(s_sc.h, 1)

            @block.gpsimd
            def _(g, b=b, dv8=dv8, y8=y8):
                for ci in range(16):
                    g.wait_ge(s_dv.h, dv8 + 7 * (ci + 1))
                    dst = bass.AP(x1.tensor, (b * 16 + ci) * 128 * 1024,
                                  [[1024, 128], [1, 1024]])
                    g.dma_start(
                        out=dst, in_=X1O[:, 1024 * (ci % 2):1024 * (ci % 2 + 1)]
                    ).then_inc(s_y[ci % 2].h, 16)

            pe_before_p9 = s_pe.v
            gd9 = s_gd.take(16)

            @block.gpsimd
            def _(g, y0t=s_y[0].v, y1t=s_y[1].v, pe_before_p9=pe_before_p9, b=b):
                g.wait_ge(s_y[0].h, y0t)
                g.wait_ge(s_y[1].h, y1t)
                g.wait_ge(s_pe.h, pe_before_p9)
                src = bass.AP(x1.tensor, b * 16 * 128 * 1024,
                              [[1024, 128], [128 * 1024, 16], [1, 1024]])
                g.dma_start(out=XBF[:].rearrange("p (c d) -> p c d", c=16),
                            in_=src).then_inc(s_gd.h, 16)

            dv_before_p9 = s_dv.v
            pe9 = s_pe.take(16)
            dv9 = s_dv.take(16)

            @block.tensor
            def _(t, gd9=gd9, pe9=pe9, dv9=dv9, dv_before_p9=dv_before_p9):
                t.wait_ge(s_gd.h, gd9 + 16)
                t.wait_ge(s_dv.h, dv_before_p9)
                for ci in range(16):
                    if ci > 0:
                        t.wait_ge(s_dv.h, dv9 + ci)
                    for ch in range(8):
                        last = t.transpose(
                            psT[:, 128 * ch:128 * (ch + 1)],
                            XBF[:, ci * 1024 + 128 * ch:ci * 1024 + 128 * (ch + 1)],
                            IDS[:])
                    last.then_inc(s_pe.h, 1)

            @block.vector
            def _(v, pe9=pe9, dv9=dv9):
                cur = [dv9]
                op = mkop(v, s_dv, cur)
                for ci in range(16):
                    v.wait_ge(s_pe.h, pe9 + ci + 1)
                    out_ap = bass.AP(XT[:].tensor, XT[:].offset + 128 * ci,
                                     [[8 * T, 128], [T, 8], [1, 128]])
                    op("tensor_copy", out_ap,
                       psT[:].rearrange("p (c f) -> p c f", c=8))

            pe9b = s_pe.take(16)
            sc9 = s_sc.take(16)
            dv9b = s_dv.take(16 * 9)
            st9 = s_st.take(16 * 16)

            @block.tensor
            def _(t, dv9=dv9, pe9b=pe9b, sc9=sc9):
                t.wait_ge(s_dv.h, dv9 + 16)
                t.wait_ge(s_wg.h, gd_wg + 16)
                for tb in range(16):
                    if tb > 0:
                        t.wait_ge(s_sc.h, sc9 + tb)
                    for ich in range(8):
                        last = t.matmul(
                            psGate[:],
                            XT[:, ich * T + 128 * tb:ich * T + 128 * (tb + 1)],
                            WGB[:, ich * 8:(ich + 1) * 8],
                            start=(ich == 0), stop=(ich == 7))
                    last.then_inc(s_pe.h, 1)

            @block.scalar
            def _(s, pe9b=pe9b, dv9b=dv9b, sc9=sc9):
                for tb in range(16):
                    s.wait_ge(s_pe.h, pe9b + tb + 1)
                    if tb > 0:
                        s.wait_ge(s_dv.h, dv9b + 9 * tb)
                    s.activation(PRB[:], psGate, AF.Exp,
                                 accum_out=GSM[:, 0:1]).then_inc(s_sc.h, 1)

            @block.vector
            def _(v, sc9=sc9, dv9b=dv9b, st9=st9):
                cur = [dv9b]
                op = mkop(v, s_dv, cur)
                for tb in range(16):
                    v.wait_ge(s_sc.h, sc9 + tb + 1)
                    v.wait_ge(s_st.h, st9 + 16 * tb)
                    op("reciprocal", GSM[:, 1:2], GSM[:, 0:1])
                    op("tensor_scalar", PRB[:], PRB[:], GSM[:, 1:2], None, ALU.mult)
                    op("reduce_max", MT[:, 2:3], PRB[:], axis=AX.X)
                    op("tensor_scalar", GSM[:, 2:10], PRB[:], MT[:, 2:3], None,
                       ALU.is_ge)
                    op("scalar_tensor_tensor", GSM[:, 16:24], GSM[:, 2:10], 1.0,
                       IO8[:], ALU.mult, ALU.mult, accum_out=MT[:, 0:1])
                    op("scalar_tensor_tensor", GSM[:, 2:10], GSM[:, 2:10], -1e30,
                       PRB[:], ALU.mult, ALU.add)
                    op("reduce_max", MT[:, 3:4], GSM[:, 2:10], axis=AX.X)
                    op("tensor_scalar", GSM[:, 2:10], GSM[:, 2:10], MT[:, 3:4],
                       None, ALU.is_ge)
                    op("scalar_tensor_tensor", GSM[:, 16:24], GSM[:, 2:10], 1.0,
                       IO8[:], ALU.mult, ALU.mult, accum_out=MT[:, 1:2])

            @block.sync
            def _(sync, b=b, dv9b=dv9b, st9=st9):
                for tb in range(16):
                    sync.wait_ge(s_dv.h, dv9b + 9 * (tb + 1))
                    dst = bass.AP(meta.tensor, (b * 16 + tb) * 128 * 4,
                                  [[4, 128], [1, 4]])
                    sync.dma_start(out=dst, in_=MT[:]).then_inc(s_st.h, 16)

        @block.gpsimd
        def _(g):
            g.wait_ge(s_st.h, s_st.v)
            g.wait_ge(s_y[0].h, s_y[0].v)
            g.wait_ge(s_y[1].h, s_y[1].v)

    return nc



def build_moe(sh=None):
    NTOK = 16 * T
    NCQ = NTOK // 128
    tag = "" if sh is None else "C_"
    nc = bass.Bass("TRN2", target_bir_lowering=False, debug=False) if sh is None else sh["nc"]
    x1o = nc.dram_tensor("x1o", [NT, D], F32, kind="ExternalInput").ap()
    metaF = nc.dram_tensor("metaF", [NTOK, 4], F32, kind="ExternalInput").ap()
    cfc = nc.dram_tensor("cfc", [E_EXP, D, 3072], F32, kind="ExternalInput").ap()
    cpj = nc.dram_tensor("cpj", [E_EXP, 1536, D], F32, kind="ExternalInput").ap()
    x2 = (nc.dram_tensor("x2", [NT, D], F32, kind="ExternalOutput").ap()
          if sh is None else sh["x2d"])
    gscr = nc.dram_tensor("gscr", [128, NCQ, 8], F32, kind="Internal").ap()

    ident = nc.inline_tensor(np.eye(128, dtype=BFNP), name=tag + "identbf")
    ltri = np.zeros((128, 128), dtype=BFNP)
    for k in range(128):
        ltri[k, k + 1:] = 1
    LTc = nc.inline_tensor(ltri, name=tag + "LT")
    iota8 = nc.inline_tensor(np.tile(np.arange(8, dtype=np.float32), (128, 1)),
                             name=tag + "iota8")
    epsc = nc.inline_tensor(np.full((128, 1), 1e-5, dtype=np.float32), name=tag + "epsc")
    onesr = nc.inline_tensor(np.ones((1, 128), dtype=np.float32), name=tag + "onesr")
    onesc = nc.inline_tensor(np.ones((128, 1), dtype=BFNP), name=tag + "onesc")

    with (ExitStack() if sh is None else contextlib.nullcontext()) as _cm:
        ctx = _cm if sh is None else sh["ctx"]
        En = ctx.enter_context
        if sh is None:
            XT = En(nc.sbuf_tensor("XT", [128, 8 * T], BF16))[:]
            XBF = En(nc.sbuf_tensor("XBF", [128, 16 * 1024], BF16))[:]
        else:
            XT = sh["B1"]
            XBF = sh["B2"]
        ACC = XBF
        if sh is None:
            CFA = En(nc.sbuf_tensor("CFA", [128, 8 * 1536], BF16))[:]
            CFG = En(nc.sbuf_tensor("CFG", [128, 8 * 1536], BF16))[:]
            CPJ = En(nc.sbuf_tensor("CPJ", [128, 12 * 1024], BF16))[:]
        else:
            CFA = sh["B3"][:, 0:12288]
            CFG = sh["B4"][:, 0:12288]
            CPJ = sh["B5"][:, 0:12288]
        HH1 = En(nc.sbuf_tensor(tag + "HH1", [128, 1536], BF16))
        HH = En(nc.sbuf_tensor(tag + "HH", [128, 1536], BF16))
        HHT = En(nc.sbuf_tensor(tag + "HHT", [128, 1536], BF16))
        MF = XT[:, 0:2048].bitcast(F32)
        OH0 = XT[:, 2048:4096]
        OH1 = XT[:, 4096:6144]
        OHP = XT[:, 6144:8192]
        GALL = XT[:, 8192:12288].bitcast(F32)
        GOWN = En(nc.sbuf_tensor(tag + "GOWN", [128, 256], F32))
        RS = En(nc.sbuf_tensor(tag + "RS", [128, 40], F32))
        BcSB = En(nc.sbuf_tensor(tag + "BcSB", [1, 8], F32))
        LTS = En(nc.sbuf_tensor(tag + "LTS", [128, 128], BF16))
        IO8 = En(nc.sbuf_tensor(tag + "IO8", [128, 8], F32))
        ON1 = En(nc.sbuf_tensor(tag + "ON1", [1, 128], F32))
        ONC = En(nc.sbuf_tensor(tag + "ONC", [128, 1], BF16))
        if sh is None:
            XRES = En(nc.sbuf_tensor("XRES", [128, 2 * 1024], F32))[:]
            X2O = En(nc.sbuf_tensor("X2O", [128, 2 * 1024], F32))[:]
            SQS = En(nc.sbuf_tensor("SQS", [128, 1024], BF16))[:]
            LNS = En(nc.sbuf_tensor("LNS", [128, 8], F32))[:]
            IDS = En(nc.sbuf_tensor("IDS", [128, 128], BF16))[:]
            EPS = En(nc.sbuf_tensor("EPS", [128, 1], F32))[:]
            psHt = [En(nc.psum_tensor(f"psH{i}", [128, 512], F32)) for i in range(3)]
            psH = [p[:] for p in psHt]
            psO = [En(nc.psum_tensor(f"psO{i}", [128, 512], F32))[:] for i in range(2)]
            psT = En(nc.psum_tensor("psT", [128, 1024], BF16))[:]
        else:
            XRES = sh["XRES"]
            X2O = sh["XOUT"]
            SQS = sh["SQS"][:, 0:1024]
            LNS = sh["LNS"][:, 0:8]
            IDS = sh["IDS"]
            EPS = sh["EPS"]
            PS = sh["PS"]
            psH = [PS[0][:, 0:512], PS[0][:, 512:1024], PS[1][:, 0:512]]
            psO = [PS[2][:, 0:512], PS[3][:, 0:512]]
            psT = PS[1][:, 512:1024].bitcast(BF16)
        psP = psH[0][:, 0:8]
        psB = psH[1][0:1, 0:8]

        if sh is None:
            s_ld = Ctr(ctx, nc, "s_ld")
            s_gd = Ctr(ctx, nc, "s_gd")
            s_pe = Ctr(ctx, nc, "s_pe")
            s_sc = Ctr(ctx, nc, "s_sc")
            s_dv = Ctr(ctx, nc, "s_dv")
            s_wa = Ctr(ctx, nc, "s_wa")
            s_wg2 = Ctr(ctx, nc, "s_wg2")
            s_wp = Ctr(ctx, nc, "s_wp")
            s_x = [Ctr(ctx, nc, "s_x0"), Ctr(ctx, nc, "s_x1")]
            s_y = [Ctr(ctx, nc, "s_y0"), Ctr(ctx, nc, "s_y1")]
            s_gs = Ctr(ctx, nc, "s_gs")
            s_go = Ctr(ctx, nc, "s_go")
            block = En(nc.Block())
        else:
            s_ld = sh["s_ld"]; s_gd = sh["s_gd"]; s_pe = sh["s_pe"]
            s_sc = sh["s_sc"]; s_dv = sh["s_dv"]; s_wa = sh["s_wa"]
            s_wg2 = sh["s_wg2"]; s_wp = sh["s_wp"]; s_x = sh["s_x"]
            s_y = sh["s_y"]; s_gs = sh["s_gs"]; s_go = sh["s_go"]
            block = sh["block"]

        ld0 = s_ld.take(16 * 7)

        @block.sync
        def _(sync):
            sync.dma_start(out=IDS[:], in_=ident.ap()).then_inc(s_ld.h, 16)
            sync.dma_start(out=LTS[:], in_=LTc.ap()).then_inc(s_ld.h, 16)
            sync.dma_start(out=IO8[:], in_=iota8.ap()).then_inc(s_ld.h, 16)
            sync.dma_start(out=EPS[:], in_=epsc.ap()).then_inc(s_ld.h, 16)
            sync.dma_start(out=ON1[:], in_=onesr.ap()).then_inc(s_ld.h, 16)
            sync.dma_start(out=ONC[:], in_=onesc.ap()).then_inc(s_ld.h, 16)
            sync.dma_start(
                out=MF.rearrange("p (c f) -> p c f", c=NCQ),
                in_=bass.AP(metaF.tensor, 0, [[4, 128], [512, NCQ], [1, 4]])
            ).then_inc(s_ld.h, 16)

        dvR1 = s_dv.take(NCQ * 7)

        @block.vector
        def _(v, dvR1=dvR1):
            cur = [dvR1]
            op = mkop(v, s_dv, cur)
            v.wait_ge(s_ld.h, ld0 + 16 * 7)
            for cq in range(NCQ):
                e0c = MF[:, 4 * cq:4 * cq + 1]
                e1c = MF[:, 4 * cq + 1:4 * cq + 2]
                op("tensor_scalar", RS[:, 0:8], IO8[:], e0c, None, ALU.subtract)
                op("tensor_tensor", RS[:, 8:16], RS[:, 0:8], RS[:, 0:8], ALU.mult)
                op("tensor_scalar", OH0[:, 8 * cq:8 * cq + 8], RS[:, 8:16], 0.25,
                   None, ALU.is_le)
                op("tensor_scalar", RS[:, 0:8], IO8[:], e1c, None, ALU.subtract)
                op("tensor_tensor", RS[:, 8:16], RS[:, 0:8], RS[:, 0:8], ALU.mult)
                op("tensor_scalar", OH1[:, 8 * cq:8 * cq + 8], RS[:, 8:16], 0.25,
                   None, ALU.is_le)
                op("tensor_tensor", OHP[:, 8 * cq:8 * cq + 8],
                   OH0[:, 8 * cq:8 * cq + 8], OH1[:, 8 * cq:8 * cq + 8], ALU.add)

        peR = s_pe.take(NCQ)
        dvR2 = s_dv.take(NCQ * 10)

        @block.tensor
        def _(t, peR=peR, dvR2=dvR2, dvR1=dvR1):
            t.wait_ge(s_dv.h, dvR1 + NCQ * 7)
            for cq in range(NCQ):
                t.wait_ge(s_dv.h, dvR2 + 10 * cq + 1)
                ohp = OHP[:, 8 * cq:8 * cq + 8]
                t.matmul(psP, LTS[:], ohp, start=True, stop=False)
                t.matmul(psP, ON1[:], BcSB[:], start=False, stop=True)
                t.matmul(psB, ONC[:], ohp, start=(cq == 0), stop=(cq == NCQ - 1),
                         skip_group_check=True).then_inc(s_pe.h, 1)

        @block.vector
        def _(v, peR=peR, dvR2=dvR2):
            cur = [dvR2]
            op = mkop(v, s_dv, cur)
            for cq in range(NCQ):
                if cq > 0:
                    v.wait_ge(s_pe.h, peR + cq)
                    op("tensor_copy", BcSB[:], psB)
                else:
                    op("memset", BcSB[:], 0.0)
                v.wait_ge(s_pe.h, peR + cq + 1)
                op("scalar_tensor_tensor", RS[:, 16:24], psP, 1.0,
                   OH0[:, 8 * cq:8 * cq + 8], ALU.mult, ALU.mult,
                   accum_out=RS[:, 32:33])
                op("scalar_tensor_tensor", RS[:, 24:32], psP, 1.0,
                   OH1[:, 8 * cq:8 * cq + 8], ALU.mult, ALU.mult,
                   accum_out=RS[:, 33:34])
                op("tensor_scalar", RS[:, 34:35], RS[:, 32:33], 4095.5, None,
                   ALU.is_le)
                op("tensor_scalar", RS[:, 35:36], RS[:, 33:34], 4095.5, None,
                   ALU.is_le)
                op("tensor_tensor", RS[:, 36:37], MF[:, 4 * cq + 2:4 * cq + 3],
                   RS[:, 34:35], ALU.mult)
                op("tensor_tensor", RS[:, 37:38], MF[:, 4 * cq + 3:4 * cq + 4],
                   RS[:, 35:36], ALU.mult)
                op("tensor_scalar", RS[:, 16:24], OH0[:, 8 * cq:8 * cq + 8],
                   RS[:, 36:37], None, ALU.mult)
                op("tensor_scalar", RS[:, 24:32], OH1[:, 8 * cq:8 * cq + 8],
                   RS[:, 37:38], None, ALU.mult)
                op("tensor_tensor", GALL[:, 8 * cq:8 * cq + 8], RS[:, 16:24],
                   RS[:, 24:32], ALU.add)

        gdR = s_gd.take(16)
        dv_after_R = s_dv.v

        @block.gpsimd
        def _(g, dv_after_R=dv_after_R, gdR=gdR):
            g.wait_ge(s_dv.h, dv_after_R)
            g.dma_start(
                out=bass.AP(gscr.tensor, 0, [[NCQ * 8, 128], [8, NCQ], [1, 8]]),
                in_=GALL.rearrange("p (c f) -> p c f", c=NCQ)
            ).then_inc(s_gd.h, 16)
            g.wait_ge(s_gd.h, gdR + 16)
            pid = g.partition_id()
            src = bass.AP(gscr.tensor, pid * (32 * 8),
                          [[NCQ * 8, 128], [8, 32], [1, 8]])
            g.dma_start(out=GOWN[:].rearrange("p (c f) -> p c f", c=32),
                        in_=src).then_inc(s_go.h, 16)

        pe_a_done = {}
        pe_g_done = {}
        pe_p_done = {}

        for b in range(B):
            pe_before_x = s_pe.v
            dv_before_x = s_dv.v
            gd_x = s_gd.take(16)

            @block.gpsimd
            def _(g, b=b, pe_before_x=pe_before_x, dv_before_x=dv_before_x):
                g.wait_ge(s_pe.h, pe_before_x)
                g.wait_ge(s_dv.h, dv_before_x)
                src = bass.AP(x1o.tensor, b * 16 * 128 * 1024,
                              [[1024, 128], [128 * 1024, 16], [1, 1024]])
                g.dma_start(out=XBF[:].rearrange("p (c d) -> p c d", c=16),
                            in_=src).then_inc(s_gd.h, 16)

            pe1 = s_pe.take(16)
            dv1 = s_dv.take(17)

            @block.tensor
            def _(t, gd_x=gd_x, pe1=pe1, dv1=dv1, gdR=gdR):
                t.wait_ge(s_gd.h, gd_x + 16)
                t.wait_ge(s_go.h, 16)
                for ci in range(16):
                    if ci > 0:
                        t.wait_ge(s_dv.h, dv1 + ci)
                    for ch in range(8):
                        last = t.transpose(
                            psT[:, 128 * ch:128 * (ch + 1)],
                            XBF[:, ci * 1024 + 128 * ch:ci * 1024 + 128 * (ch + 1)],
                            IDS[:])
                    last.then_inc(s_pe.h, 1)

            @block.vector
            def _(v, pe1=pe1, dv1=dv1):
                cur = [dv1]
                op = mkop(v, s_dv, cur)
                for ci in range(16):
                    v.wait_ge(s_pe.h, pe1 + ci + 1)
                    out_ap = bass.AP(XT[:].tensor, XT[:].offset + 128 * ci,
                                     [[8 * T, 128], [T, 8], [1, 128]])
                    op("tensor_copy", out_ap,
                       psT[:].rearrange("p (c f) -> p c f", c=8))
                op("memset", ACC[:], 0.0)

            for e in range(E_EXP):
                wa = s_wa.take(16 * 8)
                wg2 = s_wg2.take(16 * 8)
                wp = s_wp.take(16 * 12)

                @block.gpsimd
                def _(g, b=b, e=e, wa=wa, wg2=wg2, wp=wp):
                    if (b, e) != (0, 0):
                        pb, pe_ = (b, e - 1) if e > 0 else (b - 1, 7)
                        g.wait_ge(s_pe.h, pe_a_done[(pb, pe_)])
                    for ich in range(8):
                        src = bass.AP(cfc.tensor,
                                      e * D * 3072 + 128 * ich * 3072,
                                      [[3072, 128], [1, 1536]])
                        g.dma_start(out=CFA[:, 1536 * ich:1536 * (ich + 1)],
                                    in_=src).then_inc(s_wa.h, 16)
                    if (b, e) != (0, 0):
                        pb, pe_ = (b, e - 1) if e > 0 else (b - 1, 7)
                        g.wait_ge(s_pe.h, pe_g_done[(pb, pe_)])
                    for ich in range(8):
                        src = bass.AP(cfc.tensor,
                                      e * D * 3072 + 128 * ich * 3072 + 1536,
                                      [[3072, 128], [1, 1536]])
                        g.dma_start(out=CFG[:, 1536 * ich:1536 * (ich + 1)],
                                    in_=src).then_inc(s_wg2.h, 16)
                    if (b, e) != (0, 0):
                        pb, pe_ = (b, e - 1) if e > 0 else (b - 1, 7)
                        g.wait_ge(s_pe.h, pe_p_done[(pb, pe_)])
                    for fc in range(12):
                        src = bass.AP(cpj.tensor,
                                      e * 1536 * D + 128 * fc * D,
                                      [[1024, 128], [1, 1024]])
                        g.dma_start(out=CPJ[:, 1024 * fc:1024 * (fc + 1)],
                                    in_=src).then_inc(s_wp.h, 16)

                pe_e = s_pe.take(16 * 5)
                sc_e = s_sc.take(16 * 3)
                dv_e = s_dv.take(16 * 7)
                dv_xt_done = dv1 + 16

                @block.tensor
                def _(t, b=b, e=e, wa=wa, wg2=wg2, wp=wp, pe_e=pe_e,
                      sc_e=sc_e, dv_e=dv_e, dv_xt_done=dv_xt_done):
                    t.wait_ge(s_dv.h, dv_xt_done + 1)
                    t.wait_ge(s_wa.h, wa + 128)
                    for tb in range(16):
                        base5 = pe_e + 5 * tb
                        based = dv_e + 7 * tb
                        if tb > 0 or (b, e) != (0, 0):
                            t.wait_ge(s_dv.h, based - 7 + 3 if tb > 0 else dv_e)
                        for ich in range(8):
                            stat = XT[:, ich * T + 128 * tb:ich * T + 128 * (tb + 1)]
                            for fb in range(3):
                                last = t.matmul(
                                    psH[fb][:], stat,
                                    CFA[:, 1536 * ich + 512 * fb:
                                        1536 * ich + 512 * (fb + 1)],
                                    start=(ich == 0), stop=(ich == 7))
                        last.then_inc(s_pe.h, 1)
                        t.wait_ge(s_sc.h, sc_e + 3 * tb + 3)
                        if tb == 0:
                            t.wait_ge(s_wg2.h, wg2 + 128)
                        for ich in range(8):
                            stat = XT[:, ich * T + 128 * tb:ich * T + 128 * (tb + 1)]
                            for fb in range(3):
                                last = t.matmul(
                                    psH[fb][:], stat,
                                    CFG[:, 1536 * ich + 512 * fb:
                                        1536 * ich + 512 * (fb + 1)],
                                    start=(ich == 0), stop=(ich == 7))
                        last.then_inc(s_pe.h, 1)
                        t.wait_ge(s_dv.h, based + 3)
                        for j in range(8):
                            last = t.transpose(psT[:, 128 * j:128 * (j + 1)],
                                               HH[:, 128 * j:128 * (j + 1)], IDS[:])
                        last.then_inc(s_pe.h, 1)
                        t.wait_ge(s_dv.h, based + 4)
                        for j in range(4):
                            last = t.transpose(psT[:, 128 * j:128 * (j + 1)],
                                               HH[:, 1024 + 128 * j:1024 + 128 * (j + 1)],
                                               IDS[:])
                        last.then_inc(s_pe.h, 1)
                        t.wait_ge(s_dv.h, based + 5)
                        if tb == 0:
                            t.wait_ge(s_wp.h, wp + 192)
                        for fc in range(12):
                            stat = HHT[:, 128 * fc:128 * (fc + 1)]
                            t.matmul(psO[0][:], stat,
                                     CPJ[:, 1024 * fc:1024 * fc + 512],
                                     start=(fc == 0), stop=(fc == 11))
                            last = t.matmul(psO[1][:], stat,
                                            CPJ[:, 1024 * fc + 512:1024 * (fc + 1)],
                                            start=(fc == 0), stop=(fc == 11))
                        last.then_inc(s_pe.h, 1)
                    pe_a_done[(b, e)] = pe_e + 5 * 15 + 1
                    pe_g_done[(b, e)] = pe_e + 5 * 15 + 2
                    pe_p_done[(b, e)] = pe_e + 5 * 16

                @block.scalar
                def _(s, b=b, e=e, pe_e=pe_e, sc_e=sc_e):
                    for tb in range(16):
                        s.wait_ge(s_pe.h, pe_e + 5 * tb + 1)
                        for fb in range(3):
                            s.activation(HH1[:, 512 * fb:512 * (fb + 1)],
                                         psH[fb][:], AF.Silu).then_inc(s_sc.h, 1)

                @block.vector
                def _(v, b=b, e=e, pe_e=pe_e, dv_e=dv_e, sc_e=sc_e):
                    cur = [dv_e]
                    op = mkop(v, s_dv, cur)
                    for tb in range(16):
                        v.wait_ge(s_pe.h, pe_e + 5 * tb + 2)
                        for fb in range(3):
                            op("tensor_tensor", HH[:, 512 * fb:512 * (fb + 1)],
                               HH1[:, 512 * fb:512 * (fb + 1)], psH[fb][:],
                               ALU.mult)
                        v.wait_ge(s_pe.h, pe_e + 5 * tb + 3)
                        op("tensor_copy", HHT[:, 0:1024], psT[:])
                        v.wait_ge(s_pe.h, pe_e + 5 * tb + 4)
                        op("tensor_copy", HHT[:, 1024:1536], psT[:, 0:512])
                        v.wait_ge(s_pe.h, pe_e + 5 * tb + 5)
                        gcol = GOWN[:, (b * 16 + tb) * 8 + e:
                                    (b * 16 + tb) * 8 + e + 1]
                        op("scalar_tensor_tensor",
                           ACC[:, 1024 * tb:1024 * tb + 512], psO[0][:], gcol,
                           ACC[:, 1024 * tb:1024 * tb + 512], ALU.mult, ALU.add)
                        op("scalar_tensor_tensor",
                           ACC[:, 1024 * tb + 512:1024 * (tb + 1)], psO[1][:],
                           gcol, ACC[:, 1024 * tb + 512:1024 * (tb + 1)],
                           ALU.mult, ALU.add)

            x8 = [s_x[0].take(16 * 8), s_x[1].take(16 * 8)]
            y8 = [s_y[0].take(16 * 8), s_y[1].take(16 * 8)]
            dv8 = s_dv.take(16 * 7)
            sc8 = s_sc.take(16 * 2)

            @block.sync
            def _(sync, b=b, dv8=dv8):
                for ci in range(16):
                    sync.wait_ge(s_dv.h, dv8 + 7 * (ci - 2) + 7 if ci >= 2 else dv8)
                    src = bass.AP(x1o.tensor, (b * 16 + ci) * 128 * 1024,
                                  [[1024, 128], [1, 1024]])
                    sync.dma_start(out=XRES[:, 1024 * (ci % 2):1024 * (ci % 2 + 1)],
                                   in_=src).then_inc(s_x[ci % 2].h, 16)

            @block.vector
            def _(v, b=b, x8=x8, y8=y8, sc8=sc8, dv8=dv8):
                cur = [dv8]
                op = mkop(v, s_dv, cur)
                for ci in range(16):
                    xs = XRES[:, 1024 * (ci % 2):1024 * (ci % 2 + 1)]
                    v.wait_ge(s_x[ci % 2].h, x8[ci % 2] + 16 * (ci // 2 + 1))
                    op("tensor_tensor", xs, xs,
                       ACC[:, ci * 1024:(ci + 1) * 1024], ALU.add)
                    op("reduce_sum", LNS[:, 0:1], xs, axis=AX.X, negate=True)
                    op("tensor_scalar", LNS[:, 1:2], LNS[:, 0:1], 1.0 / D, None,
                       ALU.mult)
                    op("tensor_tensor", LNS[:, 6:7], LNS[:, 1:2], LNS[:, 1:2],
                       ALU.mult)
                    v.wait_ge(s_sc.h, sc8 + 2 * ci + 1)
                    op("scalar_tensor_tensor", LNS[:, 2:3], LNS[:, 3:4], 1.0 / D,
                       LNS[:, 6:7], ALU.mult, ALU.subtract)
                    v.wait_ge(s_sc.h, sc8 + 2 * ci + 2)
                    op("reciprocal", LNS[:, 5:6], LNS[:, 4:5])
                    ob = X2O[:, 1024 * (ci % 2):1024 * (ci % 2 + 1)]
                    v.wait_ge(s_y[ci % 2].h,
                              y8[ci % 2] + 16 * (ci // 2) if ci >= 2 else y8[ci % 2])
                    op("tensor_scalar", ob, xs, LNS[:, 1:2], LNS[:, 5:6],
                       ALU.add, ALU.mult)

            @block.scalar
            def _(s, b=b, dv8=dv8, sc8=sc8):
                for ci in range(16):
                    xs = XRES[:, 1024 * (ci % 2):1024 * (ci % 2 + 1)]
                    s.wait_ge(s_dv.h, dv8 + 7 * ci + 1)
                    s.activation(SQS[:], xs, AF.Square,
                                 accum_out=LNS[:, 3:4]).then_inc(s_sc.h, 1)
                    s.wait_ge(s_dv.h, dv8 + 7 * ci + 5)
                    s.activation(LNS[:, 4:5], LNS[:, 2:3], AF.Sqrt,
                                 bias=EPS[:, 0:1]).then_inc(s_sc.h, 1)

            @block.gpsimd
            def _(g, b=b, dv8=dv8, y8=y8):
                for ci in range(16):
                    g.wait_ge(s_dv.h, dv8 + 7 * (ci + 1))
                    dst = bass.AP(x2.tensor, (b * 16 + ci) * 128 * 1024,
                                  [[1024, 128], [1, 1024]])
                    g.dma_start(
                        out=dst, in_=X2O[:, 1024 * (ci % 2):1024 * (ci % 2 + 1)]
                    ).then_inc(s_y[ci % 2].h, 16)

        @block.gpsimd
        def _(g):
            g.wait_ge(s_y[0].h, s_y[0].v)
            g.wait_ge(s_y[1].h, s_y[1].v)

    return nc



def _rope_tables():
    j = np.arange(64) % 32
    inv = (10000.0 ** (-2.0 * j / 64.0))[:, None]
    tpos = np.arange(T, dtype=np.float64)[None, :]
    ang = inv * tpos
    cos = np.cos(ang)
    sin = np.sin(ang)
    cos2 = np.tile(cos, (2, 1)).astype(BFNP)
    sin2 = np.tile(sin, (2, 1)).astype(BFNP)
    return cos2, sin2


def build_l1attn(sh=None):
    tag = "" if sh is None else "D_"
    nc = bass.Bass("TRN2", target_bir_lowering=False, debug=False) if sh is None else sh["nc"]
    x2i = (nc.dram_tensor("x2i", [NT, D], F32, kind="ExternalInput").ap()
           if sh is None else sh["x2d"])
    wqT = nc.dram_tensor("wqT", [D, D], F32, kind="ExternalInput").ap()
    wkT = nc.dram_tensor("wkT", [D, D], F32, kind="ExternalInput").ap()
    wvT = nc.dram_tensor("wvT", [D, D], F32, kind="ExternalInput").ap()
    woT = nc.dram_tensor("woT", [D, D], F32, kind="ExternalInput").ap()
    x3 = (nc.dram_tensor("x3", [NT, D], F32, kind="ExternalOutput").ap()
          if sh is None else sh["x3d"])

    ident = nc.inline_tensor(np.eye(128, dtype=BFNP), name=tag + "identbf")
    epsc = nc.inline_tensor(np.full((128, 1), 1e-5, dtype=np.float32), name=tag + "epsc")
    cos2, sin2 = _rope_tables()
    cosc = nc.inline_tensor(cos2, name=tag + "cosc")
    sinc = nc.inline_tensor(sin2, name=tag + "sinc")
    utri = np.triu(np.ones((128, 128))).astype(BFNP)
    utric = nc.inline_tensor(utri, name=tag + "utri")
    onesc = nc.inline_tensor(np.ones((128, 1), dtype=BFNP), name=tag + "onesc")

    with (ExitStack() if sh is None else contextlib.nullcontext()) as _cm:
        ctx = _cm if sh is None else sh["ctx"]
        En = ctx.enter_context
        if sh is None:
            XT = En(nc.sbuf_tensor("XT", [128, 8 * T], BF16))[:]
            XBF = En(nc.sbuf_tensor("XBF", [128, 16 * 1024], BF16))[:]
            UST = En(nc.sbuf_tensor("UST", [128, 16 * 1024], BF16))[:]
            WOS = En(nc.sbuf_tensor("WOS", [128, 8 * 1024], BF16))[:]
            COS = En(nc.sbuf_tensor("COS", [128, T], BF16))[:]
            SIN = En(nc.sbuf_tensor("SIN", [128, T], BF16))[:]
            QR = En(nc.sbuf_tensor("QR", [128, T], BF16))[:]
            KR = En(nc.sbuf_tensor("KR", [128, T], BF16))[:]
            VTF = En(nc.sbuf_tensor("VTF", [128, T], BF16))[:]
            VTM = En(nc.sbuf_tensor("VTM", [128, T], BF16))[:]
            RTMP = En(nc.sbuf_tensor("RTMP", [128, T], BF16))[:]
            TT2 = En(nc.sbuf_tensor("TT2", [128, 1024], BF16))[:]
            WQP = En(nc.sbuf_tensor("WQP", [128, 1024], BF16))[:]
            WKP = En(nc.sbuf_tensor("WKP", [128, 1024], BF16))[:]
            WVP = En(nc.sbuf_tensor("WVP", [128, 1024], BF16))[:]
            WQRP = En(nc.sbuf_tensor("WQRP", [128, 1024], BF16))[:]
            WKRP = En(nc.sbuf_tensor("WKRP", [128, 1024], BF16))[:]
            XRES = En(nc.sbuf_tensor("XRES", [128, 2 * 1024], F32))[:]
            X3O = En(nc.sbuf_tensor("X3O", [128, 2 * 1024], F32))[:]
            SQS = En(nc.sbuf_tensor("SQS", [128, 1024], BF16))[:]
            LNS = En(nc.sbuf_tensor("LNS", [128, 16], F32))[:]
            IDS = En(nc.sbuf_tensor("IDS", [128, 128], BF16))[:]
            EPS = En(nc.sbuf_tensor("EPS", [128, 1], F32))[:]
        else:
            XT = sh["B1"]
            XBF = sh["B2"]
            UST = sh["B3"][:, 0:16384]
            WOS = sh["B4"][:, 0:8192]
            COS = sh["B4"][:, 8192:10240]
            SIN = sh["B4"][:, 10240:12288]
            QR = sh["B5"][:, 0:2048]
            KR = sh["B5"][:, 2048:4096]
            VTF = sh["B5"][:, 4096:6144]
            VTM = sh["B5"][:, 6144:8192]
            RTMP = sh["B5"][:, 8192:10240]
            TT2 = sh["B5"][:, 10240:11264]
            WQP = sh["B5"][:, 11264:12288]
            WKP = sh["B5"][:, 12288:13312]
            WVP = sh["B5"][:, 13312:14336]
            WQRP = sh["B5"][:, 14336:15360]
            WKRP = sh["B5"][:, 15360:16384]
            XRES = sh["XRES"]
            X3O = sh["XOUT"]
            SQS = sh["SQS"]
            LNS = sh["LNS"]
            IDS = sh["IDS"]
            EPS = sh["EPS"]
        ATTNT = XT
        ATTN = XBF
        UTS = En(nc.sbuf_tensor(tag + "UTS", [128, 128], BF16))
        ONC = En(nc.sbuf_tensor(tag + "ONC", [128, 1], BF16))

        if sh is None:
            psPF = En(nc.psum_tensor("psPF", [128, 1024], F32))[:]
            psQKa = En(nc.psum_tensor("psQKa", [128, 512], F32))[:]
            psQKb = En(nc.psum_tensor("psQKb", [128, 512], F32))[:]
            psAV2 = [En(nc.psum_tensor(f"psAV{i}", [128, 65], F32))[:] for i in range(2)]
            psT = En(nc.psum_tensor("psT", [128, 1024], BF16))[:]
        else:
            PS = sh["PS"]
            psPF = PS[0]
            psQKa = PS[1][:, 0:512]
            psQKb = PS[1][:, 512:1024]
            psAV2 = [PS[2][:, 0:65], PS[2][:, 512:577]]
            psT = PS[3][:, 0:512].bitcast(BF16)

        if sh is None:
            s_ld = Ctr(ctx, nc, "s_ld")
            s_gd = Ctr(ctx, nc, "s_gd")
            s_pe = Ctr(ctx, nc, "s_pe")
            s_sc = Ctr(ctx, nc, "s_sc")
            s_dv = Ctr(ctx, nc, "s_dv")
            s_w = Ctr(ctx, nc, "s_w")
            s_wo = Ctr(ctx, nc, "s_wo")
            s_x = [Ctr(ctx, nc, "s_x0"), Ctr(ctx, nc, "s_x1")]
            s_y = [Ctr(ctx, nc, "s_y0"), Ctr(ctx, nc, "s_y1")]
            block = En(nc.Block())
        else:
            s_ld = sh["s_ld"]; s_gd = sh["s_gd"]; s_pe = sh["s_pe"]
            s_sc = sh["s_sc"]; s_dv = sh["s_dv"]; s_w = sh["s_w"]
            s_wo = sh["s_wo"]; s_x = sh["s_x"]; s_y = sh["s_y"]
            block = sh["block"]
        y_in0, y_in1 = s_y[0].v, s_y[1].v
        pe_in, dv_in, sc_in = s_pe.v, s_dv.v, s_sc.v

        ld0 = s_ld.take(16 * 6)

        @block.sync
        def _(sync):
            sync.wait_ge(s_pe.h, pe_in)
            sync.wait_ge(s_dv.h, dv_in)
            sync.wait_ge(s_sc.h, sc_in)
            sync.dma_start(out=IDS[:], in_=ident.ap()).then_inc(s_ld.h, 16)
            sync.dma_start(out=EPS[:], in_=epsc.ap()).then_inc(s_ld.h, 16)
            sync.dma_start(out=COS[:], in_=cosc.ap()).then_inc(s_ld.h, 16)
            sync.dma_start(out=SIN[:], in_=sinc.ap()).then_inc(s_ld.h, 16)
            sync.dma_start(out=UTS[:], in_=utric.ap()).then_inc(s_ld.h, 16)
            sync.dma_start(out=ONC[:], in_=onesc.ap()).then_inc(s_ld.h, 16)

        gd_wo = s_wo.take(16)

        @block.gpsimd
        def _(g):
            g.wait_ge(s_pe.h, pe_in)
            g.wait_ge(s_dv.h, dv_in)
            src = bass.AP(woT.tensor, 0, [[1024, 128], [131072, 8], [1, 1024]])
            g.dma_start(out=WOS[:].rearrange("p (i o) -> p i o", i=8),
                        in_=src).then_inc(s_wo.h, 16)

        for b in range(B):
            pe_before_x = s_pe.v
            dv_before_x = s_dv.v
            gd_x = s_gd.take(16)

            @block.gpsimd
            def _(g, b=b, pe_before_x=pe_before_x, dv_before_x=dv_before_x):
                g.wait_ge(s_pe.h, pe_before_x)
                g.wait_ge(s_dv.h, dv_before_x)
                g.wait_ge(s_y[0].h, y_in0)
                g.wait_ge(s_y[1].h, y_in1)
                src = bass.AP(x2i.tensor, b * 16 * 128 * 1024,
                              [[1024, 128], [128 * 1024, 16], [1, 1024]])
                g.dma_start(out=XBF[:].rearrange("p (c d) -> p c d", c=16),
                            in_=src).then_inc(s_gd.h, 16)

            pe1 = s_pe.take(16)
            dv1 = s_dv.take(16)

            @block.tensor
            def _(t, gd_x=gd_x, pe1=pe1, dv1=dv1):
                t.wait_ge(s_gd.h, gd_x + 16)
                t.wait_ge(s_ld.h, ld0 + 96)
                for ci in range(16):
                    if ci > 0:
                        t.wait_ge(s_dv.h, dv1 + ci)
                    for ch in range(8):
                        last = t.transpose(
                            psT[:, 128 * ch:128 * (ch + 1)],
                            XBF[:, ci * 1024 + 128 * ch:ci * 1024 + 128 * (ch + 1)],
                            IDS[:])
                    last.then_inc(s_pe.h, 1)

            @block.vector
            def _(v, pe1=pe1, dv1=dv1):
                cur = [dv1]
                op = mkop(v, s_dv, cur)
                for ci in range(16):
                    v.wait_ge(s_pe.h, pe1 + ci + 1)
                    out_ap = bass.AP(XT[:].tensor, XT[:].offset + 128 * ci,
                                     [[8 * T, 128], [T, 8], [1, 128]])
                    op("tensor_copy", out_ap,
                       psT[:].rearrange("p (c f) -> p c f", c=8))

            for hp in range(8):
                wld = s_w.take(16 * 11)
                pe_prev_pair = s_pe.v
                dv_prev_pair = s_dv.v

                @block.gpsimd
                def _(g, hp=hp, pe_prev_pair=pe_prev_pair):
                    g.wait_ge(s_pe.h, pe_prev_pair)
                    for wdram, wsb in ((wqT, WQP), (wkT, WKP), (wvT, WVP)):
                        src = bass.AP(wdram.tensor, 128 * hp,
                                      [[1024, 128], [131072, 8], [1, 128]])
                        g.dma_start(out=wsb[:].rearrange("p (i o) -> p i o", i=8),
                                    in_=src).then_inc(s_w.h, 16)
                    for wdram, wsb in ((wqT, WQRP), (wkT, WKRP)):
                        wpit = wsb[:].tensor.shape[1]
                        for half in range(2):
                            for hl in range(2):
                                src = bass.AP(
                                    wdram.tensor,
                                    128 * hp + 64 * hl + (1 - half) * 32,
                                    [[1024, 128], [131072, 8], [1, 32]])
                                dst = bass.AP(
                                    wsb[:].tensor,
                                    wsb[:].offset + 64 * hl + half * 32,
                                    [[wpit, 128], [128, 8], [1, 32]])
                                g.dma_start(out=dst, in_=src).then_inc(s_w.h, 16)
                dv_neg = s_dv.take(2)

                @block.vector
                def _(v, wld=wld, dv_neg=dv_neg):
                    cur = [dv_neg]
                    op = mkop(v, s_dv, cur)
                    v.wait_ge(s_w.h, wld + 16 * 11)
                    for wsb in (WQRP, WKRP):
                        wpit = wsb[:].tensor.shape[1]
                        neg = bass.AP(wsb[:].tensor, wsb[:].offset,
                                      [[wpit, 128], [64, 16], [1, 32]])
                        op("tensor_scalar", neg, neg, -1.0, None, ALU.mult)

                projs = [("qrot", WQRP, None), ("q", WQP, QR),
                         ("krot", WKRP, None), ("k", WKP, KR),
                         ("v", WVP, VTF)]
                pe_pf = s_pe.take(5 * 2)
                dv_pf = s_dv.take(2 * (1 + 3 + 1 + 3 + 1))

                @block.tensor
                def _(t, b=b, hp=hp, wld=wld, pe_pf=pe_pf, dv_pf=dv_pf,
                      dv_neg=dv_neg, dv1=dv1):
                    t.wait_ge(s_dv.h, dv_neg + 2)
                    t.wait_ge(s_dv.h, dv1 + 16)
                    k = 0
                    for pi, (nm, wsb, dst) in enumerate(projs):
                        for th in range(2):
                            t.wait_ge(s_dv.h, dv_pf + _PF_EVAC_BEFORE[k])
                            for ich in range(8):
                                for half in range(2):
                                    last = t.matmul(
                                        psPF[:, 512 * (th % 1) + 512 * half:
                                             512 * (th % 1) + 512 * (half + 1)]
                                        if False else
                                        psPF[:, 512 * half:512 * (half + 1)],
                                        wsb[:, 128 * ich:128 * (ich + 1)],
                                        XT[:, ich * T + 1024 * th + 512 * half:
                                           ich * T + 1024 * th + 512 * (half + 1)],
                                        start=(ich == 0), stop=(ich == 7))
                            last.then_inc(s_pe.h, 1)
                            k += 1

                @block.vector
                def _(v, hp=hp, pe_pf=pe_pf, dv_pf=dv_pf):
                    cur = [dv_pf]
                    op = mkop(v, s_dv, cur)
                    k = 0
                    for pi, (nm, wsb, dst) in enumerate(projs):
                        for th in range(2):
                            v.wait_ge(s_pe.h, pe_pf + k + 1)
                            sl = slice(1024 * th, 1024 * (th + 1))
                            if nm in ("qrot", "krot"):
                                op("tensor_copy", RTMP[:, sl], psPF[:])
                            elif nm == "v":
                                op("tensor_copy", VTF[:, sl], psPF[:])
                            else:
                                op("tensor_tensor", dst[:, sl], psPF[:],
                                   COS[:, sl], ALU.mult)
                                op("tensor_tensor", TT2[:], RTMP[:, sl],
                                   SIN[:, sl], ALU.mult)
                                op("tensor_tensor", dst[:, sl], dst[:, sl],
                                   TT2[:], ALU.add)
                            k += 1

                pe_vt = s_pe.take(2)
                dv_vt = s_dv.take(2)

                @block.tensor
                def _(t, pe_vt=pe_vt, dv_pf=dv_pf, dv_vt=dv_vt):
                    t.wait_ge(s_dv.h, dv_pf + 18)
                    for half in range(2):
                        if half == 1:
                            t.wait_ge(s_dv.h, dv_vt + 1)
                        for j in range(8):
                            last = t.transpose(
                                psT[:, 128 * j:128 * (j + 1)],
                                VTF[:, 1024 * half + 128 * j:
                                    1024 * half + 128 * (j + 1)], IDS[:])
                        last.then_inc(s_pe.h, 1)

                @block.vector
                def _(v, pe_vt=pe_vt, dv_vt=dv_vt):
                    cur = [dv_vt]
                    op = mkop(v, s_dv, cur)
                    for half in range(2):
                        v.wait_ge(s_pe.h, pe_vt + half + 1)
                        op("tensor_copy", VTM[:, 1024 * half:1024 * (half + 1)],
                           psT[:])

                for tq in range(4):
                    nr = 4 * tq + 4
                    pe_qk = s_pe.take(nr)
                    sc_ex = s_sc.take(2 * nr)
                    dv_msk = s_dv.take(2 * 4)

                    @block.tensor
                    def _(t, hp=hp, tq=tq, nr=nr, pe_qk=pe_qk, sc_ex=sc_ex,
                          dv_vt=dv_vt):
                        t.wait_ge(s_dv.h, dv_vt + 2)
                        for r in range(nr):
                            if r > 0:
                                t.wait_ge(s_sc.h, sc_ex + 2 * r)
                            t.matmul(psQKa[:], KR[0:64, 128 * r:128 * (r + 1)],
                                     QR[0:64, 512 * tq:512 * (tq + 1)],
                                     start=True, stop=True,
                                     tile_position=(0, 0))
                            t.matmul(psQKb[:], KR[64:128, 128 * r:128 * (r + 1)],
                                     QR[64:128, 512 * tq:512 * (tq + 1)],
                                     start=True, stop=True,
                                     tile_position=(64, 0)).then_inc(s_pe.h, 1)

                    @block.scalar
                    def _(s, hp=hp, tq=tq, nr=nr, pe_qk=pe_qk, sc_ex=sc_ex,
                          dv_msk=dv_msk):
                        for r in range(nr):
                            s.wait_ge(s_pe.h, pe_qk + r + 1)
                            if r == 0:
                                pass
                            s.activation(UST[:, 1024 * r:1024 * r + 512],
                                         psQKa[:], AF.Exp, scale=0.125
                                         ).then_inc(s_sc.h, 1)
                            s.activation(UST[:, 1024 * r + 512:1024 * (r + 1)],
                                         psQKb[:], AF.Exp, scale=0.125
                                         ).then_inc(s_sc.h, 1)

                    @block.vector
                    def _(v, tq=tq, nr=nr, sc_ex=sc_ex, dv_msk=dv_msk):
                        cur = [dv_msk]
                        op = mkop(v, s_dv, cur)
                        for tbl in range(4):
                            r = 4 * tq + tbl
                            v.wait_ge(s_sc.h, sc_ex + 2 * (r + 1))
                            for hl in range(2):
                                sl = slice(1024 * r + 512 * hl + 128 * tbl,
                                           1024 * r + 512 * hl + 128 * (tbl + 1))
                                op("tensor_tensor", UST[:, sl], UST[:, sl],
                                   UTS[:], ALU.mult)

                    pe_av = s_pe.take(8)
                    dv_av = s_dv.take(8 * 3)

                    @block.tensor
                    def _(t, b=b, hp=hp, tq=tq, nr=nr, sc_ex=sc_ex,
                          dv_msk=dv_msk, pe_av=pe_av, dv_av=dv_av):
                        t.wait_ge(s_sc.h, sc_ex + 2 * nr)
                        t.wait_ge(s_dv.h, dv_msk + 8)
                        for tbl in range(4):
                            tb = 4 * tq + tbl
                            for hl in range(2):
                                if tbl > 0:
                                    t.wait_ge(s_dv.h,
                                              dv_av + 3 * (2 * (tbl - 1) + hl + 1))
                                psv = psAV2[hl]
                                for r in range(tb + 1):
                                    t.matmul(psv[:, 0:64],
                                             UST[:, 1024 * r + 512 * hl + 128 * tbl:
                                                 1024 * r + 512 * hl + 128 * (tbl + 1)],
                                             VTM[:, 128 * r + 64 * hl:
                                                 128 * r + 64 * (hl + 1)],
                                             start=(r == 0), stop=False,
                                             skip_group_check=True)
                                    last = t.matmul(
                                        psv[:, 64:65],
                                        UST[:, 1024 * r + 512 * hl + 128 * tbl:
                                            1024 * r + 512 * hl + 128 * (tbl + 1)],
                                        ONC[:], start=False, stop=(r == tb),
                                        skip_group_check=True)
                                last.then_inc(s_pe.h, 1)

                    @block.vector
                    def _(v, b=b, hp=hp, tq=tq, pe_av=pe_av, dv_av=dv_av):
                        cur = [dv_av]
                        op = mkop(v, s_dv, cur)
                        for tbl in range(4):
                            tb = 4 * tq + tbl
                            for hl in range(2):
                                v.wait_ge(s_pe.h, pe_av + 2 * tbl + hl + 1)
                                psv = psAV2[hl]
                                op("tensor_copy", LNS[:, 8 + hl:9 + hl],
                                   psv[:, 64:65])
                                op("reciprocal", LNS[:, 10 + hl:11 + hl],
                                   LNS[:, 8 + hl:9 + hl])
                                op("tensor_scalar",
                                   ATTN[:, tb * 1024 + 128 * hp + 64 * hl:
                                        tb * 1024 + 128 * hp + 64 * (hl + 1)],
                                   psv[:, 0:64],
                                   LNS[:, 10 + hl:11 + hl], None, ALU.mult)

            pe_at = s_pe.take(16)
            dv_at = s_dv.take(16)
            dv_all_attn = s_dv.v - 16

            @block.tensor
            def _(t, b=b, pe_at=pe_at, dv_at=dv_at, dv_all=s_dv.v - 16):
                t.wait_ge(s_dv.h, dv_all)
                for ci in range(16):
                    if ci > 0:
                        t.wait_ge(s_dv.h, dv_at + ci)
                    for ch in range(8):
                        last = t.transpose(
                            psT[:, 128 * ch:128 * (ch + 1)],
                            ATTN[:, ci * 1024 + 128 * ch:ci * 1024 + 128 * (ch + 1)],
                            IDS[:])
                    last.then_inc(s_pe.h, 1)

            @block.vector
            def _(v, pe_at=pe_at, dv_at=dv_at):
                cur = [dv_at]
                op = mkop(v, s_dv, cur)
                for ci in range(16):
                    v.wait_ge(s_pe.h, pe_at + ci + 1)
                    out_ap = bass.AP(ATTNT[:].tensor, ATTNT[:].offset + 128 * ci,
                                     [[8 * T, 128], [T, 8], [1, 128]])
                    op("tensor_copy", out_ap,
                       psT[:].rearrange("p (c f) -> p c f", c=8))

            pe_op = s_pe.take(16)
            x8 = [s_x[0].take(16 * 8), s_x[1].take(16 * 8)]
            y8 = [s_y[0].take(16 * 8), s_y[1].take(16 * 8)]
            dv8 = s_dv.take(16 * 7)
            sc8 = s_sc.take(16 * 2)

            @block.tensor
            def _(t, b=b, dv_at=dv_at, pe_op=pe_op, dv8=dv8, gd_wo=gd_wo):
                t.wait_ge(s_dv.h, dv_at + 16)
                t.wait_ge(s_wo.h, gd_wo + 16)
                for tb in range(16):
                    if tb > 0:
                        t.wait_ge(s_dv.h, dv8 + 7 * (tb - 1) + 1)
                    for ich in range(8):
                        stat = ATTNT[:, ich * T + 128 * tb:ich * T + 128 * (tb + 1)]
                        for half in range(2):
                            last = t.matmul(
                                psPF[:, 512 * half:512 * (half + 1)], stat,
                                WOS[:, 1024 * ich + 512 * half:
                                    1024 * ich + 512 * (half + 1)],
                                start=(ich == 0), stop=(ich == 7))
                    last.then_inc(s_pe.h, 1)

            @block.sync
            def _(sync, b=b, dv8=dv8):
                sync.wait_ge(s_y[0].h, y_in0)
                sync.wait_ge(s_y[1].h, y_in1)
                for ci in range(16):
                    sync.wait_ge(s_dv.h, dv8 + 7 * (ci - 2) + 7 if ci >= 2 else dv8)
                    src = bass.AP(x2i.tensor, (b * 16 + ci) * 128 * 1024,
                                  [[1024, 128], [1, 1024]])
                    sync.dma_start(out=XRES[:, 1024 * (ci % 2):1024 * (ci % 2 + 1)],
                                   in_=src).then_inc(s_x[ci % 2].h, 16)

            @block.vector
            def _(v, b=b, x8=x8, y8=y8, sc8=sc8, dv8=dv8, pe_op=pe_op):
                cur = [dv8]
                op = mkop(v, s_dv, cur)
                for ci in range(16):
                    xs = XRES[:, 1024 * (ci % 2):1024 * (ci % 2 + 1)]
                    v.wait_ge(s_x[ci % 2].h, x8[ci % 2] + 16 * (ci // 2 + 1))
                    v.wait_ge(s_pe.h, pe_op + ci + 1)
                    op("tensor_tensor", xs, xs, psPF[:], ALU.add)
                    op("reduce_sum", LNS[:, 0:1], xs, axis=AX.X, negate=True)
                    op("tensor_scalar", LNS[:, 1:2], LNS[:, 0:1], 1.0 / D, None,
                       ALU.mult)
                    op("tensor_tensor", LNS[:, 6:7], LNS[:, 1:2], LNS[:, 1:2],
                       ALU.mult)
                    v.wait_ge(s_sc.h, sc8 + 2 * ci + 1)
                    op("scalar_tensor_tensor", LNS[:, 2:3], LNS[:, 3:4], 1.0 / D,
                       LNS[:, 6:7], ALU.mult, ALU.subtract)
                    v.wait_ge(s_sc.h, sc8 + 2 * ci + 2)
                    op("reciprocal", LNS[:, 5:6], LNS[:, 4:5])
                    ob = X3O[:, 1024 * (ci % 2):1024 * (ci % 2 + 1)]
                    v.wait_ge(s_y[ci % 2].h,
                              y8[ci % 2] + 16 * (ci // 2) if ci >= 2 else y8[ci % 2])
                    op("tensor_scalar", ob, xs, LNS[:, 1:2], LNS[:, 5:6],
                       ALU.add, ALU.mult)

            @block.scalar
            def _(s, b=b, dv8=dv8, sc8=sc8):
                for ci in range(16):
                    xs = XRES[:, 1024 * (ci % 2):1024 * (ci % 2 + 1)]
                    s.wait_ge(s_dv.h, dv8 + 7 * ci + 1)
                    s.activation(SQS[:], xs, AF.Square,
                                 accum_out=LNS[:, 3:4]).then_inc(s_sc.h, 1)
                    s.wait_ge(s_dv.h, dv8 + 7 * ci + 5)
                    s.activation(LNS[:, 4:5], LNS[:, 2:3], AF.Sqrt,
                                 bias=EPS[:, 0:1]).then_inc(s_sc.h, 1)

            @block.gpsimd
            def _(g, b=b, dv8=dv8, y8=y8):
                for ci in range(16):
                    g.wait_ge(s_dv.h, dv8 + 7 * (ci + 1))
                    dst = bass.AP(x3.tensor, (b * 16 + ci) * 128 * 1024,
                                  [[1024, 128], [1, 1024]])
                    g.dma_start(
                        out=dst, in_=X3O[:, 1024 * (ci % 2):1024 * (ci % 2 + 1)]
                    ).then_inc(s_y[ci % 2].h, 16)

        @block.gpsimd
        def _(g):
            g.wait_ge(s_y[0].h, s_y[0].v)
            g.wait_ge(s_y[1].h, s_y[1].v)

    return nc


_PF_EVAC_BEFORE = {}
_k = 0
_dv_count = 0
for _pi, _nm in enumerate(["qrot", "q", "krot", "k", "v"]):
    for _th in range(2):
        _PF_EVAC_BEFORE[_k] = _dv_count
        _dv_count += 1 if _nm in ("qrot", "krot", "v") else 3
        _k += 1



def build_l1ffn(sh=None):
    DH = 4096
    tag = "" if sh is None else "E_"
    nc = (bass.Bass("TRN2", target_bir_lowering=False, debug=False)
          if sh is None else sh["nc"])
    x3i = (nc.dram_tensor("x3i", [NT, D], F32, kind="ExternalInput").ap()
           if sh is None else sh["x3d"])
    w1T = nc.dram_tensor("w1T", [D, DH], F32, kind="ExternalInput").ap()
    w2T = nc.dram_tensor("w2T", [DH, D], F32, kind="ExternalInput").ap()
    cwL = nc.dram_tensor("cwL", [128, 96], F32, kind="ExternalInput").ap()
    cbL = nc.dram_tensor("cbL", [128, 32], F32, kind="ExternalInput").ap()
    out = nc.dram_tensor("out", [NT, D], F32, kind="ExternalOutput").ap()
    t1d = nc.dram_tensor("t1d", [32, 128, 1024], BF16, kind="Internal").ap()
    t2d = nc.dram_tensor("t2d", [DH, D], BF16, kind="Internal").ap()

    ident = nc.inline_tensor(np.eye(128, dtype=BFNP), name=tag + "identbf")
    epsc = nc.inline_tensor(np.full((128, 1), 1e-5, dtype=np.float32), name=tag + "epsc")
    onesf = nc.inline_tensor(np.ones((128, 1), dtype=np.float32), name=tag + "onesf")
    onesr = nc.inline_tensor(np.ones((1, 128), dtype=np.float32), name=tag + "onesr")

    NW = 514

    with (ExitStack() if sh is None else contextlib.nullcontext()) as _cm:
        ctx = _cm if sh is None else sh["ctx"]
        En = ctx.enter_context
        if sh is None:
            XQT = En(nc.sbuf_tensor("XQT", [128, 8 * T], BF16))[:]
            XQD = En(nc.sbuf_tensor("XQD", [128, 16 * 1024], BF16))[:]
            HFC = En(nc.sbuf_tensor("HFC", [128, 32 * 516], BF16))[:]
            SFM = En(nc.sbuf_tensor("SFM", [128, 32 * 512], BF16))[:]
            XQ2F = En(nc.sbuf_tensor("XQ2F", [128, 32 * 512], BF16))[:]
            XRES = En(nc.sbuf_tensor("XRES", [128, 2 * 1024], F32))[:]
            XOO = En(nc.sbuf_tensor("XOO", [128, 2 * 1024], F32))[:]
            SQS = En(nc.sbuf_tensor("SQS", [128, 1024], BF16))[:]
            LNS = En(nc.sbuf_tensor("LNS", [128, 8], F32))[:]
            IDS = En(nc.sbuf_tensor("IDS", [128, 128], BF16))[:]
            EPS = En(nc.sbuf_tensor("EPS", [128, 1], F32))[:]
        else:
            XQT = sh["B1"]
            XQD = sh["B2"]
            HFC = sh["B3"]
            SFM = sh["B4"][:, 0:16384]
            XQ2F = sh["B5"][:, 0:16384]
            XRES = sh["XRES"]
            XOO = sh["XOUT"]
            SQS = sh["SQS"]
            LNS = sh["LNS"][:, 0:8]
            IDS = sh["IDS"]
            EPS = sh["EPS"]
        XQ2D = XQD
        W1S = XQT[:, 0:8192].bitcast(F32)
        TRS = XQT[:, 8192:16384].bitcast(F32)
        TRN = XQD[:, 0:4096]
        TS1 = [En(nc.sbuf_tensor(tag + f"TS1_{i}", [128, 1024], BF16)) for i in range(2)]
        TS2 = [En(nc.sbuf_tensor(tag + f"TS2_{i}", [128, 1024], BF16)) for i in range(2)]
        CW = En(nc.sbuf_tensor(tag + "CW", [128, 96], F32))
        CB = En(nc.sbuf_tensor(tag + "CB", [128, 32], F32))
        AB = En(nc.sbuf_tensor(tag + "AB", [128, 40], F32))
        SC = En(nc.sbuf_tensor(tag + "SC", [128, 8], F32))
        SCT = En(nc.sbuf_tensor(tag + "SCT", [1, 8], F32))
        DQ = En(nc.sbuf_tensor(tag + "DQ", [128, 64], F32))
        DQ2 = En(nc.sbuf_tensor(tag + "DQ2", [128, 8], F32))
        ONF = En(nc.sbuf_tensor(tag + "ONF", [128, 1], F32))
        ON1 = En(nc.sbuf_tensor(tag + "ON1", [1, 128], F32))

        if sh is None:
            psHa = En(nc.psum_tensor("psHa", [128, 258], F32))[:]
            psHb = En(nc.psum_tensor("psHb", [128, 258], F32))[:]
            psT = En(nc.psum_tensor("psT", [128, 1024], BF16))[:]
            psO = [En(nc.psum_tensor(f"psO{i}", [128, 1024], F32))[:] for i in range(2)]
        else:
            PS = sh["PS"]
            psHa = PS[0][:, 0:258]
            psHb = PS[0][:, 512:770]
            psT = PS[1][:, 0:512].bitcast(BF16)
            psO = [PS[2], PS[3]]
        psHs = [(psHa[:], psHb[:]),
                (psO[0][:, 0:258], psO[0][:, 512:770])]
        psS = psHa[0:1, 0:1]
        psBC = psHb[:, 0:1]

        if sh is None:
            s_ld = Ctr(ctx, nc, "s_ld")
            s_gd = Ctr(ctx, nc, "s_gd")
            s_pe = Ctr(ctx, nc, "s_pe")
            s_sc = Ctr(ctx, nc, "s_sc")
            s_dv = Ctr(ctx, nc, "s_dv")
            s_w1 = Ctr(ctx, nc, "s_w1")
            s_w2 = Ctr(ctx, nc, "s_w2")
            s_ts = Ctr(ctx, nc, "s_ts")
            s_t2 = Ctr(ctx, nc, "s_t2")
            s_x = [Ctr(ctx, nc, "s_x0"), Ctr(ctx, nc, "s_x1")]
            s_q = [Ctr(ctx, nc, "s_q0"), Ctr(ctx, nc, "s_q1")]
            s_y = [Ctr(ctx, nc, "s_y0"), Ctr(ctx, nc, "s_y1")]
            block = En(nc.Block())
        else:
            s_ld = sh["s_ld"]; s_gd = sh["s_gd"]; s_pe = sh["s_pe"]
            s_sc = sh["s_sc"]; s_dv = sh["s_dv"]; s_w1 = sh["s_w1"]
            s_w2 = sh["s_w2"]; s_ts = sh["s_ts"]; s_t2 = sh["s_t2"]
            s_x = sh["s_x"]; s_q = sh["s_q"]; s_y = sh["s_y"]
            block = sh["block"]
        y_in0, y_in1 = s_y[0].v, s_y[1].v
        pe_in, dv_in, sc_in = s_pe.v, s_dv.v, s_sc.v

        ld0 = s_ld.take(16 * 6)

        @block.sync
        def _(sync):
            sync.wait_ge(s_pe.h, pe_in)
            sync.wait_ge(s_dv.h, dv_in)
            sync.wait_ge(s_sc.h, sc_in)
            sync.dma_start(out=IDS[:], in_=ident.ap()).then_inc(s_ld.h, 16)
            sync.dma_start(out=EPS[:], in_=epsc.ap()).then_inc(s_ld.h, 16)
            sync.dma_start(out=ONF[:], in_=onesf.ap()).then_inc(s_ld.h, 16)
            sync.dma_start(out=ON1[:], in_=onesr.ap()).then_inc(s_ld.h, 16)
            sync.dma_start(out=CW[:], in_=cwL).then_inc(s_ld.h, 16)
            sync.dma_start(out=CB[:], in_=cbL).then_inc(s_ld.h, 16)

        for wi, (wdram, nt, tdram, scol, mcol) in enumerate(
                [(w1T, 8, t1d, 0, 1), (w2T, 32, t2d, 2, 3)]):
            wid = DH if wi == 0 else 1024
            w1l = s_w1.take(16 * nt)
            dvA = s_dv.take(nt + 3)
            peA = s_pe.take(1)
            dvB = s_dv.take(6)
            scA = s_sc.take(0)

            @block.sync
            def _(sync, wdram=wdram, nt=nt, wid=wid, w1l=w1l, dvA=dvA):
                sync.wait_ge(s_dv.h, dvA)
                sync.wait_ge(s_pe.h, pe_in)
                for i in range(nt):
                    if i >= 1:
                        sync.wait_ge(s_dv.h, dvA + i)
                    src = bass.AP(wdram.tensor, i * 128 * wid,
                                  [[wid, 128], [1, wid]])
                    sync.dma_start(out=W1S[:, 0:wid], in_=src
                                   ).then_inc(s_w1.h, 16)

            @block.vector
            def _(v, nt=nt, wid=wid, w1l=w1l, dvA=dvA):
                cur = [dvA]
                op = mkop(v, s_dv, cur)
                for i in range(nt):
                    v.wait_ge(s_w1.h, w1l + 16 * (i + 1))
                    op("tensor_reduce", AB[:, i:i + 1], W1S[:, 0:wid],
                       axis=AX.X, op=ALU.add, apply_absolute_value=True)
                op("tensor_reduce", AB[:, 36:37], AB[:, 0:nt], axis=AX.X,
                   op=ALU.add)
                op("tensor_copy", AB[:, 37:38], AB[:, 36:37])
                op("tensor_copy", AB[:, 38:39], AB[:, 36:37])

            @block.tensor
            def _(t, dvA=dvA, peA=peA, nt=nt):
                t.wait_ge(s_dv.h, dvA + nt + 3)
                t.wait_ge(s_ld.h, ld0 + 96)
                t.matmul(psS, ONF[:], AB[:, 36:37], start=True, stop=True
                         ).then_inc(s_pe.h, 1)

            @block.vector
            def _(v, peA=peA, dvB=dvB, wi=wi, scol=scol, mcol=mcol, wid=wid,
                  nt=nt):
                cur = [dvB]
                op = mkop(v, s_dv, cur)
                v.wait_ge(s_pe.h, peA + 1)
                op("tensor_copy", SCT[:, 0:1], psS)
                op("tensor_scalar", SCT[:, 1:2], SCT[:, 0:1],
                   1.0 / (nt * 128 * wid), None, ALU.mult)
                op("tensor_scalar", SCT[:, 1:2], SCT[:, 1:2], 1e-5, None,
                   ALU.max)
                op("reciprocal", SCT[:, 2:3], SCT[:, 1:2])
                op("tensor_copy", SCT[:, 4:5], SCT[:, 2:3])
                op("tensor_copy", SCT[:, 5:6], SCT[:, 1:2])

            peB = s_pe.take(1)
            dvC = s_dv.take(1)

            @block.tensor
            def _(t, dvB=dvB, peB=peB):
                t.wait_ge(s_dv.h, dvB + 6)
                t.matmul(psBC.tensor_or_self() if False else psHb[:, 0:2],
                         ON1[:], SCT[:, 4:6], start=True, stop=True
                         ).then_inc(s_pe.h, 1)

            @block.vector
            def _(v, peB=peB, dvC=dvC, scol=scol):
                cur = [dvC]
                op = mkop(v, s_dv, cur)
                v.wait_ge(s_pe.h, peB + 1)
                op("tensor_copy", SC[:, scol:scol + 2], psHb[:, 0:2])

            w2l = s_w1.take(16 * nt)
            dvD = s_dv.take(nt * 3)
            gdT = s_gd.take(16 * nt)

            @block.sync
            def _(sync, wdram=wdram, nt=nt, wid=wid, w2l=w2l, dvD=dvD,
                  dvC=dvC):
                for i in range(nt):
                    sync.wait_ge(s_dv.h, dvD + 3 * i if i >= 1 else dvC + 1)
                    src = bass.AP(wdram.tensor, i * 128 * wid,
                                  [[wid, 128], [1, wid]])
                    sync.dma_start(out=W1S[:, 0:wid], in_=src
                                   ).then_inc(s_w1.h, 16)

            @block.vector
            def _(v, nt=nt, wid=wid, w2l=w2l, dvD=dvD, scol=scol, gdT=gdT):
                cur = [dvD]
                op = mkop(v, s_dv, cur)
                v.wait_ge(s_gd.h, gdT)
                for i in range(nt):
                    v.wait_ge(s_w1.h, w2l + 16 * (i + 1))
                    if i >= 1:
                        v.wait_ge(s_gd.h, gdT + 16 * i)
                    op("tensor_scalar", TRS[:, 0:wid], W1S[:, 0:wid],
                       SC[:, scol:scol + 1], MAGIC, ALU.mult, ALU.add)
                    op("tensor_scalar", TRS[:, 0:wid], TRS[:, 0:wid],
                       -MAGIC, 1.0, ALU.add, ALU.min)
                    op("tensor_scalar", TRN[:, 0:wid], TRS[:, 0:wid],
                       -1.0, None, ALU.max)

            @block.gpsimd
            def _(g, nt=nt, wid=wid, tdram=tdram, dvD=dvD, gdT=gdT, wi=wi):
                for i in range(nt):
                    g.wait_ge(s_dv.h, dvD + 3 * (i + 1))
                    if wi == 0:
                        dst = bass.AP(tdram.tensor, i * 128,
                                      [[1024, 128], [128 * 1024, 32], [1, 128]])
                        g.dma_start(out=dst,
                                    in_=TRN[:, 0:wid].rearrange(
                                        "p (o c) -> p o c", o=32)
                                    ).then_inc(s_gd.h, 16)
                    else:
                        dst = bass.AP(tdram.tensor, i * 128 * wid,
                                      [[wid, 128], [1, wid]])
                        g.dma_start(out=dst, in_=TRN[:, 0:wid]
                                    ).then_inc(s_gd.h, 16)

        dvCW = s_dv.take(2)

        @block.vector
        def _(v, dvCW=dvCW):
            cur = [dvCW]
            op = mkop(v, s_dv, cur)
            v.wait_ge(s_ld.h, ld0 + 96)
            op("tensor_scalar", CW[:], CW[:], SC[:, 1:2], None, ALU.mult)
            op("tensor_copy", AB[:, 39:40], SC[:, 3:4])

        for b in range(B):
            xq = [s_q[0].take(16 * 8), s_q[1].take(16 * 8)]
            dvQ = s_dv.take(16 * 7)

            @block.sync
            def _(sync, b=b, xq=xq, dvQ=dvQ):
                sync.wait_ge(s_y[0].h, y_in0)
                sync.wait_ge(s_y[1].h, y_in1)
                for ci in range(16):
                    sync.wait_ge(s_dv.h, dvQ + 7 * (ci - 2) + 7 if ci >= 2 else dvQ)
                    src = bass.AP(x3i.tensor, (b * 16 + ci) * 128 * 1024,
                                  [[1024, 128], [1, 1024]])
                    sync.dma_start(out=XRES[:, 1024 * (ci % 2):1024 * (ci % 2 + 1)],
                                   in_=src).then_inc(s_q[ci % 2].h, 16)

            @block.vector
            def _(v, b=b, xq=xq, dvQ=dvQ, ypv0=s_y[0].v, ypv1=s_y[1].v,
                  pepv=s_pe.v, gdpv=s_gd.v):
                cur = [dvQ]
                op = mkop(v, s_dv, cur)
                v.wait_ge(s_y[0].h, ypv0)
                v.wait_ge(s_y[1].h, ypv1)
                v.wait_ge(s_pe.h, pepv)
                v.wait_ge(s_gd.h, gdpv)
                for ci in range(16):
                    xs = XRES[:, 1024 * (ci % 2):1024 * (ci % 2 + 1)]
                    v.wait_ge(s_q[ci % 2].h, xq[ci % 2] + 16 * (ci // 2 + 1))
                    op("reduce_max", DQ[:, 4 * ci:4 * ci + 1], xs, axis=AX.X,
                       apply_absolute_value=True)
                    op("tensor_scalar", DQ[:, 4 * ci:4 * ci + 1],
                       DQ[:, 4 * ci:4 * ci + 1], 1e-5, None, ALU.max)
                    op("reciprocal", DQ[:, 4 * ci + 1:4 * ci + 2],
                       DQ[:, 4 * ci:4 * ci + 1])
                    op("tensor_scalar", DQ[:, 4 * ci + 1:4 * ci + 2],
                       DQ[:, 4 * ci + 1:4 * ci + 2], 127.0, None, ALU.mult)
                    op("tensor_scalar", DQ[:, 4 * ci + 2:4 * ci + 3],
                       DQ[:, 4 * ci:4 * ci + 1], 1.0 / 127.0, None, ALU.mult)
                    op("tensor_scalar", XOO[:, 0:1024], xs,
                       DQ[:, 4 * ci + 1:4 * ci + 2], MAGIC, ALU.mult, ALU.add)
                    op("tensor_scalar", XQD[:, ci * 1024:(ci + 1) * 1024],
                       XOO[:, 0:1024],
                       -MAGIC, DQ[:, 4 * ci + 2:4 * ci + 3], ALU.add, ALU.mult)

            pe1 = s_pe.take(16)
            dv1 = s_dv.take(16)

            @block.tensor
            def _(t, b=b, dvQ=dvQ, pe1=pe1, dv1=dv1):
                t.wait_ge(s_dv.h, dvQ + 112)
                for ci in range(16):
                    if ci > 0:
                        t.wait_ge(s_dv.h, dv1 + ci)
                    for ch in range(8):
                        last = t.transpose(
                            psT[:, 128 * ch:128 * (ch + 1)],
                            XQD[:, ci * 1024 + 128 * ch:ci * 1024 + 128 * (ch + 1)],
                            IDS[:])
                    last.then_inc(s_pe.h, 1)

            @block.vector
            def _(v, pe1=pe1, dv1=dv1):
                cur = [dv1]
                op = mkop(v, s_dv, cur)
                for ci in range(16):
                    v.wait_ge(s_pe.h, pe1 + ci + 1)
                    out_ap = bass.AP(XQT[:].tensor, XQT[:].offset + 128 * ci,
                                     [[8 * T, 128], [T, 8], [1, 128]])
                    op("tensor_copy", out_ap,
                       psT[:].rearrange("p (c f) -> p c f", c=8))

            for tt in range(4):
                t0 = 512 * tt
                lo = max(t0 - 1, 0)
                hi = min(t0 + 513, 2048)
                wa = lo
                wb = t0 + 255
                na = 258 if tt > 0 else 257
                nb = 258 if tt < 3 else 257
                offa = 0 if tt > 0 else 1
                offb = 256

                tl1 = s_ts.take(16 * 32)
                pe_m1 = s_pe.take(32 * 2)
                dv_m1 = s_dv.take(32 * (2 + (1 if tt == 0 or tt == 3 else 0) + 3))
                sc_m1 = s_sc.take(32)
                nEv = 2 + (1 if tt in (0, 3) else 0) + 3

                @block.gpsimd
                def _(g, b=b, tt=tt, tl1=tl1, pe_m1=pe_m1):
                    for och in range(32):
                        if och >= 2:
                            g.wait_ge(s_pe.h, pe_m1 + 2 * (och - 1))
                        src = bass.AP(t1d.tensor, och * 128 * 1024,
                                      [[1024, 128], [1, 1024]])
                        g.dma_start(out=TS1[och % 2][:], in_=src
                                    ).then_inc(s_ts.h, 16)

                @block.tensor
                def _(t, b=b, tt=tt, tl1=tl1, pe_m1=pe_m1, dv_m1=dv_m1,
                      dv1=dv1, wa=wa, wb=wb, na=na, nb=nb, nEv=nEv):
                    t.wait_ge(s_dv.h, dv1 + 16)
                    for och in range(32):
                        t.wait_ge(s_ts.h, tl1 + 16 * (och + 1))
                        if och > 1:
                            t.wait_ge(s_dv.h, dv_m1 + nEv * (och - 2) + 2)
                        pa, pb = psHs[och % 2]
                        for ich in range(8):
                            stat = TS1[och % 2][:, 128 * ich:128 * (ich + 1)]
                            t.matmul(pa[:, 0:na], stat,
                                     XQT[:, ich * T + wa:ich * T + wa + na],
                                     start=(ich == 0), stop=(ich == 7),
                                     skip_group_check=True)
                            last = t.matmul(pb[:, 0:nb], stat,
                                            XQT[:, ich * T + wb:ich * T + wb + nb],
                                            start=(ich == 0), stop=(ich == 7),
                                            skip_group_check=True)
                        last.then_inc(s_pe.h, 2)

                @block.vector
                def _(v, b=b, tt=tt, pe_m1=pe_m1, dv_m1=dv_m1, sc_m1=sc_m1,
                      na=na, nb=nb, offa=offa, nEv=nEv):
                    cur = [dv_m1]
                    op = mkop(v, s_dv, cur)
                    for och in range(32):
                        v.wait_ge(s_pe.h, pe_m1 + 2 * (och + 1))
                        base = 516 * och
                        pa, pb = psHs[och % 2]
                        op("tensor_copy", HFC[:, base + offa:base + offa + na],
                           pa[:, 0:na])
                        op("tensor_copy", HFC[:, base + 256:base + 256 + nb],
                           pb[:, 0:nb])
                        if tt == 0:
                            op("memset", HFC[:, base:base + 1], 0.0)
                        elif tt == 3:
                            op("memset", HFC[:, base + 513:base + 514], 0.0)
                        op("tensor_scalar", SFM[:, 512 * och:512 * (och + 1)],
                           HFC[:, base:base + 512], CW[:, 3 * och:3 * och + 1],
                           None, ALU.mult)
                        op("scalar_tensor_tensor",
                           SFM[:, 512 * och:512 * (och + 1)],
                           HFC[:, base + 1:base + 513],
                           CW[:, 3 * och + 1:3 * och + 2],
                           SFM[:, 512 * och:512 * (och + 1)], ALU.mult, ALU.add)
                        op("scalar_tensor_tensor",
                           SFM[:, 512 * och:512 * (och + 1)],
                           HFC[:, base + 2:base + 514],
                           CW[:, 3 * och + 2:3 * och + 3],
                           SFM[:, 512 * och:512 * (och + 1)], ALU.mult, ALU.add)

                @block.scalar
                def _(s, b=b, tt=tt, dv_m1=dv_m1, sc_m1=sc_m1, nEv=nEv):
                    for och in range(32):
                        s.wait_ge(s_dv.h, dv_m1 + nEv * (och + 1))
                        s.activation(SFM[:, 512 * och:512 * (och + 1)],
                                     SFM[:, 512 * och:512 * (och + 1)], AF.Silu,
                                     bias=CB[:, och:och + 1]).then_inc(s_sc.h, 1)

                pe_t1 = s_pe.take(16)
                dv_t1 = s_dv.take(16)

                @block.tensor
                def _(t, sc_m1=sc_m1, pe_t1=pe_t1, dv_t1=dv_t1):
                    t.wait_ge(s_sc.h, sc_m1 + 32)
                    k = 0
                    for tb4 in range(4):
                        for part in range(4):
                            if k > 0:
                                t.wait_ge(s_dv.h, dv_t1 + k)
                            for j in range(8):
                                och = part * 8 + j
                                last = t.transpose(
                                    psT[:, 128 * j:128 * (j + 1)],
                                    SFM[:, 512 * och + 128 * tb4:
                                        512 * och + 128 * (tb4 + 1)], IDS[:])
                            last.then_inc(s_pe.h, 1)
                            k += 1

                @block.vector
                def _(v, pe_t1=pe_t1, dv_t1=dv_t1):
                    cur = [dv_t1]
                    op = mkop(v, s_dv, cur)
                    k = 0
                    for tb4 in range(4):
                        for part in range(4):
                            v.wait_ge(s_pe.h, pe_t1 + k + 1)
                            op("tensor_copy",
                               XQ2D[:, 4096 * tb4 + 1024 * part:
                                    4096 * tb4 + 1024 * (part + 1)], psT[:])
                            k += 1

                dv_q2 = s_dv.take(4 * 9)

                @block.vector
                def _(v, dv_q2=dv_q2, dv_t1=dv_t1):
                    cur = [dv_q2]
                    op = mkop(v, s_dv, cur)
                    v.wait_ge(s_dv.h, dv_t1 + 16)
                    for tb4 in range(4):
                        xs = XQ2D[:, 4096 * tb4:4096 * (tb4 + 1)]
                        op("reduce_max", DQ2[:, 0:1], xs, axis=AX.X,
                           apply_absolute_value=True)
                        op("tensor_scalar", DQ2[:, 0:1], DQ2[:, 0:1], 1e-5,
                           None, ALU.max)
                        op("reciprocal", DQ2[:, 1:2], DQ2[:, 0:1])
                        op("tensor_scalar", DQ2[:, 1:2], DQ2[:, 1:2], 127.0,
                           None, ALU.mult)
                        op("tensor_scalar", DQ2[:, 2 + tb4:3 + tb4], DQ2[:, 0:1],
                           1.0 / 127.0, None, ALU.mult)
                        for hf in range(2):
                            xh = XQ2D[:, 4096 * tb4 + 2048 * hf:
                                      4096 * tb4 + 2048 * (hf + 1)]
                            op("tensor_scalar", XRES[:], xh, DQ2[:, 1:2],
                               MAGIC, ALU.mult, ALU.add)
                            op("tensor_scalar", xh, XRES[:], -MAGIC, None,
                               ALU.add)

                pe_t2 = s_pe.take(16)
                dv_t2 = s_dv.take(16)

                @block.tensor
                def _(t, dv_q2=dv_q2, pe_t2=pe_t2, dv_t2=dv_t2):
                    t.wait_ge(s_dv.h, dv_q2 + 36)
                    k = 0
                    for part in range(4):
                        for tb4 in range(4):
                            if k > 0:
                                t.wait_ge(s_dv.h, dv_t2 + k)
                            for j in range(8):
                                hch = part * 8 + j
                                last = t.transpose(
                                    psT[:, 128 * j:128 * (j + 1)],
                                    XQ2D[:, 4096 * tb4 + 128 * hch:
                                        4096 * tb4 + 128 * (hch + 1)], IDS[:])
                            last.then_inc(s_pe.h, 1)
                            k += 1

                @block.vector
                def _(v, pe_t2=pe_t2, dv_t2=dv_t2):
                    cur = [dv_t2]
                    op = mkop(v, s_dv, cur)
                    k = 0
                    for part in range(4):
                        for tb4 in range(4):
                            v.wait_ge(s_pe.h, pe_t2 + k + 1)
                            out_ap = bass.AP(
                                XQ2F[:].tensor,
                                XQ2F[:].offset + 512 * 8 * part + 128 * tb4,
                                [[32 * 512, 128], [512, 8], [1, 128]])
                            op("tensor_copy", out_ap,
                               psT[:].rearrange("p (c f) -> p c f", c=8))
                            k += 1

                for tt2 in range(2):
                    tl2 = s_ts2 = s_t2.take(16 * 32)
                    pe_m2 = s_pe.take(32)
                    x8 = [s_x[0].take(16), s_x[1].take(16)]
                    y8 = [s_y[0].take(16), s_y[1].take(16)]
                    dv8 = s_dv.take(2 * 8)
                    sc8 = s_sc.take(2 * 2)

                    @block.gpsimd
                    def _(g, b=b, tt=tt, tt2=tt2, tl2=tl2, pe_m2=pe_m2):
                        for hch in range(32):
                            if hch >= 2:
                                g.wait_ge(s_pe.h, pe_m2 + hch - 1)
                            src = bass.AP(t2d.tensor, 128 * hch * 1024,
                                          [[1024, 128], [1, 1024]])
                            g.dma_start(out=TS2[hch % 2][:], in_=src
                                        ).then_inc(s_t2.h, 16)

                    @block.tensor
                    def _(t, b=b, tt=tt, tt2=tt2, tl2=tl2, pe_m2=pe_m2,
                          dv_t2=dv_t2, dv8=dv8):
                        t.wait_ge(s_dv.h, dv_t2 + 16)
                        if True:
                            t.wait_ge(s_dv.h, dv8)
                        for hch in range(32):
                            t.wait_ge(s_t2.h, tl2 + 16 * (hch + 1))
                            for tbl in range(2):
                                stat = XQ2F[:, 512 * hch + 256 * tt2 + 128 * tbl:
                                            512 * hch + 256 * tt2 + 128 * (tbl + 1)]
                                t.matmul(psO[tbl][:, 0:512], stat,
                                         TS2[hch % 2][:, 0:512],
                                         start=(hch == 0), stop=(hch == 31))
                                last = t.matmul(psO[tbl][:, 512:1024], stat,
                                                TS2[hch % 2][:, 512:1024],
                                                start=(hch == 0), stop=(hch == 31))
                            last.then_inc(s_pe.h, 1)

                    @block.sync
                    def _(sync, b=b, tt=tt, tt2=tt2, x8=x8, dv8=dv8):
                        for tbl in range(2):
                            ci = 4 * tt + 2 * tt2 + tbl
                            sync.wait_ge(s_dv.h, dv8)
                            src = bass.AP(x3i.tensor,
                                          (b * 16 + ci) * 128 * 1024,
                                          [[1024, 128], [1, 1024]])
                            sync.dma_start(
                                out=XRES[:, 1024 * tbl:1024 * (tbl + 1)],
                                in_=src).then_inc(s_x[tbl].h, 16)

                    @block.vector
                    def _(v, b=b, tt=tt, tt2=tt2, x8=x8, y8=y8, dv8=dv8,
                          sc8=sc8, pe_m2=pe_m2):
                        cur = [dv8]
                        op = mkop(v, s_dv, cur)
                        for tbl in range(2):
                            ci = 4 * tt + 2 * tt2 + tbl
                            tb4 = 2 * tt2 + tbl
                            xs = XRES[:, 1024 * tbl:1024 * (tbl + 1)]
                            v.wait_ge(s_x[tbl].h, x8[tbl] + 16)
                            v.wait_ge(s_pe.h, pe_m2 + 32)
                            op("tensor_tensor", LNS[:, 7:8],
                               DQ2[:, 2 + tb4:3 + tb4], SC[:, 3:4], ALU.mult)
                            op("scalar_tensor_tensor", xs, psO[tbl][:],
                               LNS[:, 7:8], xs, ALU.mult, ALU.add)
                            op("reduce_sum", LNS[:, 0:1], xs, axis=AX.X,
                               negate=True)
                            op("tensor_scalar", LNS[:, 1:2], LNS[:, 0:1],
                               1.0 / D, None, ALU.mult)
                            op("tensor_tensor", LNS[:, 6:7], LNS[:, 1:2],
                               LNS[:, 1:2], ALU.mult)
                            v.wait_ge(s_sc.h, sc8 + 2 * tbl + 1)
                            op("scalar_tensor_tensor", LNS[:, 2:3], LNS[:, 3:4],
                               1.0 / D, LNS[:, 6:7], ALU.mult, ALU.subtract)
                            v.wait_ge(s_sc.h, sc8 + 2 * tbl + 2)
                            op("reciprocal", LNS[:, 5:6], LNS[:, 4:5])
                            ob = XOO[:, 1024 * tbl:1024 * (tbl + 1)]
                            v.wait_ge(s_y[tbl].h, y8[tbl])
                            op("tensor_scalar", ob, xs, LNS[:, 1:2],
                               LNS[:, 5:6], ALU.add, ALU.mult)

                    @block.scalar
                    def _(s, b=b, dv8=dv8, sc8=sc8):
                        for tbl in range(2):
                            xs = XRES[:, 1024 * tbl:1024 * (tbl + 1)]
                            s.wait_ge(s_dv.h, dv8 + 8 * tbl + 2)
                            s.activation(SQS[:], xs, AF.Square,
                                         accum_out=LNS[:, 3:4]).then_inc(s_sc.h, 1)
                            s.wait_ge(s_dv.h, dv8 + 8 * tbl + 6)
                            s.activation(LNS[:, 4:5], LNS[:, 2:3], AF.Sqrt,
                                         bias=EPS[:, 0:1]).then_inc(s_sc.h, 1)

                    @block.gpsimd
                    def _(g, b=b, tt=tt, tt2=tt2, dv8=dv8, y8=y8):
                        for tbl in range(2):
                            ci = 4 * tt + 2 * tt2 + tbl
                            g.wait_ge(s_dv.h, dv8 + 8 * (tbl + 1))
                            dst = bass.AP(out.tensor, (b * 16 + ci) * 128 * 1024,
                                          [[1024, 128], [1, 1024]])
                            g.dma_start(
                                out=dst,
                                in_=XOO[:, 1024 * tbl:1024 * (tbl + 1)]
                            ).then_inc(s_y[tbl].h, 16)

        @block.gpsimd
        def _(g):
            g.wait_ge(s_y[0].h, s_y[0].v)
            g.wait_ge(s_y[1].h, s_y[1].v)

    return nc



def build_cde():
    nc = bass.Bass("TRN2", target_bir_lowering=False, debug=False)
    x2d = nc.dram_tensor("x2d", [NT, D], F32, kind="Internal").ap()
    x3d = nc.dram_tensor("x3d", [NT, D], F32, kind="Internal").ap()

    with ExitStack() as ctx:
        En = ctx.enter_context
        sh = {"nc": nc, "ctx": ctx, "x2d": x2d, "x3d": x3d}
        sh["B1"] = En(nc.sbuf_tensor("B1", [128, 16384], BF16))[:]
        sh["B2"] = En(nc.sbuf_tensor("B2", [128, 16384], BF16))[:]
        sh["B3"] = En(nc.sbuf_tensor("B3", [128, 16512], BF16))[:]
        sh["B4"] = En(nc.sbuf_tensor("B4", [128, 16384], BF16))[:]
        sh["B5"] = En(nc.sbuf_tensor("B5", [128, 16384], BF16))[:]
        sh["PS"] = [En(nc.psum_tensor(f"PS{i}", [128, 1024], F32))[:]
                    for i in range(4)]
        sh["XRES"] = En(nc.sbuf_tensor("XRES", [128, 2048], F32))[:]
        sh["XOUT"] = En(nc.sbuf_tensor("XOUT", [128, 2048], F32))[:]
        sh["SQS"] = En(nc.sbuf_tensor("SQS", [128, 1024], BF16))[:]
        sh["LNS"] = En(nc.sbuf_tensor("LNS", [128, 16], F32))[:]
        sh["IDS"] = En(nc.sbuf_tensor("IDS", [128, 128], BF16))[:]
        sh["EPS"] = En(nc.sbuf_tensor("EPS", [128, 1], F32))[:]
        for nm in ["s_ld", "s_gd", "s_pe", "s_sc", "s_dv", "s_wa", "s_wg2",
                   "s_wp", "s_gs", "s_go", "s_w", "s_wo", "s_w1", "s_w2",
                   "s_ts", "s_t2"]:
            sh[nm] = Ctr(ctx, nc, nm)
        sh["s_x"] = [Ctr(ctx, nc, "s_x0"), Ctr(ctx, nc, "s_x1")]
        sh["s_y"] = [Ctr(ctx, nc, "s_y0"), Ctr(ctx, nc, "s_y1")]
        sh["s_q"] = [Ctr(ctx, nc, "s_q0"), Ctr(ctx, nc, "s_q1")]
        sh["block"] = En(nc.Block())

        build_moe(sh)
        build_l1attn(sh)
        build_l1ffn(sh)

    return nc



_NC_CACHE = {}


def _get_nc(name, builder):
    if name not in _NC_CACHE:
        _NC_CACHE[name] = builder()
    return _NC_CACHE[name]


def kernel(**inputs):
    cores = list(range(NCORES))
    x0_full = np.ascontiguousarray(
        inputs["L0_attn_x"].reshape(16 * T, D).astype(np.float32, copy=False))
    wg = np.ascontiguousarray(inputs["L0_ffn_w_gate"].astype(np.float32, copy=False))
    cfc = np.ascontiguousarray(inputs["L0_ffn_c_fc"].astype(np.float32, copy=False))
    cpj = np.ascontiguousarray(inputs["L0_ffn_c_proj"].astype(np.float32, copy=False))
    wqT = np.ascontiguousarray(inputs["L1_attn_Wq"].astype(np.float32, copy=False).T)
    wkT = np.ascontiguousarray(inputs["L1_attn_Wk"].astype(np.float32, copy=False).T)
    wvT = np.ascontiguousarray(inputs["L1_attn_Wv"].astype(np.float32, copy=False).T)
    woT = np.ascontiguousarray(inputs["L1_attn_Wo"].astype(np.float32, copy=False).T)
    w1T = np.ascontiguousarray(inputs["L1_ffn_w1"].astype(np.float32, copy=False).T)
    w2T = np.ascontiguousarray(inputs["L1_ffn_w2"].astype(np.float32, copy=False).T)
    cw = inputs["L1_ffn_conv_w"].astype(np.float32, copy=False).reshape(4096, 3)
    cwL = np.ascontiguousarray(cw.reshape(32, 128, 3).transpose(1, 0, 2).reshape(128, 96))
    cbL = np.ascontiguousarray(
        inputs["L1_ffn_conv_b"].astype(np.float32, copy=False).reshape(32, 128).T)

    ncA = _get_nc("A", build_l0attn)
    resA = run_bass_kernel_spmd(
        ncA, [{"x0": np.ascontiguousarray(x0_full[c * NT:(c + 1) * NT]),
               "wg": wg} for c in cores], core_ids=cores)
    metaF = np.concatenate([resA.results[c]["meta"] for c in cores], axis=0)
    metaF = np.ascontiguousarray(metaF)
    x1d = [np.ascontiguousarray(resA.results[c]["x1"]) for c in cores]

    ncCDE = _get_nc("CDE", build_cde)
    resCDE = run_bass_kernel_spmd(
        ncCDE, [{"x1o": x1d[c], "metaF": metaF, "cfc": cfc, "cpj": cpj,
                 "wqT": wqT, "wkT": wkT, "wvT": wvT, "woT": woT,
                 "w1T": w1T, "w2T": w2T, "cwL": cwL, "cbL": cbL}
                for c in cores], core_ids=cores)
    out = np.concatenate([resCDE.results[c]["out"] for c in cores], axis=0)
    return out.reshape(16, T, D).astype(np.float32, copy=False)


# version 03
# speedup vs baseline: 1.2644x; 1.1201x over previous
import numpy as np
import ml_dtypes
import contextlib
from contextlib import ExitStack
import concourse.bass as bass
import concourse.mybir as mybir
from concourse.bass_utils import run_bass_kernel_spmd

F32 = mybir.dt.float32
BF16 = mybir.dt.bfloat16
AF = mybir.ActivationFunctionType
ALU = mybir.AluOpType
AX = mybir.AxisListType
BFNP = ml_dtypes.bfloat16
MAGIC = 12582912.0

B = 2
T = 2048
D = 1024
H = 16
HD = 64
NT = B * T
E_EXP = 8
NCORES = 8


class Ctr:
    def __init__(self, ctx, nc, name):
        self.h = ctx.enter_context(nc.semaphore(name))
        self.v = 0
    def take(self, n):
        base = self.v
        self.v += n
        return base


def mkop(eng, sem, cursor):
    def op(fn, *a, **k):
        eng.wait_ge(sem.h, cursor[0])
        inst = getattr(eng, fn)(*a, **k)
        inst.then_inc(sem.h, 1)
        cursor[0] += 1
        return inst
    return op



def build_l0attn():
    nc = bass.Bass("TRN2", target_bir_lowering=False, debug=False)
    x0 = nc.dram_tensor("x0", [NT, D], F32, kind="ExternalInput").ap()
    wg = nc.dram_tensor("wg", [D, E_EXP], F32, kind="ExternalInput").ap()
    x1 = nc.dram_tensor("x1", [NT, D], F32, kind="ExternalOutput").ap()
    meta = nc.dram_tensor("meta", [NT, 4], F32, kind="ExternalOutput").ap()

    ident = nc.inline_tensor(np.eye(128, dtype=BFNP), name="identbf")
    e2np = np.zeros((128, 2), dtype=BFNP)
    e2np[0:64, 0] = 1
    e2np[64:128, 1] = 1
    E2c = nc.inline_tensor(e2np, name="E2")
    iota8 = nc.inline_tensor(np.tile(np.arange(8, dtype=np.float32), (128, 1)),
                             name="iota8")
    epsc = nc.inline_tensor(np.full((128, 1), 1e-5, dtype=np.float32), name="epsc")

    with ExitStack() as ctx:
        En = ctx.enter_context
        XT = En(nc.sbuf_tensor("XT", [128, 8 * T], BF16))
        XBF = En(nc.sbuf_tensor("XBF", [128, 16 * 1024], BF16))
        BUFA = En(nc.sbuf_tensor("BUFA", [128, 16 * 1024], BF16))
        ATTN = En(nc.sbuf_tensor("ATTN", [128, 16 * 1024], BF16))
        GH = En(nc.sbuf_tensor("GH", [128, 1024], BF16))
        Z = En(nc.sbuf_tensor("Z", [128, 1024], BF16))
        NEGU = En(nc.sbuf_tensor("NEGU", [128, 256], F32))
        NEGUBF = En(nc.sbuf_tensor("NEGUBF", [128, 256], BF16))
        NEGUFM = En(nc.sbuf_tensor("NEGUFM", [16, T], F32))
        NEGC = En(nc.sbuf_tensor("NEGC", [16, 1], F32))
        SUFM = En(nc.sbuf_tensor("SUFM", [16, T], BF16))
        SUTM = En(nc.sbuf_tensor("SUTM", [128, 256], BF16))
        SUREP = En(nc.sbuf_tensor("SUREP", [128, 1024], BF16))
        ACCT = En(nc.sbuf_tensor("ACCT", [128, 128], F32))
        FS = En(nc.sbuf_tensor("FS", [128, 32], F32))
        FST = En(nc.sbuf_tensor("FST", [128, 32], F32))
        XRES = En(nc.sbuf_tensor("XRES", [128, 2 * 1024], F32))
        X1O = En(nc.sbuf_tensor("X1O", [128, 2 * 1024], F32))
        SQS = En(nc.sbuf_tensor("SQS", [128, 1024], BF16))
        LNS = En(nc.sbuf_tensor("LNS", [128, 8], F32))
        WGB = En(nc.sbuf_tensor("WGB", [128, 64], BF16))
        PRB = En(nc.sbuf_tensor("PRB", [128, 8], F32))
        GSM = En(nc.sbuf_tensor("GSM", [128, 24], F32))
        MT = En(nc.sbuf_tensor("MT", [128, 4], F32))
        IDS = En(nc.sbuf_tensor("IDS", [128, 128], BF16))
        E2S = En(nc.sbuf_tensor("E2S", [128, 2], BF16))
        EPS = En(nc.sbuf_tensor("EPS", [128, 1], F32))
        IO8 = En(nc.sbuf_tensor("IO8", [128, 8], F32))

        psQKa = En(nc.psum_tensor("psQKa", [128, 512], F32))
        psQKb = En(nc.psum_tensor("psQKb", [128, 512], F32))
        psAV0 = En(nc.psum_tensor("psAV0", [128, 1024], F32))
        psAV1 = En(nc.psum_tensor("psAV1", [128, 1024], F32))
        psT = En(nc.psum_tensor("psT", [128, 1024], BF16))
        psU = psT[:, 0:32].bitcast(F32)
        psF = psAV0[0:16, :].bitcast(BF16)
        psSU = psT[:, 0:256]
        psGate = psQKa[:, 0:8]

        s_ld = Ctr(ctx, nc, "s_ld")
        s_gd = Ctr(ctx, nc, "s_gd")
        s_pe = Ctr(ctx, nc, "s_pe")
        s_sc = Ctr(ctx, nc, "s_sc")
        s_dv = Ctr(ctx, nc, "s_dv")
        s_st = Ctr(ctx, nc, "s_st")
        s_x = [Ctr(ctx, nc, "s_x0"), Ctr(ctx, nc, "s_x1")]
        s_wg = Ctr(ctx, nc, "s_wg")
        s_y = [Ctr(ctx, nc, "s_y0"), Ctr(ctx, nc, "s_y1")]
        block = En(nc.Block())

        ld0 = s_ld.take(64)

        @block.sync
        def _(sync):
            sync.dma_start(out=IDS[:], in_=ident.ap()).then_inc(s_ld.h, 16)
            sync.dma_start(out=E2S[:], in_=E2c.ap()).then_inc(s_ld.h, 16)
            sync.dma_start(out=IO8[:], in_=iota8.ap()).then_inc(s_ld.h, 16)
            sync.dma_start(out=EPS[:], in_=epsc.ap()).then_inc(s_ld.h, 16)

        gd_wg = s_wg.take(16)

        @block.gpsimd
        def _(g):
            g.dma_start(
                out=WGB[:].rearrange("p (i e) -> p i e", i=8),
                in_=bass.AP(wg.tensor, 0, [[8, 128], [1024, 8], [1, 8]])
            ).then_inc(s_wg.h, 16)

        def emit_transpose_tm_to_fm(src_dram_off_rows, gd_wait=None):
            raise NotImplementedError

        for b in range(B):
            pe_before_p0 = s_pe.v
            gd_x = s_gd.take(16)

            @block.gpsimd
            def _(g, b=b, pe_before_p0=pe_before_p0):
                g.wait_ge(s_pe.h, pe_before_p0)
                src = bass.AP(x0.tensor, b * 16 * 128 * 1024,
                              [[1024, 128], [128 * 1024, 16], [1, 1024]])
                g.dma_start(out=XBF[:].rearrange("p (c d) -> p c d", c=16),
                            in_=src).then_inc(s_gd.h, 16)

            pe1 = s_pe.take(16)
            dv1 = s_dv.take(16)

            @block.tensor
            def _(t, gd_x=gd_x, pe1=pe1, dv1=dv1):
                t.wait_ge(s_gd.h, gd_x + 16)
                t.wait_ge(s_ld.h, ld0 + 64)
                for ci in range(16):
                    if ci > 0:
                        t.wait_ge(s_dv.h, dv1 + ci)
                    for ch in range(8):
                        last = t.transpose(
                            psT[:, 128 * ch:128 * (ch + 1)],
                            XBF[:, ci * 1024 + 128 * ch:ci * 1024 + 128 * (ch + 1)],
                            IDS[:])
                    last.then_inc(s_pe.h, 1)

            @block.vector
            def _(v, pe1=pe1, dv1=dv1):
                cur = [dv1]
                op = mkop(v, s_dv, cur)
                for ci in range(16):
                    v.wait_ge(s_pe.h, pe1 + ci + 1)
                    out_ap = bass.AP(XT[:].tensor, XT[:].offset + 128 * ci,
                                     [[8 * T, 128], [T, 8], [1, 128]])
                    op("tensor_copy", out_ap,
                       psT[:].rearrange("p (c f) -> p c f", c=8))

            dv2 = s_dv.take(24)

            @block.vector
            def _(v, dv2=dv2):
                cur = [dv2]
                op = mkop(v, s_dv, cur)
                for ch in range(8):
                    sl = XT[:, ch * T:(ch + 1) * T]
                    op("reduce_sum", LNS[:, 0:1], sl, axis=AX.X, negate=True)
                    op("tensor_scalar", LNS[:, 1:2], LNS[:, 0:1], 1.0 / T, None,
                       ALU.mult)
                    op("tensor_scalar", sl, sl, LNS[:, 1:2], None, ALU.add)

            sc3 = s_sc.take(8)

            @block.scalar
            def _(s, dv2=dv2):
                s.wait_ge(s_dv.h, dv2 + 24)
                for ch in range(8):
                    s.activation(BUFA[:, ch * T:(ch + 1) * T],
                                 XT[:, ch * T:(ch + 1) * T], AF.Square
                                 ).then_inc(s_sc.h, 1)

            pe4 = s_pe.take(16)
            dv4 = s_dv.take(16)

            @block.tensor
            def _(t, sc3=sc3, pe4=pe4, dv4=dv4):
                t.wait_ge(s_sc.h, sc3 + 8)
                for tb in range(16):
                    if tb > 0:
                        t.wait_ge(s_dv.h, dv4 + tb)
                    for ch in range(8):
                        last = t.matmul(
                            psU[:, 2 * ch:2 * ch + 2],
                            BUFA[:, ch * T + 128 * tb:ch * T + 128 * (tb + 1)],
                            E2S[:], start=(ch == 0), stop=(ch == 7))
                    last.then_inc(s_pe.h, 1)

            @block.vector
            def _(v, pe4=pe4, dv4=dv4):
                cur = [dv4]
                op = mkop(v, s_dv, cur)
                for tb in range(16):
                    v.wait_ge(s_pe.h, pe4 + tb + 1)
                    op("tensor_scalar", NEGU[:, 16 * tb:16 * (tb + 1)], psU,
                       -0.5, None, ALU.mult)

            dv5a = s_dv.take(1)

            @block.vector
            def _(v, dv5a=dv5a):
                cur = [dv5a]
                op = mkop(v, s_dv, cur)
                op("tensor_copy", NEGUBF[:], NEGU[:])

            pe5 = s_pe.take(1)

            @block.tensor
            def _(t, dv5a=dv5a):
                t.wait_ge(s_dv.h, dv5a + 1)
                for tb in range(16):
                    last = t.transpose(psF[:, 128 * tb:128 * (tb + 1)],
                                       NEGUBF[:, 16 * tb:16 * (tb + 1)], IDS[:])
                last.then_inc(s_pe.h, 1)

            dv5b = s_dv.take(2)

            @block.vector
            def _(v, pe5=pe5, dv5b=dv5b):
                cur = [dv5b]
                op = mkop(v, s_dv, cur)
                v.wait_ge(s_pe.h, pe5 + 1)
                op("tensor_copy", NEGUFM[:], psF)
                op("tensor_reduce", NEGC[:], NEGUFM[:], axis=AX.X, op=ALU.min)

            sc5 = s_sc.take(1)

            @block.scalar
            def _(s, dv5b=dv5b):
                s.wait_ge(s_dv.h, dv5b + 2)
                s.activation(SUFM[:], NEGUFM[:], AF.Exp, scale=-1.0,
                             bias=NEGC[:, 0:1]).then_inc(s_sc.h, 1)

            pe5b = s_pe.take(1)

            @block.tensor
            def _(t, sc5=sc5):
                t.wait_ge(s_sc.h, sc5 + 1)
                for tb in range(16):
                    last = t.transpose(psSU[:, 16 * tb:16 * (tb + 1)],
                                       SUFM[:, 128 * tb:128 * (tb + 1)],
                                       IDS[0:16, 0:16])
                last.then_inc(s_pe.h, 1)

            dv5c = s_dv.take(1)

            @block.vector
            def _(v, pe5b=pe5b, dv5c=dv5c):
                cur = [dv5c]
                op = mkop(v, s_dv, cur)
                v.wait_ge(s_pe.h, pe5b + 1)
                op("tensor_copy", SUTM[:], psSU)

            pe_before_p6 = s_pe.v
            dv6 = s_dv.take(32)

            @block.vector
            def _(v, pe_before_p6=pe_before_p6, dv6=dv6):
                cur = [dv6]
                op = mkop(v, s_dv, cur)
                v.wait_ge(s_pe.h, pe_before_p6)
                for ci in range(16):
                    su_b = bass.AP(SUTM[:].tensor, SUTM[:].offset + 16 * ci,
                                   [[256, 128], [1, 16], [0, 64]])
                    op("tensor_copy", SUREP[:].rearrange("p (h j) -> p h j", h=16),
                       su_b)
                    op("tensor_tensor", BUFA[:, ci * 1024:(ci + 1) * 1024],
                       XBF[:, ci * 1024:(ci + 1) * 1024], SUREP[:], ALU.mult)

            dv_prev_hp = s_dv.v
            for hp in range(8):
                pe_qk = s_pe.take(3 * 64)
                sc_ex = s_sc.take(2 * 64)

                @block.tensor
                def _(t, hp=hp, pe_qk=pe_qk, sc_ex=sc_ex, dv_prev_hp=dv_prev_hp):
                    t.wait_ge(s_dv.h, dv_prev_hp)
                    for q in range(4):
                        ps_av = psAV0 if q < 2 else psAV1
                        avcol = 512 * (q % 2)
                        for r in range(16):
                            i = 16 * q + r
                            stat = XT[:, hp * T + 128 * r: hp * T + 128 * (r + 1)]
                            mov = XT[:, hp * T + 512 * q: hp * T + 512 * (q + 1)]
                            if i > 0:
                                t.wait_ge(s_sc.h, sc_ex + 2 * i)
                            t.matmul(psQKa[:], stat[0:64, :], mov[0:64, :],
                                     start=True, stop=True,
                                     tile_position=(0, 0)).then_inc(s_pe.h, 1)
                            t.matmul(psQKb[:], stat[64:128, :], mov[64:128, :],
                                     start=True, stop=True,
                                     tile_position=(64, 0)).then_inc(s_pe.h, 1)
                            t.wait_ge(s_sc.h, sc_ex + 2 * i + 2)
                            wx0 = BUFA[:, r * 1024 + 128 * hp:
                                       r * 1024 + 128 * hp + 64]
                            wx1 = BUFA[:, r * 1024 + 128 * hp + 64:
                                       r * 1024 + 128 * hp + 128]
                            t.matmul(ps_av[0:64, avcol:avcol + 512], wx0,
                                     GH[:, 0:512], start=(r == 0), stop=(r == 15),
                                     tile_position=(0, 0), skip_group_check=True)
                            t.matmul(ps_av[64:128, avcol:avcol + 512], wx1,
                                     GH[:, 512:1024], start=(r == 0), stop=(r == 15),
                                     tile_position=(0, 64), skip_group_check=True
                                     ).then_inc(s_pe.h, 1)

                @block.scalar
                def _(s, hp=hp, pe_qk=pe_qk, sc_ex=sc_ex, dv_prev_hp=dv_prev_hp):
                    s.wait_ge(s_dv.h, dv_prev_hp)
                    for q in range(4):
                        for r in range(16):
                            i = 16 * q + r
                            s.wait_ge(s_pe.h, pe_qk + 3 * i + 2)
                            s.activation(
                                GH[:, 0:512], psQKa[:], AF.Exp,
                                bias=NEGU[:, 16 * r + 2 * hp:16 * r + 2 * hp + 1],
                                accum_out=ACCT[:, r * 8 + q:r * 8 + q + 1]
                            ).then_inc(s_sc.h, 1)
                            s.activation(
                                GH[:, 512:1024], psQKb[:], AF.Exp,
                                bias=NEGU[:, 16 * r + 2 * hp + 1:16 * r + 2 * hp + 2],
                                accum_out=ACCT[:, r * 8 + 4 + q:r * 8 + 4 + q + 1]
                            ).then_inc(s_sc.h, 1)

                dv_fs = s_dv.take(10)

                @block.vector
                def _(v, hp=hp, sc_ex=sc_ex, dv_fs=dv_fs):
                    cur = [dv_fs]
                    op = mkop(v, s_dv, cur)
                    v.wait_ge(s_sc.h, sc_ex + 128)
                    for hl in range(2):
                        off = 4 * hl
                        def acc_ap(k):
                            return bass.AP(ACCT[:].tensor,
                                           ACCT[:].offset + off + k,
                                           [[128, 128], [8, 16]])
                        sl = slice(16 * hl, 16 * hl + 16)
                        op("tensor_tensor", FST[:, sl], acc_ap(0), acc_ap(1), ALU.add)
                        op("tensor_tensor", FS[:, sl], acc_ap(2), acc_ap(3), ALU.add)
                        op("tensor_tensor", FST[:, sl], FST[:, sl], FS[:, sl], ALU.add)
                        su_ap = bass.AP(SUTM[:].tensor,
                                        SUTM[:].offset + 2 * hp + hl,
                                        [[256, 128], [16, 16]])
                        op("tensor_tensor", FST[:, sl], FST[:, sl], su_ap, ALU.mult)
                        op("reciprocal", FS[:, sl], FST[:, sl])

                pe_after_qk = s_pe.v
                for half in range(2):
                    dv_z = s_dv.take(1)

                    @block.vector
                    def _(v, half=half, pe_after_qk=pe_after_qk, dv_z=dv_z):
                        cur = [dv_z]
                        op = mkop(v, s_dv, cur)
                        v.wait_ge(s_pe.h, pe_after_qk + half)
                        ps = psAV0 if half == 0 else psAV1
                        op("tensor_copy", Z[:], ps[:])

                    pe_z = s_pe.take(1)

                    @block.tensor
                    def _(t, dv_z=dv_z):
                        t.wait_ge(s_dv.h, dv_z + 1)
                        for j in range(8):
                            last = t.transpose(psT[:, 128 * j:128 * (j + 1)],
                                               Z[:, 128 * j:128 * (j + 1)], IDS[:])
                        last.then_inc(s_pe.h, 1)

                    dv_w = s_dv.take(16)

                    @block.vector
                    def _(v, hp=hp, half=half, pe_z=pe_z, dv_w=dv_w):
                        cur = [dv_w]
                        op = mkop(v, s_dv, cur)
                        v.wait_ge(s_pe.h, pe_z + 1)
                        for j in range(8):
                            ci = half * 8 + j
                            for hl in range(2):
                                op("tensor_scalar",
                                   ATTN[:, ci * 1024 + 128 * hp + 64 * hl:
                                        ci * 1024 + 128 * hp + 64 * (hl + 1)],
                                   psT[:, 128 * j + 64 * hl:
                                       128 * j + 64 * (hl + 1)],
                                   FS[:, 16 * hl + ci:16 * hl + ci + 1], None,
                                   ALU.mult)
                dv_prev_hp = s_dv.v

            x8 = [s_x[0].take(16 * 8), s_x[1].take(16 * 8)]
            y8 = [s_y[0].take(16 * 8), s_y[1].take(16 * 8)]
            st8 = s_st.take(0)
            dv8 = s_dv.take(16 * 7)
            sc8 = s_sc.take(16 * 2)

            @block.sync
            def _(sync, b=b, dv8=dv8, x8=x8):
                for ci in range(16):
                    sync.wait_ge(s_dv.h, dv8 + 7 * (ci - 2) + 7 if ci >= 2 else dv8)
                    src = bass.AP(x0.tensor, (b * 16 + ci) * 128 * 1024,
                                  [[1024, 128], [1, 1024]])
                    sync.dma_start(out=XRES[:, 1024 * (ci % 2):1024 * (ci % 2 + 1)],
                                   in_=src).then_inc(s_x[ci % 2].h, 16)

            @block.vector
            def _(v, x8=x8, y8=y8, sc8=sc8, dv8=dv8):
                cur = [dv8]
                op = mkop(v, s_dv, cur)
                for ci in range(16):
                    xs = XRES[:, 1024 * (ci % 2):1024 * (ci % 2 + 1)]
                    v.wait_ge(s_x[ci % 2].h, x8[ci % 2] + 16 * (ci // 2 + 1))
                    op("tensor_tensor", xs, xs,
                       ATTN[:, ci * 1024:(ci + 1) * 1024], ALU.add)
                    op("reduce_sum", LNS[:, 0:1], xs, axis=AX.X, negate=True)
                    op("tensor_scalar", LNS[:, 1:2], LNS[:, 0:1], 1.0 / D, None,
                       ALU.mult)
                    op("tensor_tensor", LNS[:, 6:7], LNS[:, 1:2], LNS[:, 1:2],
                       ALU.mult)
                    v.wait_ge(s_sc.h, sc8 + 2 * ci + 1)
                    op("scalar_tensor_tensor", LNS[:, 2:3], LNS[:, 3:4], 1.0 / D,
                       LNS[:, 6:7], ALU.mult, ALU.subtract)
                    v.wait_ge(s_sc.h, sc8 + 2 * ci + 2)
                    op("reciprocal", LNS[:, 5:6], LNS[:, 4:5])
                    ob = X1O[:, 1024 * (ci % 2):1024 * (ci % 2 + 1)]
                    v.wait_ge(s_y[ci % 2].h,
                              y8[ci % 2] + 16 * (ci // 2) if ci >= 2 else y8[ci % 2])
                    op("tensor_scalar", ob, xs, LNS[:, 1:2], LNS[:, 5:6],
                       ALU.add, ALU.mult)

            @block.scalar
            def _(s, dv8=dv8, sc8=sc8):
                for ci in range(16):
                    xs = XRES[:, 1024 * (ci % 2):1024 * (ci % 2 + 1)]
                    s.wait_ge(s_dv.h, dv8 + 7 * ci + 1)
                    s.activation(SQS[:], xs, AF.Square,
                                 accum_out=LNS[:, 3:4]).then_inc(s_sc.h, 1)
                    s.wait_ge(s_dv.h, dv8 + 7 * ci + 5)
                    s.activation(LNS[:, 4:5], LNS[:, 2:3], AF.Sqrt,
                                 bias=EPS[:, 0:1]).then_inc(s_sc.h, 1)

            @block.gpsimd
            def _(g, b=b, dv8=dv8, y8=y8):
                for ci in range(16):
                    g.wait_ge(s_dv.h, dv8 + 7 * (ci + 1))
                    dst = bass.AP(x1.tensor, (b * 16 + ci) * 128 * 1024,
                                  [[1024, 128], [1, 1024]])
                    g.dma_start(
                        out=dst, in_=X1O[:, 1024 * (ci % 2):1024 * (ci % 2 + 1)]
                    ).then_inc(s_y[ci % 2].h, 16)

            pe_before_p9 = s_pe.v
            gd9 = s_gd.take(16)

            @block.gpsimd
            def _(g, y0t=s_y[0].v, y1t=s_y[1].v, pe_before_p9=pe_before_p9, b=b):
                g.wait_ge(s_y[0].h, y0t)
                g.wait_ge(s_y[1].h, y1t)
                g.wait_ge(s_pe.h, pe_before_p9)
                src = bass.AP(x1.tensor, b * 16 * 128 * 1024,
                              [[1024, 128], [128 * 1024, 16], [1, 1024]])
                g.dma_start(out=XBF[:].rearrange("p (c d) -> p c d", c=16),
                            in_=src).then_inc(s_gd.h, 16)

            dv_before_p9 = s_dv.v
            pe9 = s_pe.take(16)
            dv9 = s_dv.take(16)

            @block.tensor
            def _(t, gd9=gd9, pe9=pe9, dv9=dv9, dv_before_p9=dv_before_p9):
                t.wait_ge(s_gd.h, gd9 + 16)
                t.wait_ge(s_dv.h, dv_before_p9)
                for ci in range(16):
                    if ci > 0:
                        t.wait_ge(s_dv.h, dv9 + ci)
                    for ch in range(8):
                        last = t.transpose(
                            psT[:, 128 * ch:128 * (ch + 1)],
                            XBF[:, ci * 1024 + 128 * ch:ci * 1024 + 128 * (ch + 1)],
                            IDS[:])
                    last.then_inc(s_pe.h, 1)

            @block.vector
            def _(v, pe9=pe9, dv9=dv9):
                cur = [dv9]
                op = mkop(v, s_dv, cur)
                for ci in range(16):
                    v.wait_ge(s_pe.h, pe9 + ci + 1)
                    out_ap = bass.AP(XT[:].tensor, XT[:].offset + 128 * ci,
                                     [[8 * T, 128], [T, 8], [1, 128]])
                    op("tensor_copy", out_ap,
                       psT[:].rearrange("p (c f) -> p c f", c=8))

            pe9b = s_pe.take(16)
            sc9 = s_sc.take(16)
            dv9b = s_dv.take(16 * 9)
            st9 = s_st.take(16 * 16)

            @block.tensor
            def _(t, dv9=dv9, pe9b=pe9b, sc9=sc9):
                t.wait_ge(s_dv.h, dv9 + 16)
                t.wait_ge(s_wg.h, gd_wg + 16)
                for tb in range(16):
                    if tb > 0:
                        t.wait_ge(s_sc.h, sc9 + tb)
                    for ich in range(8):
                        last = t.matmul(
                            psGate[:],
                            XT[:, ich * T + 128 * tb:ich * T + 128 * (tb + 1)],
                            WGB[:, ich * 8:(ich + 1) * 8],
                            start=(ich == 0), stop=(ich == 7))
                    last.then_inc(s_pe.h, 1)

            @block.scalar
            def _(s, pe9b=pe9b, dv9b=dv9b, sc9=sc9):
                for tb in range(16):
                    s.wait_ge(s_pe.h, pe9b + tb + 1)
                    if tb > 0:
                        s.wait_ge(s_dv.h, dv9b + 9 * tb)
                    s.activation(PRB[:], psGate, AF.Exp,
                                 accum_out=GSM[:, 0:1]).then_inc(s_sc.h, 1)

            @block.vector
            def _(v, sc9=sc9, dv9b=dv9b, st9=st9):
                cur = [dv9b]
                op = mkop(v, s_dv, cur)
                for tb in range(16):
                    v.wait_ge(s_sc.h, sc9 + tb + 1)
                    v.wait_ge(s_st.h, st9 + 16 * tb)
                    op("reciprocal", GSM[:, 1:2], GSM[:, 0:1])
                    op("tensor_scalar", PRB[:], PRB[:], GSM[:, 1:2], None, ALU.mult)
                    op("reduce_max", MT[:, 2:3], PRB[:], axis=AX.X)
                    op("tensor_scalar", GSM[:, 2:10], PRB[:], MT[:, 2:3], None,
                       ALU.is_ge)
                    op("scalar_tensor_tensor", GSM[:, 16:24], GSM[:, 2:10], 1.0,
                       IO8[:], ALU.mult, ALU.mult, accum_out=MT[:, 0:1])
                    op("scalar_tensor_tensor", GSM[:, 2:10], GSM[:, 2:10], -1e30,
                       PRB[:], ALU.mult, ALU.add)
                    op("reduce_max", MT[:, 3:4], GSM[:, 2:10], axis=AX.X)
                    op("tensor_scalar", GSM[:, 2:10], GSM[:, 2:10], MT[:, 3:4],
                       None, ALU.is_ge)
                    op("scalar_tensor_tensor", GSM[:, 16:24], GSM[:, 2:10], 1.0,
                       IO8[:], ALU.mult, ALU.mult, accum_out=MT[:, 1:2])

            @block.sync
            def _(sync, b=b, dv9b=dv9b, st9=st9):
                for tb in range(16):
                    sync.wait_ge(s_dv.h, dv9b + 9 * (tb + 1))
                    dst = bass.AP(meta.tensor, (b * 16 + tb) * 128 * 4,
                                  [[4, 128], [1, 4]])
                    sync.dma_start(out=dst, in_=MT[:]).then_inc(s_st.h, 16)

        @block.gpsimd
        def _(g):
            g.wait_ge(s_st.h, s_st.v)
            g.wait_ge(s_y[0].h, s_y[0].v)
            g.wait_ge(s_y[1].h, s_y[1].v)

    return nc



def build_moe(sh=None):
    NTOK = 16 * T
    NCQ = NTOK // 128
    tag = "" if sh is None else "C_"
    nc = bass.Bass("TRN2", target_bir_lowering=False, debug=False) if sh is None else sh["nc"]
    x1o = nc.dram_tensor("x1o", [NT, D], F32, kind="ExternalInput").ap()
    metaF = nc.dram_tensor("metaF", [NTOK, 4], F32, kind="ExternalInput").ap()
    cfc = nc.dram_tensor("cfc", [E_EXP, D, 3072], F32, kind="ExternalInput").ap()
    cpj = nc.dram_tensor("cpj", [E_EXP, 1536, D], F32, kind="ExternalInput").ap()
    x2 = (nc.dram_tensor("x2", [NT, D], F32, kind="ExternalOutput").ap()
          if sh is None else sh["x2d"])
    gscr = nc.dram_tensor("gscr", [128, NCQ, 8], F32, kind="Internal").ap()

    ident = nc.inline_tensor(np.eye(128, dtype=BFNP), name=tag + "identbf")
    ltri = np.zeros((128, 128), dtype=BFNP)
    for k in range(128):
        ltri[k, k + 1:] = 1
    LTc = nc.inline_tensor(ltri, name=tag + "LT")
    iota8 = nc.inline_tensor(np.tile(np.arange(8, dtype=np.float32), (128, 1)),
                             name=tag + "iota8")
    epsc = nc.inline_tensor(np.full((128, 1), 1e-5, dtype=np.float32), name=tag + "epsc")
    onesr = nc.inline_tensor(np.ones((1, 128), dtype=np.float32), name=tag + "onesr")
    onesc = nc.inline_tensor(np.ones((128, 1), dtype=BFNP), name=tag + "onesc")

    with (ExitStack() if sh is None else contextlib.nullcontext()) as _cm:
        ctx = _cm if sh is None else sh["ctx"]
        En = ctx.enter_context
        if sh is None:
            XT = En(nc.sbuf_tensor("XT", [128, 8 * T], BF16))[:]
            XBF = En(nc.sbuf_tensor("XBF", [128, 16 * 1024], BF16))[:]
        else:
            XT = sh["B1"]
            XBF = sh["B2"]
        ACC = XBF
        if sh is None:
            CFA = En(nc.sbuf_tensor("CFA", [128, 8 * 1536], BF16))[:]
            CFG = En(nc.sbuf_tensor("CFG", [128, 8 * 1536], BF16))[:]
            CPJ = En(nc.sbuf_tensor("CPJ", [128, 12 * 1024], BF16))[:]
        else:
            CFA = sh["B3"][:, 0:12288]
            CFG = sh["B4"][:, 0:12288]
            CPJ = sh["B5"][:, 0:12288]
        HH1 = En(nc.sbuf_tensor(tag + "HH1", [128, 1536], BF16))
        HH = En(nc.sbuf_tensor(tag + "HH", [128, 1536], BF16))
        HHT = En(nc.sbuf_tensor(tag + "HHT", [128, 1536], BF16))
        if sh is None:
            HH1B = En(nc.sbuf_tensor("HH1B", [128, 1536], BF16))[:]
            HHB = En(nc.sbuf_tensor("HHB", [128, 1536], BF16))[:]
            HHTB = En(nc.sbuf_tensor("HHTB", [128, 1536], BF16))[:]
        else:
            HH1B = sh["B4"][:, 12288:13824]
            HHB = sh["B4"][:, 13824:15360]
            HHTB = sh["B3"][:, 12288:13824]
        HH1s = [HH1[:], HH1B[:]]
        HHs = [HH[:], HHB[:]]
        HHTs = [HHT[:], HHTB[:]]
        MF = XT[:, 0:2048].bitcast(F32)
        OH0 = XT[:, 2048:4096]
        OH1 = XT[:, 4096:6144]
        OHP = XT[:, 6144:8192]
        GALL = XT[:, 8192:12288].bitcast(F32)
        GOWN = En(nc.sbuf_tensor(tag + "GOWN", [128, 256], F32))
        RS = En(nc.sbuf_tensor(tag + "RS", [128, 40], F32))
        BcSB = En(nc.sbuf_tensor(tag + "BcSB", [1, 8], F32))
        LTS = En(nc.sbuf_tensor(tag + "LTS", [128, 128], BF16))
        IO8 = En(nc.sbuf_tensor(tag + "IO8", [128, 8], F32))
        ON1 = En(nc.sbuf_tensor(tag + "ON1", [1, 128], F32))
        ONC = En(nc.sbuf_tensor(tag + "ONC", [128, 1], BF16))
        if sh is None:
            XRES = En(nc.sbuf_tensor("XRES", [128, 2 * 1024], F32))[:]
            X2O = En(nc.sbuf_tensor("X2O", [128, 2 * 1024], F32))[:]
            SQS = En(nc.sbuf_tensor("SQS", [128, 1024], BF16))[:]
            LNS = En(nc.sbuf_tensor("LNS", [128, 8], F32))[:]
            IDS = En(nc.sbuf_tensor("IDS", [128, 128], BF16))[:]
            EPS = En(nc.sbuf_tensor("EPS", [128, 1], F32))[:]
            PS = [En(nc.psum_tensor(f"psb{i}", [128, 1024], F32))[:]
                  for i in range(4)]
        else:
            XRES = sh["XRES"]
            X2O = sh["XOUT"]
            SQS = sh["SQS"][:, 0:1024]
            LNS = sh["LNS"][:, 0:8]
            IDS = sh["IDS"]
            EPS = sh["EPS"]
            PS = sh["PS"]
        psH = [PS[0][:, 0:512], PS[0][:, 512:1024], PS[1][:, 0:512]]
        psH2 = [PS[1][:, 512:1024], PS[3][:, 0:512], PS[3][:, 512:1024]]
        psO = [PS[2][:, 0:512], PS[2][:, 512:1024]]
        psT = PS[0][:, 0:512].bitcast(BF16)
        psTs = [PS[0][:, 0:512].bitcast(BF16),
                PS[1][:, 512:1024].bitcast(BF16)]
        psTs2 = [PS[0][:, 512:1024].bitcast(BF16),
                 PS[3][:, 0:512].bitcast(BF16)]
        psP = psH[0][:, 0:8]
        psB = psH[1][0:1, 0:8]

        if sh is None:
            s_ld = Ctr(ctx, nc, "s_ld")
            s_gd = Ctr(ctx, nc, "s_gd")
            s_pe = Ctr(ctx, nc, "s_pe")
            s_sc = Ctr(ctx, nc, "s_sc")
            s_dv = Ctr(ctx, nc, "s_dv")
            s_wa = Ctr(ctx, nc, "s_wa")
            s_wg2 = Ctr(ctx, nc, "s_wg2")
            s_wp = Ctr(ctx, nc, "s_wp")
            s_x = [Ctr(ctx, nc, "s_x0"), Ctr(ctx, nc, "s_x1")]
            s_y = [Ctr(ctx, nc, "s_y0"), Ctr(ctx, nc, "s_y1")]
            s_gs = Ctr(ctx, nc, "s_gs")
            s_go = Ctr(ctx, nc, "s_go")
            block = En(nc.Block())
        else:
            s_ld = sh["s_ld"]; s_gd = sh["s_gd"]; s_pe = sh["s_pe"]
            s_sc = sh["s_sc"]; s_dv = sh["s_dv"]; s_wa = sh["s_wa"]
            s_wg2 = sh["s_wg2"]; s_wp = sh["s_wp"]; s_x = sh["s_x"]
            s_y = sh["s_y"]; s_gs = sh["s_gs"]; s_go = sh["s_go"]
            block = sh["block"]

        ld0 = s_ld.take(16 * 7)

        @block.sync
        def _(sync):
            sync.dma_start(out=IDS[:], in_=ident.ap()).then_inc(s_ld.h, 16)
            sync.dma_start(out=LTS[:], in_=LTc.ap()).then_inc(s_ld.h, 16)
            sync.dma_start(out=IO8[:], in_=iota8.ap()).then_inc(s_ld.h, 16)
            sync.dma_start(out=EPS[:], in_=epsc.ap()).then_inc(s_ld.h, 16)
            sync.dma_start(out=ON1[:], in_=onesr.ap()).then_inc(s_ld.h, 16)
            sync.dma_start(out=ONC[:], in_=onesc.ap()).then_inc(s_ld.h, 16)
            sync.dma_start(
                out=MF.rearrange("p (c f) -> p c f", c=NCQ),
                in_=bass.AP(metaF.tensor, 0, [[4, 128], [512, NCQ], [1, 4]])
            ).then_inc(s_ld.h, 16)

        dvR1 = s_dv.take(NCQ * 7)

        @block.vector
        def _(v, dvR1=dvR1):
            cur = [dvR1]
            op = mkop(v, s_dv, cur)
            v.wait_ge(s_ld.h, ld0 + 16 * 7)
            for cq in range(NCQ):
                e0c = MF[:, 4 * cq:4 * cq + 1]
                e1c = MF[:, 4 * cq + 1:4 * cq + 2]
                op("tensor_scalar", RS[:, 0:8], IO8[:], e0c, None, ALU.subtract)
                op("tensor_tensor", RS[:, 8:16], RS[:, 0:8], RS[:, 0:8], ALU.mult)
                op("tensor_scalar", OH0[:, 8 * cq:8 * cq + 8], RS[:, 8:16], 0.25,
                   None, ALU.is_le)
                op("tensor_scalar", RS[:, 0:8], IO8[:], e1c, None, ALU.subtract)
                op("tensor_tensor", RS[:, 8:16], RS[:, 0:8], RS[:, 0:8], ALU.mult)
                op("tensor_scalar", OH1[:, 8 * cq:8 * cq + 8], RS[:, 8:16], 0.25,
                   None, ALU.is_le)
                op("tensor_tensor", OHP[:, 8 * cq:8 * cq + 8],
                   OH0[:, 8 * cq:8 * cq + 8], OH1[:, 8 * cq:8 * cq + 8], ALU.add)

        peR = s_pe.take(NCQ)
        dvR2 = s_dv.take(NCQ * 10)

        @block.tensor
        def _(t, peR=peR, dvR2=dvR2, dvR1=dvR1):
            t.wait_ge(s_dv.h, dvR1 + NCQ * 7)
            for cq in range(NCQ):
                t.wait_ge(s_dv.h, dvR2 + 10 * cq + 1)
                ohp = OHP[:, 8 * cq:8 * cq + 8]
                t.matmul(psP, LTS[:], ohp, start=True, stop=False)
                t.matmul(psP, ON1[:], BcSB[:], start=False, stop=True)
                t.matmul(psB, ONC[:], ohp, start=(cq == 0), stop=(cq == NCQ - 1),
                         skip_group_check=True).then_inc(s_pe.h, 1)

        @block.vector
        def _(v, peR=peR, dvR2=dvR2):
            cur = [dvR2]
            op = mkop(v, s_dv, cur)
            for cq in range(NCQ):
                if cq > 0:
                    v.wait_ge(s_pe.h, peR + cq)
                    op("tensor_copy", BcSB[:], psB)
                else:
                    op("memset", BcSB[:], 0.0)
                v.wait_ge(s_pe.h, peR + cq + 1)
                op("scalar_tensor_tensor", RS[:, 16:24], psP, 1.0,
                   OH0[:, 8 * cq:8 * cq + 8], ALU.mult, ALU.mult,
                   accum_out=RS[:, 32:33])
                op("scalar_tensor_tensor", RS[:, 24:32], psP, 1.0,
                   OH1[:, 8 * cq:8 * cq + 8], ALU.mult, ALU.mult,
                   accum_out=RS[:, 33:34])
                op("tensor_scalar", RS[:, 34:35], RS[:, 32:33], 4095.5, None,
                   ALU.is_le)
                op("tensor_scalar", RS[:, 35:36], RS[:, 33:34], 4095.5, None,
                   ALU.is_le)
                op("tensor_tensor", RS[:, 36:37], MF[:, 4 * cq + 2:4 * cq + 3],
                   RS[:, 34:35], ALU.mult)
                op("tensor_tensor", RS[:, 37:38], MF[:, 4 * cq + 3:4 * cq + 4],
                   RS[:, 35:36], ALU.mult)
                op("tensor_scalar", RS[:, 16:24], OH0[:, 8 * cq:8 * cq + 8],
                   RS[:, 36:37], None, ALU.mult)
                op("tensor_scalar", RS[:, 24:32], OH1[:, 8 * cq:8 * cq + 8],
                   RS[:, 37:38], None, ALU.mult)
                op("tensor_tensor", GALL[:, 8 * cq:8 * cq + 8], RS[:, 16:24],
                   RS[:, 24:32], ALU.add)

        gdR = s_gd.take(16)
        dv_after_R = s_dv.v

        @block.gpsimd
        def _(g, dv_after_R=dv_after_R, gdR=gdR):
            g.wait_ge(s_dv.h, dv_after_R)
            g.dma_start(
                out=bass.AP(gscr.tensor, 0, [[NCQ * 8, 128], [8, NCQ], [1, 8]]),
                in_=GALL.rearrange("p (c f) -> p c f", c=NCQ)
            ).then_inc(s_gd.h, 16)
            g.wait_ge(s_gd.h, gdR + 16)
            pid = g.partition_id()
            src = bass.AP(gscr.tensor, pid * (32 * 8),
                          [[NCQ * 8, 128], [8, 32], [1, 8]])
            g.dma_start(out=GOWN[:].rearrange("p (c f) -> p c f", c=32),
                        in_=src).then_inc(s_go.h, 16)

        pe_a_done = {}
        pe_g_done = {}
        pe_p_done = {}

        for b in range(B):
            pe_before_x = s_pe.v
            dv_before_x = s_dv.v
            gd_x = s_gd.take(16)

            @block.gpsimd
            def _(g, b=b, pe_before_x=pe_before_x, dv_before_x=dv_before_x):
                g.wait_ge(s_pe.h, pe_before_x)
                g.wait_ge(s_dv.h, dv_before_x)
                src = bass.AP(x1o.tensor, b * 16 * 128 * 1024,
                              [[1024, 128], [128 * 1024, 16], [1, 1024]])
                g.dma_start(out=XBF[:].rearrange("p (c d) -> p c d", c=16),
                            in_=src).then_inc(s_gd.h, 16)

            pe1 = s_pe.take(16)
            dv1 = s_dv.take(17)

            @block.tensor
            def _(t, gd_x=gd_x, pe1=pe1, dv1=dv1, gdR=gdR):
                t.wait_ge(s_gd.h, gd_x + 16)
                t.wait_ge(s_go.h, 16)
                for ci in range(16):
                    if ci > 0:
                        t.wait_ge(s_dv.h, dv1 + ci)
                    for ch in range(8):
                        last = t.transpose(
                            psT[:, 128 * ch:128 * (ch + 1)],
                            XBF[:, ci * 1024 + 128 * ch:ci * 1024 + 128 * (ch + 1)],
                            IDS[:])
                    last.then_inc(s_pe.h, 1)

            @block.vector
            def _(v, pe1=pe1, dv1=dv1):
                cur = [dv1]
                op = mkop(v, s_dv, cur)
                for ci in range(16):
                    v.wait_ge(s_pe.h, pe1 + ci + 1)
                    out_ap = bass.AP(XT[:].tensor, XT[:].offset + 128 * ci,
                                     [[8 * T, 128], [T, 8], [1, 128]])
                    op("tensor_copy", out_ap,
                       psT[:].rearrange("p (c f) -> p c f", c=8))
                op("memset", ACC[:], 0.0)

            for e in range(E_EXP):
                wa = s_wa.take(16 * 8)
                wg2 = s_wg2.take(16 * 8)
                wp = s_wp.take(16 * 12)

                @block.gpsimd
                def _(g, b=b, e=e, wa=wa, wg2=wg2, wp=wp):
                    if (b, e) != (0, 0):
                        pb, pe_ = (b, e - 1) if e > 0 else (b - 1, 7)
                        g.wait_ge(s_pe.h, pe_a_done[(pb, pe_)])
                    for ich in range(8):
                        src = bass.AP(cfc.tensor,
                                      e * D * 3072 + 128 * ich * 3072,
                                      [[3072, 128], [1, 1536]])
                        g.dma_start(out=CFA[:, 1536 * ich:1536 * (ich + 1)],
                                    in_=src).then_inc(s_wa.h, 16)
                    if (b, e) != (0, 0):
                        pb, pe_ = (b, e - 1) if e > 0 else (b - 1, 7)
                        g.wait_ge(s_pe.h, pe_g_done[(pb, pe_)])
                    for ich in range(8):
                        src = bass.AP(cfc.tensor,
                                      e * D * 3072 + 128 * ich * 3072 + 1536,
                                      [[3072, 128], [1, 1536]])
                        g.dma_start(out=CFG[:, 1536 * ich:1536 * (ich + 1)],
                                    in_=src).then_inc(s_wg2.h, 16)
                    if (b, e) != (0, 0):
                        pb, pe_ = (b, e - 1) if e > 0 else (b - 1, 7)
                        g.wait_ge(s_pe.h, pe_p_done[(pb, pe_)])
                    for fc in range(12):
                        src = bass.AP(cpj.tensor,
                                      e * 1536 * D + 128 * fc * D,
                                      [[1024, 128], [1, 1024]])
                        g.dma_start(out=CPJ[:, 1024 * fc:1024 * (fc + 1)],
                                    in_=src).then_inc(s_wp.h, 16)

                pe_e = s_pe.take(8 * 10)
                sc_e = s_sc.take(8 * 10)
                dv_e = s_dv.take(8 * 10)
                dv_xt_done = dv1 + 16

                @block.tensor
                def _(t, b=b, e=e, wa=wa, wg2=wg2, wp=wp, pe_e=pe_e,
                      sc_e=sc_e, dv_e=dv_e, dv_xt_done=dv_xt_done):
                    t.wait_ge(s_dv.h, dv_xt_done + 1)
                    t.wait_ge(s_wa.h, wa + 128)
                    for tp in range(8):
                        tb0, tb1 = 2 * tp, 2 * tp + 1
                        base = pe_e + 10 * tp
                        dbase = dv_e + 10 * tp
                        scb = sc_e + 10 * tp
                        for k, (tb, S) in enumerate(((tb0, psH), (tb1, psH2))):
                            if tp > 0:
                                t.wait_ge(s_dv.h, dv_e + 10 * (tp - 1) + 3 * (k + 1))
                                t.wait_ge(s_sc.h, sc_e + 10 * (tp - 1) + 8 + 2 * k)
                            elif (b, e) != (0, 0):
                                t.wait_ge(s_dv.h, dv_e)
                                t.wait_ge(s_sc.h, sc_e)
                            for ich in range(8):
                                stat = XT[:, ich * T + 128 * tb:ich * T + 128 * (tb + 1)]
                                for fb in range(3):
                                    last = t.matmul(
                                        S[fb][:], stat,
                                        CFA[:, 1536 * ich + 512 * fb:
                                            1536 * ich + 512 * (fb + 1)],
                                        start=(ich == 0), stop=(ich == 7))
                            last.then_inc(s_pe.h, 1)
                        for k, (tb, S) in enumerate(((tb0, psH), (tb1, psH2))):
                            t.wait_ge(s_sc.h, scb + 3 * (k + 1))
                            if tp == 0 and k == 0:
                                t.wait_ge(s_wg2.h, wg2 + 128)
                            for ich in range(8):
                                stat = XT[:, ich * T + 128 * tb:ich * T + 128 * (tb + 1)]
                                for fb in range(3):
                                    last = t.matmul(
                                        S[fb][:], stat,
                                        CFG[:, 1536 * ich + 512 * fb:
                                            1536 * ich + 512 * (fb + 1)],
                                        start=(ich == 0), stop=(ich == 7))
                            last.then_inc(s_pe.h, 1)
                        for k in range(2):
                            t.wait_ge(s_dv.h, dbase + 3 * (k + 1))
                            for j in range(8):
                                last = t.transpose(psTs[k][:, 128 * j:128 * (j + 1)],
                                                   HHs[k][:, 128 * j:128 * (j + 1)],
                                                   IDS[:])
                            last.then_inc(s_pe.h, 1)
                            for j in range(4):
                                last = t.transpose(psTs2[k][:, 128 * j:128 * (j + 1)],
                                                   HHs[k][:, 1024 + 128 * j:
                                                          1024 + 128 * (j + 1)],
                                                   IDS[:])
                            last.then_inc(s_pe.h, 1)
                        for k, tb in enumerate((tb0, tb1)):
                            t.wait_ge(s_sc.h, scb + 8 + 2 * k)
                            if k == 0:
                                if tp > 0:
                                    t.wait_ge(s_dv.h, dv_e + 10 * (tp - 1) + 10)
                                elif (b, e) != (0, 0):
                                    t.wait_ge(s_dv.h, dv_e)
                                if tp == 0:
                                    t.wait_ge(s_wp.h, wp + 192)
                            else:
                                t.wait_ge(s_dv.h, dbase + 8)
                            for fc in range(12):
                                stat = HHTs[k][:, 128 * fc:128 * (fc + 1)]
                                t.matmul(psO[0][:], stat,
                                         CPJ[:, 1024 * fc:1024 * fc + 512],
                                         start=(fc == 0), stop=(fc == 11))
                                last = t.matmul(psO[1][:], stat,
                                                CPJ[:, 1024 * fc + 512:1024 * (fc + 1)],
                                                start=(fc == 0), stop=(fc == 11))
                            last.then_inc(s_pe.h, 1)
                    pe_a_done[(b, e)] = pe_e + 10 * 7 + 2
                    pe_g_done[(b, e)] = pe_e + 10 * 7 + 4
                    pe_p_done[(b, e)] = pe_e + 80

                @block.scalar
                def _(s, b=b, e=e, pe_e=pe_e, sc_e=sc_e, dv_e=dv_e):
                    for tp in range(8):
                        base = pe_e + 10 * tp
                        for k, S in enumerate((psH, psH2)):
                            s.wait_ge(s_pe.h, base + 1 + k)
                            if tp > 0:
                                s.wait_ge(s_dv.h, dv_e + 10 * (tp - 1) + 3 * (k + 1))
                            for fb in range(3):
                                s.activation(HH1s[k][:, 512 * fb:512 * (fb + 1)],
                                             S[fb][:], AF.Silu).then_inc(s_sc.h, 1)
                        for k in range(2):
                            s.wait_ge(s_pe.h, base + 6 + 2 * k)
                            s.activation(HHTs[k][:, 0:1024], psTs[k][:, 0:1024],
                                         AF.Copy).then_inc(s_sc.h, 1)
                            s.activation(HHTs[k][:, 1024:1536], psTs2[k][:, 0:512],
                                         AF.Copy).then_inc(s_sc.h, 1)

                @block.vector
                def _(v, b=b, e=e, pe_e=pe_e, dv_e=dv_e, sc_e=sc_e):
                    cur = [dv_e]
                    op = mkop(v, s_dv, cur)
                    for tp in range(8):
                        tb0, tb1 = 2 * tp, 2 * tp + 1
                        base = pe_e + 10 * tp
                        for k, S in enumerate((psH, psH2)):
                            v.wait_ge(s_pe.h, base + 3 + k)
                            for fb in range(3):
                                op("tensor_tensor", HHs[k][:, 512 * fb:512 * (fb + 1)],
                                   HH1s[k][:, 512 * fb:512 * (fb + 1)], S[fb][:],
                                   ALU.mult)
                        for k, tb in enumerate((tb0, tb1)):
                            v.wait_ge(s_pe.h, base + 9 + k)
                            gcol = GOWN[:, (b * 16 + tb) * 8 + e:
                                        (b * 16 + tb) * 8 + e + 1]
                            op("scalar_tensor_tensor",
                               ACC[:, 1024 * tb:1024 * tb + 512], psO[0][:], gcol,
                               ACC[:, 1024 * tb:1024 * tb + 512], ALU.mult, ALU.add)
                            op("scalar_tensor_tensor",
                               ACC[:, 1024 * tb + 512:1024 * (tb + 1)], psO[1][:],
                               gcol, ACC[:, 1024 * tb + 512:1024 * (tb + 1)],
                               ALU.mult, ALU.add)

            x8 = [s_x[0].take(16 * 8), s_x[1].take(16 * 8)]
            y8 = [s_y[0].take(16 * 8), s_y[1].take(16 * 8)]
            dv8 = s_dv.take(16 * 7)
            sc8 = s_sc.take(16 * 2)

            @block.sync
            def _(sync, b=b, dv8=dv8):
                for ci in range(16):
                    sync.wait_ge(s_dv.h, dv8 + 7 * (ci - 2) + 7 if ci >= 2 else dv8)
                    src = bass.AP(x1o.tensor, (b * 16 + ci) * 128 * 1024,
                                  [[1024, 128], [1, 1024]])
                    sync.dma_start(out=XRES[:, 1024 * (ci % 2):1024 * (ci % 2 + 1)],
                                   in_=src).then_inc(s_x[ci % 2].h, 16)

            @block.vector
            def _(v, b=b, x8=x8, y8=y8, sc8=sc8, dv8=dv8):
                cur = [dv8]
                op = mkop(v, s_dv, cur)
                for ci in range(16):
                    xs = XRES[:, 1024 * (ci % 2):1024 * (ci % 2 + 1)]
                    v.wait_ge(s_x[ci % 2].h, x8[ci % 2] + 16 * (ci // 2 + 1))
                    op("tensor_tensor", xs, xs,
                       ACC[:, ci * 1024:(ci + 1) * 1024], ALU.add)
                    op("reduce_sum", LNS[:, 0:1], xs, axis=AX.X, negate=True)
                    op("tensor_scalar", LNS[:, 1:2], LNS[:, 0:1], 1.0 / D, None,
                       ALU.mult)
                    op("tensor_tensor", LNS[:, 6:7], LNS[:, 1:2], LNS[:, 1:2],
                       ALU.mult)
                    v.wait_ge(s_sc.h, sc8 + 2 * ci + 1)
                    op("scalar_tensor_tensor", LNS[:, 2:3], LNS[:, 3:4], 1.0 / D,
                       LNS[:, 6:7], ALU.mult, ALU.subtract)
                    v.wait_ge(s_sc.h, sc8 + 2 * ci + 2)
                    op("reciprocal", LNS[:, 5:6], LNS[:, 4:5])
                    ob = X2O[:, 1024 * (ci % 2):1024 * (ci % 2 + 1)]
                    v.wait_ge(s_y[ci % 2].h,
                              y8[ci % 2] + 16 * (ci // 2) if ci >= 2 else y8[ci % 2])
                    op("tensor_scalar", ob, xs, LNS[:, 1:2], LNS[:, 5:6],
                       ALU.add, ALU.mult)

            @block.scalar
            def _(s, b=b, dv8=dv8, sc8=sc8):
                for ci in range(16):
                    xs = XRES[:, 1024 * (ci % 2):1024 * (ci % 2 + 1)]
                    s.wait_ge(s_dv.h, dv8 + 7 * ci + 1)
                    s.activation(SQS[:], xs, AF.Square,
                                 accum_out=LNS[:, 3:4]).then_inc(s_sc.h, 1)
                    s.wait_ge(s_dv.h, dv8 + 7 * ci + 5)
                    s.activation(LNS[:, 4:5], LNS[:, 2:3], AF.Sqrt,
                                 bias=EPS[:, 0:1]).then_inc(s_sc.h, 1)

            @block.gpsimd
            def _(g, b=b, dv8=dv8, y8=y8):
                for ci in range(16):
                    g.wait_ge(s_dv.h, dv8 + 7 * (ci + 1))
                    dst = bass.AP(x2.tensor, (b * 16 + ci) * 128 * 1024,
                                  [[1024, 128], [1, 1024]])
                    g.dma_start(
                        out=dst, in_=X2O[:, 1024 * (ci % 2):1024 * (ci % 2 + 1)]
                    ).then_inc(s_y[ci % 2].h, 16)

        @block.gpsimd
        def _(g):
            g.wait_ge(s_y[0].h, s_y[0].v)
            g.wait_ge(s_y[1].h, s_y[1].v)

    return nc



def _rope_tables():
    j = np.arange(64) % 32
    inv = (10000.0 ** (-2.0 * j / 64.0))[:, None]
    tpos = np.arange(T, dtype=np.float64)[None, :]
    ang = inv * tpos
    cos = np.cos(ang)
    sin = np.sin(ang)
    cos2 = np.tile(cos, (2, 1)).astype(BFNP)
    sin2 = np.tile(sin, (2, 1)).astype(BFNP)
    return cos2, sin2


def build_l1attn(sh=None):
    tag = "" if sh is None else "D_"
    nc = bass.Bass("TRN2", target_bir_lowering=False, debug=False) if sh is None else sh["nc"]
    x2i = (nc.dram_tensor("x2i", [NT, D], F32, kind="ExternalInput").ap()
           if sh is None else sh["x2d"])
    wqT = nc.dram_tensor("wqT", [D, D], F32, kind="ExternalInput").ap()
    wkT = nc.dram_tensor("wkT", [D, D], F32, kind="ExternalInput").ap()
    wvT = nc.dram_tensor("wvT", [D, D], F32, kind="ExternalInput").ap()
    woT = nc.dram_tensor("woT", [D, D], F32, kind="ExternalInput").ap()
    x3 = (nc.dram_tensor("x3", [NT, D], F32, kind="ExternalOutput").ap()
          if sh is None else sh["x3d"])

    ident = nc.inline_tensor(np.eye(128, dtype=BFNP), name=tag + "identbf")
    epsc = nc.inline_tensor(np.full((128, 1), 1e-5, dtype=np.float32), name=tag + "epsc")
    cos2, sin2 = _rope_tables()
    cosc = nc.inline_tensor(cos2, name=tag + "cosc")
    sinc = nc.inline_tensor(sin2, name=tag + "sinc")
    utri = np.triu(np.ones((128, 128))).astype(BFNP)
    utric = nc.inline_tensor(utri, name=tag + "utri")
    onesc = nc.inline_tensor(np.ones((128, 1), dtype=BFNP), name=tag + "onesc")

    with (ExitStack() if sh is None else contextlib.nullcontext()) as _cm:
        ctx = _cm if sh is None else sh["ctx"]
        En = ctx.enter_context
        if sh is None:
            XT = En(nc.sbuf_tensor("XT", [128, 8 * T], BF16))[:]
            XBF = En(nc.sbuf_tensor("XBF", [128, 16 * 1024], BF16))[:]
            UST = En(nc.sbuf_tensor("UST", [128, 16 * 1024], BF16))[:]
            WOS = En(nc.sbuf_tensor("WOS", [128, 8 * 1024], BF16))[:]
            COS = En(nc.sbuf_tensor("COS", [128, T], BF16))[:]
            SIN = En(nc.sbuf_tensor("SIN", [128, T], BF16))[:]
            QR = En(nc.sbuf_tensor("QR", [128, T], BF16))[:]
            KR = En(nc.sbuf_tensor("KR", [128, T], BF16))[:]
            VTF = En(nc.sbuf_tensor("VTF", [128, T], BF16))[:]
            VTM = En(nc.sbuf_tensor("VTM", [128, T], BF16))[:]
            RTMP = En(nc.sbuf_tensor("RTMP", [128, T], BF16))[:]
            TT2 = En(nc.sbuf_tensor("TT2", [128, 1024], BF16))[:]
            WQP = En(nc.sbuf_tensor("WQP", [128, 1024], BF16))[:]
            WKP = En(nc.sbuf_tensor("WKP", [128, 1024], BF16))[:]
            WVP = En(nc.sbuf_tensor("WVP", [128, 1024], BF16))[:]
            WQRP = En(nc.sbuf_tensor("WQRP", [128, 1024], BF16))[:]
            WKRP = En(nc.sbuf_tensor("WKRP", [128, 1024], BF16))[:]
            XRES = En(nc.sbuf_tensor("XRES", [128, 2 * 1024], F32))[:]
            X3O = En(nc.sbuf_tensor("X3O", [128, 2 * 1024], F32))[:]
            SQS = En(nc.sbuf_tensor("SQS", [128, 1024], BF16))[:]
            LNS = En(nc.sbuf_tensor("LNS", [128, 16], F32))[:]
            IDS = En(nc.sbuf_tensor("IDS", [128, 128], BF16))[:]
            EPS = En(nc.sbuf_tensor("EPS", [128, 1], F32))[:]
        else:
            XT = sh["B1"]
            XBF = sh["B2"]
            UST = sh["B3"][:, 0:16384]
            WOS = sh["B4"][:, 0:8192]
            COS = sh["B4"][:, 8192:10240]
            SIN = sh["B4"][:, 10240:12288]
            QR = sh["B5"][:, 0:2048]
            KR = sh["B5"][:, 2048:4096]
            VTF = sh["B5"][:, 4096:6144]
            VTM = sh["B5"][:, 6144:8192]
            RTMP = sh["B5"][:, 8192:10240]
            TT2 = sh["B5"][:, 10240:11264]
            WQP = sh["B5"][:, 11264:12288]
            WKP = sh["B5"][:, 12288:13312]
            WVP = sh["B5"][:, 13312:14336]
            WQRP = sh["B5"][:, 14336:15360]
            WKRP = sh["B5"][:, 15360:16384]
            XRES = sh["XRES"]
            X3O = sh["XOUT"]
            SQS = sh["SQS"]
            LNS = sh["LNS"]
            IDS = sh["IDS"]
            EPS = sh["EPS"]
        ATTNT = XT
        ATTN = XBF
        UTS = En(nc.sbuf_tensor(tag + "UTS", [128, 128], BF16))
        ONC = En(nc.sbuf_tensor(tag + "ONC", [128, 1], BF16))

        if sh is None:
            psPF = En(nc.psum_tensor("psPF", [128, 1024], F32))[:]
            psQKa = En(nc.psum_tensor("psQKa", [128, 512], F32))[:]
            psQKb = En(nc.psum_tensor("psQKb", [128, 512], F32))[:]
            psAV2 = [En(nc.psum_tensor(f"psAV{i}", [128, 65], F32))[:] for i in range(2)]
            psP3 = En(nc.psum_tensor("psP3", [128, 1024], F32))[:]
        else:
            PS = sh["PS"]
            psPF = PS[0]
            psQKa = PS[1][:, 0:512]
            psQKb = PS[1][:, 512:1024]
            psAV2 = [PS[2][:, 0:65], PS[2][:, 512:577]]
            psP3 = PS[3]
        psT = psP3[:, 0:512].bitcast(BF16)
        psQK2a = psP3[:, 0:512]
        psQK2b = psP3[:, 512:1024]
        qksets = [(psQKa, psQKb), (psQK2a, psQK2b)]

        if sh is None:
            s_ld = Ctr(ctx, nc, "s_ld")
            s_gd = Ctr(ctx, nc, "s_gd")
            s_pe = Ctr(ctx, nc, "s_pe")
            s_sc = Ctr(ctx, nc, "s_sc")
            s_dv = Ctr(ctx, nc, "s_dv")
            s_w = Ctr(ctx, nc, "s_w")
            s_wo = Ctr(ctx, nc, "s_wo")
            s_x = [Ctr(ctx, nc, "s_x0"), Ctr(ctx, nc, "s_x1")]
            s_y = [Ctr(ctx, nc, "s_y0"), Ctr(ctx, nc, "s_y1")]
            block = En(nc.Block())
        else:
            s_ld = sh["s_ld"]; s_gd = sh["s_gd"]; s_pe = sh["s_pe"]
            s_sc = sh["s_sc"]; s_dv = sh["s_dv"]; s_w = sh["s_w"]
            s_wo = sh["s_wo"]; s_x = sh["s_x"]; s_y = sh["s_y"]
            block = sh["block"]
        y_in0, y_in1 = s_y[0].v, s_y[1].v
        pe_in, dv_in, sc_in = s_pe.v, s_dv.v, s_sc.v

        ld0 = s_ld.take(16 * 6)

        @block.sync
        def _(sync):
            sync.wait_ge(s_pe.h, pe_in)
            sync.wait_ge(s_dv.h, dv_in)
            sync.wait_ge(s_sc.h, sc_in)
            sync.dma_start(out=IDS[:], in_=ident.ap()).then_inc(s_ld.h, 16)
            sync.dma_start(out=EPS[:], in_=epsc.ap()).then_inc(s_ld.h, 16)
            sync.dma_start(out=COS[:], in_=cosc.ap()).then_inc(s_ld.h, 16)
            sync.dma_start(out=SIN[:], in_=sinc.ap()).then_inc(s_ld.h, 16)
            sync.dma_start(out=UTS[:], in_=utric.ap()).then_inc(s_ld.h, 16)
            sync.dma_start(out=ONC[:], in_=onesc.ap()).then_inc(s_ld.h, 16)

        _D_PROJ_DONE = [s_pe.v]
        gd_wo = s_wo.take(16)

        @block.gpsimd
        def _(g):
            g.wait_ge(s_pe.h, pe_in)
            g.wait_ge(s_dv.h, dv_in)
            src = bass.AP(woT.tensor, 0, [[1024, 128], [131072, 8], [1, 1024]])
            g.dma_start(out=WOS[:].rearrange("p (i o) -> p i o", i=8),
                        in_=src).then_inc(s_wo.h, 16)

        for b in range(B):
            pe_before_x = s_pe.v
            dv_before_x = s_dv.v
            gd_x = s_gd.take(16)

            @block.gpsimd
            def _(g, b=b, pe_before_x=pe_before_x, dv_before_x=dv_before_x):
                g.wait_ge(s_pe.h, pe_before_x)
                g.wait_ge(s_dv.h, dv_before_x)
                g.wait_ge(s_y[0].h, y_in0)
                g.wait_ge(s_y[1].h, y_in1)
                src = bass.AP(x2i.tensor, b * 16 * 128 * 1024,
                              [[1024, 128], [128 * 1024, 16], [1, 1024]])
                g.dma_start(out=XBF[:].rearrange("p (c d) -> p c d", c=16),
                            in_=src).then_inc(s_gd.h, 16)

            pe1 = s_pe.take(16)
            dv1 = s_dv.take(16)

            @block.tensor
            def _(t, gd_x=gd_x, pe1=pe1, dv1=dv1):
                t.wait_ge(s_gd.h, gd_x + 16)
                t.wait_ge(s_ld.h, ld0 + 96)
                for ci in range(16):
                    if ci > 0:
                        t.wait_ge(s_dv.h, dv1 + ci)
                    for ch in range(8):
                        last = t.transpose(
                            psT[:, 128 * ch:128 * (ch + 1)],
                            XBF[:, ci * 1024 + 128 * ch:ci * 1024 + 128 * (ch + 1)],
                            IDS[:])
                    last.then_inc(s_pe.h, 1)

            @block.vector
            def _(v, pe1=pe1, dv1=dv1):
                cur = [dv1]
                op = mkop(v, s_dv, cur)
                for ci in range(16):
                    v.wait_ge(s_pe.h, pe1 + ci + 1)
                    out_ap = bass.AP(XT[:].tensor, XT[:].offset + 128 * ci,
                                     [[8 * T, 128], [T, 8], [1, 128]])
                    op("tensor_copy", out_ap,
                       psT[:].rearrange("p (c f) -> p c f", c=8))

            for hp in range(8):
                wld = s_w.take(16 * 11)
                pe_prev_pair = _D_PROJ_DONE[0]
                dv_prev_pair = s_dv.v

                @block.gpsimd
                def _(g, hp=hp, pe_prev_pair=pe_prev_pair):
                    g.wait_ge(s_pe.h, pe_prev_pair)
                    for wdram, wsb in ((wqT, WQP), (wkT, WKP), (wvT, WVP)):
                        src = bass.AP(wdram.tensor, 128 * hp,
                                      [[1024, 128], [131072, 8], [1, 128]])
                        g.dma_start(out=wsb[:].rearrange("p (i o) -> p i o", i=8),
                                    in_=src).then_inc(s_w.h, 16)
                    for wdram, wsb in ((wqT, WQRP), (wkT, WKRP)):
                        wpit = wsb[:].tensor.shape[1]
                        for half in range(2):
                            for hl in range(2):
                                src = bass.AP(
                                    wdram.tensor,
                                    128 * hp + 64 * hl + (1 - half) * 32,
                                    [[1024, 128], [131072, 8], [1, 32]])
                                dst = bass.AP(
                                    wsb[:].tensor,
                                    wsb[:].offset + 64 * hl + half * 32,
                                    [[wpit, 128], [128, 8], [1, 32]])
                                g.dma_start(out=dst, in_=src).then_inc(s_w.h, 16)
                dv_neg = s_dv.take(2)

                @block.vector
                def _(v, wld=wld, dv_neg=dv_neg):
                    cur = [dv_neg]
                    op = mkop(v, s_dv, cur)
                    v.wait_ge(s_w.h, wld + 16 * 11)
                    for wsb in (WQRP, WKRP):
                        wpit = wsb[:].tensor.shape[1]
                        neg = bass.AP(wsb[:].tensor, wsb[:].offset,
                                      [[wpit, 128], [64, 16], [1, 32]])
                        op("tensor_scalar", neg, neg, -1.0, None, ALU.mult)

                projs = [("qrot", WQRP, None), ("q", WQP, QR),
                         ("krot", WKRP, None), ("k", WKP, KR),
                         ("v", WVP, VTF)]
                pe_pf = s_pe.take(5 * 2)
                _D_PROJ_DONE[0] = pe_pf + 10
                dv_pf = s_dv.take(2 * (1 + 3 + 1 + 3 + 1))

                @block.tensor
                def _(t, b=b, hp=hp, wld=wld, pe_pf=pe_pf, dv_pf=dv_pf,
                      dv_neg=dv_neg, dv1=dv1):
                    t.wait_ge(s_dv.h, dv_neg + 2)
                    t.wait_ge(s_dv.h, dv1 + 16)
                    k = 0
                    for pi, (nm, wsb, dst) in enumerate(projs):
                        for th in range(2):
                            t.wait_ge(s_dv.h, dv_pf + _PF_EVAC_BEFORE[k])
                            for ich in range(8):
                                for half in range(2):
                                    last = t.matmul(
                                        psPF[:, 512 * (th % 1) + 512 * half:
                                             512 * (th % 1) + 512 * (half + 1)]
                                        if False else
                                        psPF[:, 512 * half:512 * (half + 1)],
                                        wsb[:, 128 * ich:128 * (ich + 1)],
                                        XT[:, ich * T + 1024 * th + 512 * half:
                                           ich * T + 1024 * th + 512 * (half + 1)],
                                        start=(ich == 0), stop=(ich == 7))
                            last.then_inc(s_pe.h, 1)
                            k += 1

                @block.vector
                def _(v, hp=hp, pe_pf=pe_pf, dv_pf=dv_pf):
                    cur = [dv_pf]
                    op = mkop(v, s_dv, cur)
                    k = 0
                    for pi, (nm, wsb, dst) in enumerate(projs):
                        for th in range(2):
                            v.wait_ge(s_pe.h, pe_pf + k + 1)
                            sl = slice(1024 * th, 1024 * (th + 1))
                            if nm in ("qrot", "krot"):
                                op("tensor_copy", RTMP[:, sl], psPF[:])
                            elif nm == "v":
                                op("tensor_copy", VTF[:, sl], psPF[:])
                            else:
                                op("tensor_tensor", dst[:, sl], psPF[:],
                                   COS[:, sl], ALU.mult)
                                op("tensor_tensor", TT2[:], RTMP[:, sl],
                                   SIN[:, sl], ALU.mult)
                                op("tensor_tensor", dst[:, sl], dst[:, sl],
                                   TT2[:], ALU.add)
                            k += 1

                pe_vt = s_pe.take(2)
                dv_vt = s_dv.take(2)

                @block.tensor
                def _(t, pe_vt=pe_vt, dv_pf=dv_pf, dv_vt=dv_vt):
                    t.wait_ge(s_dv.h, dv_pf + 18)
                    for half in range(2):
                        if half == 1:
                            t.wait_ge(s_dv.h, dv_vt + 1)
                        for j in range(8):
                            last = t.transpose(
                                psT[:, 128 * j:128 * (j + 1)],
                                VTF[:, 1024 * half + 128 * j:
                                    1024 * half + 128 * (j + 1)], IDS[:])
                        last.then_inc(s_pe.h, 1)

                @block.vector
                def _(v, pe_vt=pe_vt, dv_vt=dv_vt):
                    cur = [dv_vt]
                    op = mkop(v, s_dv, cur)
                    for half in range(2):
                        v.wait_ge(s_pe.h, pe_vt + half + 1)
                        op("tensor_copy", VTM[:, 1024 * half:1024 * (half + 1)],
                           psT[:])

                for tq in range(4):
                    nr = 4 * tq + 4
                    pe_qk = s_pe.take(nr)
                    sc_ex = s_sc.take(2 * nr)
                    dv_msk = s_dv.take(2 * 4)

                    @block.tensor
                    def _(t, hp=hp, tq=tq, nr=nr, pe_qk=pe_qk, sc_ex=sc_ex,
                          dv_vt=dv_vt):
                        t.wait_ge(s_dv.h, dv_vt + 2)
                        for r in range(nr):
                            if r > 1:
                                t.wait_ge(s_sc.h, sc_ex + 2 * (r - 1))
                            pqa, pqb = qksets[r % 2]
                            t.matmul(pqa[:], KR[0:64, 128 * r:128 * (r + 1)],
                                     QR[0:64, 512 * tq:512 * (tq + 1)],
                                     start=True, stop=True,
                                     tile_position=(0, 0))
                            t.matmul(pqb[:], KR[64:128, 128 * r:128 * (r + 1)],
                                     QR[64:128, 512 * tq:512 * (tq + 1)],
                                     start=True, stop=True,
                                     tile_position=(64, 0)).then_inc(s_pe.h, 1)

                    @block.scalar
                    def _(s, hp=hp, tq=tq, nr=nr, pe_qk=pe_qk, sc_ex=sc_ex,
                          dv_msk=dv_msk):
                        for r in range(nr):
                            s.wait_ge(s_pe.h, pe_qk + r + 1)
                            pqa, pqb = qksets[r % 2]
                            s.activation(UST[:, 1024 * r:1024 * r + 512],
                                         pqa[:], AF.Exp, scale=0.125
                                         ).then_inc(s_sc.h, 1)
                            s.activation(UST[:, 1024 * r + 512:1024 * (r + 1)],
                                         pqb[:], AF.Exp, scale=0.125
                                         ).then_inc(s_sc.h, 1)

                    @block.vector
                    def _(v, tq=tq, nr=nr, sc_ex=sc_ex, dv_msk=dv_msk):
                        cur = [dv_msk]
                        op = mkop(v, s_dv, cur)
                        for tbl in range(4):
                            r = 4 * tq + tbl
                            v.wait_ge(s_sc.h, sc_ex + 2 * (r + 1))
                            for hl in range(2):
                                sl = slice(1024 * r + 512 * hl + 128 * tbl,
                                           1024 * r + 512 * hl + 128 * (tbl + 1))
                                op("tensor_tensor", UST[:, sl], UST[:, sl],
                                   UTS[:], ALU.mult)

                    pe_av = s_pe.take(8)
                    dv_av = s_dv.take(8 * 3)

                    @block.tensor
                    def _(t, b=b, hp=hp, tq=tq, nr=nr, sc_ex=sc_ex,
                          dv_msk=dv_msk, pe_av=pe_av, dv_av=dv_av):
                        t.wait_ge(s_sc.h, sc_ex + 2 * nr)
                        t.wait_ge(s_dv.h, dv_msk + 8)
                        for tbl in range(4):
                            tb = 4 * tq + tbl
                            for hl in range(2):
                                if tbl > 0:
                                    t.wait_ge(s_dv.h,
                                              dv_av + 3 * (2 * (tbl - 1) + hl + 1))
                                psv = psAV2[hl]
                                for r in range(tb + 1):
                                    t.matmul(psv[:, 0:64],
                                             UST[:, 1024 * r + 512 * hl + 128 * tbl:
                                                 1024 * r + 512 * hl + 128 * (tbl + 1)],
                                             VTM[:, 128 * r + 64 * hl:
                                                 128 * r + 64 * (hl + 1)],
                                             start=(r == 0), stop=False,
                                             skip_group_check=True)
                                    last = t.matmul(
                                        psv[:, 64:65],
                                        UST[:, 1024 * r + 512 * hl + 128 * tbl:
                                            1024 * r + 512 * hl + 128 * (tbl + 1)],
                                        ONC[:], start=False, stop=(r == tb),
                                        skip_group_check=True)
                                last.then_inc(s_pe.h, 1)

                    @block.vector
                    def _(v, b=b, hp=hp, tq=tq, pe_av=pe_av, dv_av=dv_av):
                        cur = [dv_av]
                        op = mkop(v, s_dv, cur)
                        for tbl in range(4):
                            tb = 4 * tq + tbl
                            for hl in range(2):
                                v.wait_ge(s_pe.h, pe_av + 2 * tbl + hl + 1)
                                psv = psAV2[hl]
                                op("tensor_copy", LNS[:, 8 + hl:9 + hl],
                                   psv[:, 64:65])
                                op("reciprocal", LNS[:, 10 + hl:11 + hl],
                                   LNS[:, 8 + hl:9 + hl])
                                op("tensor_scalar",
                                   ATTN[:, tb * 1024 + 128 * hp + 64 * hl:
                                        tb * 1024 + 128 * hp + 64 * (hl + 1)],
                                   psv[:, 0:64],
                                   LNS[:, 10 + hl:11 + hl], None, ALU.mult)

            pe_at = s_pe.take(16)
            dv_at = s_dv.take(16)
            dv_all_attn = s_dv.v - 16

            @block.tensor
            def _(t, b=b, pe_at=pe_at, dv_at=dv_at, dv_all=s_dv.v - 16):
                t.wait_ge(s_dv.h, dv_all)
                for ci in range(16):
                    if ci > 0:
                        t.wait_ge(s_dv.h, dv_at + ci)
                    for ch in range(8):
                        last = t.transpose(
                            psT[:, 128 * ch:128 * (ch + 1)],
                            ATTN[:, ci * 1024 + 128 * ch:ci * 1024 + 128 * (ch + 1)],
                            IDS[:])
                    last.then_inc(s_pe.h, 1)

            @block.vector
            def _(v, pe_at=pe_at, dv_at=dv_at):
                cur = [dv_at]
                op = mkop(v, s_dv, cur)
                for ci in range(16):
                    v.wait_ge(s_pe.h, pe_at + ci + 1)
                    out_ap = bass.AP(ATTNT[:].tensor, ATTNT[:].offset + 128 * ci,
                                     [[8 * T, 128], [T, 8], [1, 128]])
                    op("tensor_copy", out_ap,
                       psT[:].rearrange("p (c f) -> p c f", c=8))

            pe_op = s_pe.take(16)
            x8 = [s_x[0].take(16 * 8), s_x[1].take(16 * 8)]
            y8 = [s_y[0].take(16 * 8), s_y[1].take(16 * 8)]
            dv8 = s_dv.take(16 * 7)
            sc8 = s_sc.take(16 * 2)

            @block.tensor
            def _(t, b=b, dv_at=dv_at, pe_op=pe_op, dv8=dv8, gd_wo=gd_wo):
                t.wait_ge(s_dv.h, dv_at + 16)
                t.wait_ge(s_wo.h, gd_wo + 16)
                for tb in range(16):
                    if tb > 0:
                        t.wait_ge(s_dv.h, dv8 + 7 * (tb - 1) + 1)
                    for ich in range(8):
                        stat = ATTNT[:, ich * T + 128 * tb:ich * T + 128 * (tb + 1)]
                        for half in range(2):
                            last = t.matmul(
                                psPF[:, 512 * half:512 * (half + 1)], stat,
                                WOS[:, 1024 * ich + 512 * half:
                                    1024 * ich + 512 * (half + 1)],
                                start=(ich == 0), stop=(ich == 7))
                    last.then_inc(s_pe.h, 1)

            @block.sync
            def _(sync, b=b, dv8=dv8):
                sync.wait_ge(s_y[0].h, y_in0)
                sync.wait_ge(s_y[1].h, y_in1)
                for ci in range(16):
                    sync.wait_ge(s_dv.h, dv8 + 7 * (ci - 2) + 7 if ci >= 2 else dv8)
                    src = bass.AP(x2i.tensor, (b * 16 + ci) * 128 * 1024,
                                  [[1024, 128], [1, 1024]])
                    sync.dma_start(out=XRES[:, 1024 * (ci % 2):1024 * (ci % 2 + 1)],
                                   in_=src).then_inc(s_x[ci % 2].h, 16)

            @block.vector
            def _(v, b=b, x8=x8, y8=y8, sc8=sc8, dv8=dv8, pe_op=pe_op):
                cur = [dv8]
                op = mkop(v, s_dv, cur)
                for ci in range(16):
                    xs = XRES[:, 1024 * (ci % 2):1024 * (ci % 2 + 1)]
                    v.wait_ge(s_x[ci % 2].h, x8[ci % 2] + 16 * (ci // 2 + 1))
                    v.wait_ge(s_pe.h, pe_op + ci + 1)
                    op("tensor_tensor", xs, xs, psPF[:], ALU.add)
                    op("reduce_sum", LNS[:, 0:1], xs, axis=AX.X, negate=True)
                    op("tensor_scalar", LNS[:, 1:2], LNS[:, 0:1], 1.0 / D, None,
                       ALU.mult)
                    op("tensor_tensor", LNS[:, 6:7], LNS[:, 1:2], LNS[:, 1:2],
                       ALU.mult)
                    v.wait_ge(s_sc.h, sc8 + 2 * ci + 1)
                    op("scalar_tensor_tensor", LNS[:, 2:3], LNS[:, 3:4], 1.0 / D,
                       LNS[:, 6:7], ALU.mult, ALU.subtract)
                    v.wait_ge(s_sc.h, sc8 + 2 * ci + 2)
                    op("reciprocal", LNS[:, 5:6], LNS[:, 4:5])
                    ob = X3O[:, 1024 * (ci % 2):1024 * (ci % 2 + 1)]
                    v.wait_ge(s_y[ci % 2].h,
                              y8[ci % 2] + 16 * (ci // 2) if ci >= 2 else y8[ci % 2])
                    op("tensor_scalar", ob, xs, LNS[:, 1:2], LNS[:, 5:6],
                       ALU.add, ALU.mult)

            @block.scalar
            def _(s, b=b, dv8=dv8, sc8=sc8):
                for ci in range(16):
                    xs = XRES[:, 1024 * (ci % 2):1024 * (ci % 2 + 1)]
                    s.wait_ge(s_dv.h, dv8 + 7 * ci + 1)
                    s.activation(SQS[:], xs, AF.Square,
                                 accum_out=LNS[:, 3:4]).then_inc(s_sc.h, 1)
                    s.wait_ge(s_dv.h, dv8 + 7 * ci + 5)
                    s.activation(LNS[:, 4:5], LNS[:, 2:3], AF.Sqrt,
                                 bias=EPS[:, 0:1]).then_inc(s_sc.h, 1)

            @block.gpsimd
            def _(g, b=b, dv8=dv8, y8=y8):
                for ci in range(16):
                    g.wait_ge(s_dv.h, dv8 + 7 * (ci + 1))
                    dst = bass.AP(x3.tensor, (b * 16 + ci) * 128 * 1024,
                                  [[1024, 128], [1, 1024]])
                    g.dma_start(
                        out=dst, in_=X3O[:, 1024 * (ci % 2):1024 * (ci % 2 + 1)]
                    ).then_inc(s_y[ci % 2].h, 16)

        @block.gpsimd
        def _(g):
            g.wait_ge(s_y[0].h, s_y[0].v)
            g.wait_ge(s_y[1].h, s_y[1].v)

    return nc


_PF_EVAC_BEFORE = {}
_k = 0
_cums = [0]
for _pi, _nm in enumerate(["qrot", "q", "krot", "k", "v"]):
    for _th in range(2):
        _cums.append(_cums[-1] + (1 if _nm in ("qrot", "krot", "v") else 3))
for _k in range(10):
    _PF_EVAC_BEFORE[_k] = 0 if _k == 0 else _cums[_k - 1] + 1



def build_l1ffn(sh=None):
    DH = 4096
    tag = "" if sh is None else "E_"
    nc = (bass.Bass("TRN2", target_bir_lowering=False, debug=False)
          if sh is None else sh["nc"])
    x3i = (nc.dram_tensor("x3i", [NT, D], F32, kind="ExternalInput").ap()
           if sh is None else sh["x3d"])
    w1T = nc.dram_tensor("w1T", [D, DH], F32, kind="ExternalInput").ap()
    w2T = nc.dram_tensor("w2T", [DH, D], F32, kind="ExternalInput").ap()
    cwL = nc.dram_tensor("cwL", [128, 96], F32, kind="ExternalInput").ap()
    cbL = nc.dram_tensor("cbL", [128, 32], F32, kind="ExternalInput").ap()
    out = nc.dram_tensor("out", [NT, D], F32, kind="ExternalOutput").ap()
    t1d = nc.dram_tensor("t1d", [32, 128, 1024], BF16, kind="Internal").ap()
    t2d = nc.dram_tensor("t2d", [DH, D], BF16, kind="Internal").ap()

    ident = nc.inline_tensor(np.eye(128, dtype=BFNP), name=tag + "identbf")
    epsc = nc.inline_tensor(np.full((128, 1), 1e-5, dtype=np.float32), name=tag + "epsc")
    onesf = nc.inline_tensor(np.ones((128, 1), dtype=np.float32), name=tag + "onesf")
    onesr = nc.inline_tensor(np.ones((1, 128), dtype=np.float32), name=tag + "onesr")

    NW = 514

    with (ExitStack() if sh is None else contextlib.nullcontext()) as _cm:
        ctx = _cm if sh is None else sh["ctx"]
        En = ctx.enter_context
        if sh is None:
            XQT = En(nc.sbuf_tensor("XQT", [128, 8 * T], BF16))[:]
            XQD = En(nc.sbuf_tensor("XQD", [128, 16 * 1024], BF16))[:]
            HFC = En(nc.sbuf_tensor("HFC", [128, 32 * 516], BF16))[:]
            SFM = En(nc.sbuf_tensor("SFM", [128, 32 * 512], BF16))[:]
            XQ2F = En(nc.sbuf_tensor("XQ2F", [128, 32 * 512], BF16))[:]
            XRES = En(nc.sbuf_tensor("XRES", [128, 2 * 1024], F32))[:]
            XOO = En(nc.sbuf_tensor("XOO", [128, 2 * 1024], F32))[:]
            SQS = En(nc.sbuf_tensor("SQS", [128, 1024], BF16))[:]
            LNS = En(nc.sbuf_tensor("LNS", [128, 8], F32))[:]
            IDS = En(nc.sbuf_tensor("IDS", [128, 128], BF16))[:]
            EPS = En(nc.sbuf_tensor("EPS", [128, 1], F32))[:]
        else:
            XQT = sh["B1"]
            XQD = sh["B2"]
            HFC = sh["B3"]
            SFM = sh["B4"][:, 0:16384]
            XQ2F = sh["B5"][:, 0:16384]
            XRES = sh["XRES"]
            XOO = sh["XOUT"]
            SQS = sh["SQS"]
            LNS = sh["LNS"][:, 0:8]
            IDS = sh["IDS"]
            EPS = sh["EPS"]
        XQ2D = XQD
        W1S = XQT[:, 0:8192].bitcast(F32)
        TRS = XQT[:, 8192:16384].bitcast(F32)
        TRN = XQD[:, 0:4096]
        TS1 = [En(nc.sbuf_tensor(tag + f"TS1_{i}", [128, 1024], BF16)) for i in range(2)]
        TS2 = [En(nc.sbuf_tensor(tag + f"TS2_{i}", [128, 1024], BF16)) for i in range(2)]
        CW = En(nc.sbuf_tensor(tag + "CW", [128, 96], F32))
        CB = En(nc.sbuf_tensor(tag + "CB", [128, 32], F32))
        AB = En(nc.sbuf_tensor(tag + "AB", [128, 40], F32))
        SC = En(nc.sbuf_tensor(tag + "SC", [128, 8], F32))
        SCT = En(nc.sbuf_tensor(tag + "SCT", [1, 8], F32))
        DQ = En(nc.sbuf_tensor(tag + "DQ", [128, 64], F32))
        DQ2 = En(nc.sbuf_tensor(tag + "DQ2", [128, 8], F32))
        ONF = En(nc.sbuf_tensor(tag + "ONF", [128, 1], F32))
        ON1 = En(nc.sbuf_tensor(tag + "ON1", [1, 128], F32))

        if sh is None:
            psHa = En(nc.psum_tensor("psHa", [128, 258], F32))[:]
            psHb = En(nc.psum_tensor("psHb", [128, 258], F32))[:]
            psT = En(nc.psum_tensor("psT", [128, 1024], BF16))[:]
            psO = [En(nc.psum_tensor(f"psO{i}", [128, 1024], F32))[:] for i in range(2)]
        else:
            PS = sh["PS"]
            psHa = PS[0][:, 0:258]
            psHb = PS[0][:, 512:770]
            psT = PS[1][:, 0:512].bitcast(BF16)
            psO = [PS[2], PS[3]]
        psHs = [(psHa[:], psHb[:]),
                (psO[0][:, 0:258], psO[0][:, 512:770])]
        psS = psHa[0:1, 0:1]
        psBC = psHb[:, 0:1]

        if sh is None:
            s_ld = Ctr(ctx, nc, "s_ld")
            s_gd = Ctr(ctx, nc, "s_gd")
            s_pe = Ctr(ctx, nc, "s_pe")
            s_sc = Ctr(ctx, nc, "s_sc")
            s_dv = Ctr(ctx, nc, "s_dv")
            s_w1 = Ctr(ctx, nc, "s_w1")
            s_w2 = Ctr(ctx, nc, "s_w2")
            s_ts = Ctr(ctx, nc, "s_ts")
            s_t2 = Ctr(ctx, nc, "s_t2")
            s_x = [Ctr(ctx, nc, "s_x0"), Ctr(ctx, nc, "s_x1")]
            s_q = [Ctr(ctx, nc, "s_q0"), Ctr(ctx, nc, "s_q1")]
            s_y = [Ctr(ctx, nc, "s_y0"), Ctr(ctx, nc, "s_y1")]
            block = En(nc.Block())
        else:
            s_ld = sh["s_ld"]; s_gd = sh["s_gd"]; s_pe = sh["s_pe"]
            s_sc = sh["s_sc"]; s_dv = sh["s_dv"]; s_w1 = sh["s_w1"]
            s_w2 = sh["s_w2"]; s_ts = sh["s_ts"]; s_t2 = sh["s_t2"]
            s_x = sh["s_x"]; s_q = sh["s_q"]; s_y = sh["s_y"]
            block = sh["block"]
        y_in0, y_in1 = s_y[0].v, s_y[1].v
        pe_in, dv_in, sc_in = s_pe.v, s_dv.v, s_sc.v

        ld0 = s_ld.take(16 * 6)

        @block.sync
        def _(sync):
            sync.wait_ge(s_pe.h, pe_in)
            sync.wait_ge(s_dv.h, dv_in)
            sync.wait_ge(s_sc.h, sc_in)
            sync.dma_start(out=IDS[:], in_=ident.ap()).then_inc(s_ld.h, 16)
            sync.dma_start(out=EPS[:], in_=epsc.ap()).then_inc(s_ld.h, 16)
            sync.dma_start(out=ONF[:], in_=onesf.ap()).then_inc(s_ld.h, 16)
            sync.dma_start(out=ON1[:], in_=onesr.ap()).then_inc(s_ld.h, 16)
            sync.dma_start(out=CW[:], in_=cwL).then_inc(s_ld.h, 16)
            sync.dma_start(out=CB[:], in_=cbL).then_inc(s_ld.h, 16)

        for wi, (wdram, nt, tdram, scol, mcol) in enumerate(
                [(w1T, 8, t1d, 0, 1), (w2T, 32, t2d, 2, 3)]):
            wid = DH if wi == 0 else 1024
            w1l = s_w1.take(16 * nt)
            dvA = s_dv.take(nt + 3)
            peA = s_pe.take(1)
            dvB = s_dv.take(6)
            scA = s_sc.take(0)

            @block.sync
            def _(sync, wdram=wdram, nt=nt, wid=wid, w1l=w1l, dvA=dvA):
                sync.wait_ge(s_dv.h, dvA)
                sync.wait_ge(s_pe.h, pe_in)
                for i in range(nt):
                    if i >= 1:
                        sync.wait_ge(s_dv.h, dvA + i)
                    src = bass.AP(wdram.tensor, i * 128 * wid,
                                  [[wid, 128], [1, wid]])
                    sync.dma_start(out=W1S[:, 0:wid], in_=src
                                   ).then_inc(s_w1.h, 16)

            @block.vector
            def _(v, nt=nt, wid=wid, w1l=w1l, dvA=dvA):
                cur = [dvA]
                op = mkop(v, s_dv, cur)
                for i in range(nt):
                    v.wait_ge(s_w1.h, w1l + 16 * (i + 1))
                    op("tensor_reduce", AB[:, i:i + 1], W1S[:, 0:wid],
                       axis=AX.X, op=ALU.add, apply_absolute_value=True)
                op("tensor_reduce", AB[:, 36:37], AB[:, 0:nt], axis=AX.X,
                   op=ALU.add)
                op("tensor_copy", AB[:, 37:38], AB[:, 36:37])
                op("tensor_copy", AB[:, 38:39], AB[:, 36:37])

            @block.tensor
            def _(t, dvA=dvA, peA=peA, nt=nt):
                t.wait_ge(s_dv.h, dvA + nt + 3)
                t.wait_ge(s_ld.h, ld0 + 96)
                t.matmul(psS, ONF[:], AB[:, 36:37], start=True, stop=True
                         ).then_inc(s_pe.h, 1)

            @block.vector
            def _(v, peA=peA, dvB=dvB, wi=wi, scol=scol, mcol=mcol, wid=wid,
                  nt=nt):
                cur = [dvB]
                op = mkop(v, s_dv, cur)
                v.wait_ge(s_pe.h, peA + 1)
                op("tensor_copy", SCT[:, 0:1], psS)
                op("tensor_scalar", SCT[:, 1:2], SCT[:, 0:1],
                   1.0 / (nt * 128 * wid), None, ALU.mult)
                op("tensor_scalar", SCT[:, 1:2], SCT[:, 1:2], 1e-5, None,
                   ALU.max)
                op("reciprocal", SCT[:, 2:3], SCT[:, 1:2])
                op("tensor_copy", SCT[:, 4:5], SCT[:, 2:3])
                op("tensor_copy", SCT[:, 5:6], SCT[:, 1:2])

            peB = s_pe.take(1)
            dvC = s_dv.take(1)

            @block.tensor
            def _(t, dvB=dvB, peB=peB):
                t.wait_ge(s_dv.h, dvB + 6)
                t.matmul(psBC.tensor_or_self() if False else psHb[:, 0:2],
                         ON1[:], SCT[:, 4:6], start=True, stop=True
                         ).then_inc(s_pe.h, 1)

            @block.vector
            def _(v, peB=peB, dvC=dvC, scol=scol):
                cur = [dvC]
                op = mkop(v, s_dv, cur)
                v.wait_ge(s_pe.h, peB + 1)
                op("tensor_copy", SC[:, scol:scol + 2], psHb[:, 0:2])

            w2l = s_w1.take(16 * nt)
            dvD = s_dv.take(nt * 3)
            gdT = s_gd.take(16 * nt)

            @block.sync
            def _(sync, wdram=wdram, nt=nt, wid=wid, w2l=w2l, dvD=dvD,
                  dvC=dvC):
                for i in range(nt):
                    sync.wait_ge(s_dv.h, dvD + 3 * i if i >= 1 else dvC + 1)
                    src = bass.AP(wdram.tensor, i * 128 * wid,
                                  [[wid, 128], [1, wid]])
                    sync.dma_start(out=W1S[:, 0:wid], in_=src
                                   ).then_inc(s_w1.h, 16)

            @block.vector
            def _(v, nt=nt, wid=wid, w2l=w2l, dvD=dvD, scol=scol, gdT=gdT):
                cur = [dvD]
                op = mkop(v, s_dv, cur)
                v.wait_ge(s_gd.h, gdT)
                for i in range(nt):
                    v.wait_ge(s_w1.h, w2l + 16 * (i + 1))
                    if i >= 1:
                        v.wait_ge(s_gd.h, gdT + 16 * i)
                    op("tensor_scalar", TRS[:, 0:wid], W1S[:, 0:wid],
                       SC[:, scol:scol + 1], MAGIC, ALU.mult, ALU.add)
                    op("tensor_scalar", TRS[:, 0:wid], TRS[:, 0:wid],
                       -MAGIC, 1.0, ALU.add, ALU.min)
                    op("tensor_scalar", TRN[:, 0:wid], TRS[:, 0:wid],
                       -1.0, None, ALU.max)

            @block.gpsimd
            def _(g, nt=nt, wid=wid, tdram=tdram, dvD=dvD, gdT=gdT, wi=wi):
                for i in range(nt):
                    g.wait_ge(s_dv.h, dvD + 3 * (i + 1))
                    if wi == 0:
                        dst = bass.AP(tdram.tensor, i * 128,
                                      [[1024, 128], [128 * 1024, 32], [1, 128]])
                        g.dma_start(out=dst,
                                    in_=TRN[:, 0:wid].rearrange(
                                        "p (o c) -> p o c", o=32)
                                    ).then_inc(s_gd.h, 16)
                    else:
                        dst = bass.AP(tdram.tensor, i * 128 * wid,
                                      [[wid, 128], [1, wid]])
                        g.dma_start(out=dst, in_=TRN[:, 0:wid]
                                    ).then_inc(s_gd.h, 16)

        dvCW = s_dv.take(2)

        @block.vector
        def _(v, dvCW=dvCW):
            cur = [dvCW]
            op = mkop(v, s_dv, cur)
            v.wait_ge(s_ld.h, ld0 + 96)
            op("tensor_scalar", CW[:], CW[:], SC[:, 1:2], None, ALU.mult)
            op("tensor_copy", AB[:, 39:40], SC[:, 3:4])

        for b in range(B):
            xq = [s_q[0].take(16 * 8), s_q[1].take(16 * 8)]
            dvQ = s_dv.take(16 * 7)

            @block.sync
            def _(sync, b=b, xq=xq, dvQ=dvQ):
                sync.wait_ge(s_y[0].h, y_in0)
                sync.wait_ge(s_y[1].h, y_in1)
                for ci in range(16):
                    sync.wait_ge(s_dv.h, dvQ + 7 * (ci - 2) + 7 if ci >= 2 else dvQ)
                    src = bass.AP(x3i.tensor, (b * 16 + ci) * 128 * 1024,
                                  [[1024, 128], [1, 1024]])
                    sync.dma_start(out=XRES[:, 1024 * (ci % 2):1024 * (ci % 2 + 1)],
                                   in_=src).then_inc(s_q[ci % 2].h, 16)

            @block.vector
            def _(v, b=b, xq=xq, dvQ=dvQ, ypv0=s_y[0].v, ypv1=s_y[1].v,
                  pepv=s_pe.v, gdpv=s_gd.v):
                cur = [dvQ]
                op = mkop(v, s_dv, cur)
                v.wait_ge(s_y[0].h, ypv0)
                v.wait_ge(s_y[1].h, ypv1)
                v.wait_ge(s_pe.h, pepv)
                v.wait_ge(s_gd.h, gdpv)
                for ci in range(16):
                    xs = XRES[:, 1024 * (ci % 2):1024 * (ci % 2 + 1)]
                    v.wait_ge(s_q[ci % 2].h, xq[ci % 2] + 16 * (ci // 2 + 1))
                    op("reduce_max", DQ[:, 4 * ci:4 * ci + 1], xs, axis=AX.X,
                       apply_absolute_value=True)
                    op("tensor_scalar", DQ[:, 4 * ci:4 * ci + 1],
                       DQ[:, 4 * ci:4 * ci + 1], 1e-5, None, ALU.max)
                    op("reciprocal", DQ[:, 4 * ci + 1:4 * ci + 2],
                       DQ[:, 4 * ci:4 * ci + 1])
                    op("tensor_scalar", DQ[:, 4 * ci + 1:4 * ci + 2],
                       DQ[:, 4 * ci + 1:4 * ci + 2], 127.0, None, ALU.mult)
                    op("tensor_scalar", DQ[:, 4 * ci + 2:4 * ci + 3],
                       DQ[:, 4 * ci:4 * ci + 1], 1.0 / 127.0, None, ALU.mult)
                    op("tensor_scalar", XOO[:, 0:1024], xs,
                       DQ[:, 4 * ci + 1:4 * ci + 2], MAGIC, ALU.mult, ALU.add)
                    op("tensor_scalar", XQD[:, ci * 1024:(ci + 1) * 1024],
                       XOO[:, 0:1024],
                       -MAGIC, DQ[:, 4 * ci + 2:4 * ci + 3], ALU.add, ALU.mult)

            pe1 = s_pe.take(16)
            dv1 = s_dv.take(16)

            @block.tensor
            def _(t, b=b, dvQ=dvQ, pe1=pe1, dv1=dv1):
                t.wait_ge(s_dv.h, dvQ + 112)
                for ci in range(16):
                    if ci > 0:
                        t.wait_ge(s_dv.h, dv1 + ci)
                    for ch in range(8):
                        last = t.transpose(
                            psT[:, 128 * ch:128 * (ch + 1)],
                            XQD[:, ci * 1024 + 128 * ch:ci * 1024 + 128 * (ch + 1)],
                            IDS[:])
                    last.then_inc(s_pe.h, 1)

            @block.vector
            def _(v, pe1=pe1, dv1=dv1):
                cur = [dv1]
                op = mkop(v, s_dv, cur)
                for ci in range(16):
                    v.wait_ge(s_pe.h, pe1 + ci + 1)
                    out_ap = bass.AP(XQT[:].tensor, XQT[:].offset + 128 * ci,
                                     [[8 * T, 128], [T, 8], [1, 128]])
                    op("tensor_copy", out_ap,
                       psT[:].rearrange("p (c f) -> p c f", c=8))

            for tt in range(4):
                t0 = 512 * tt
                lo = max(t0 - 1, 0)
                hi = min(t0 + 513, 2048)
                wa = lo
                wb = t0 + 255
                na = 258 if tt > 0 else 257
                nb = 258 if tt < 3 else 257
                offa = 0 if tt > 0 else 1
                offb = 256

                tl1 = s_ts.take(16 * 32)
                pe_m1 = s_pe.take(32 * 2)
                dv_m1 = s_dv.take(32 * (2 + (1 if tt == 0 or tt == 3 else 0) + 3))
                sc_m1 = s_sc.take(32)
                nEv = 2 + (1 if tt in (0, 3) else 0) + 3

                @block.gpsimd
                def _(g, b=b, tt=tt, tl1=tl1, pe_m1=pe_m1):
                    for och in range(32):
                        if och >= 2:
                            g.wait_ge(s_pe.h, pe_m1 + 2 * (och - 1))
                        src = bass.AP(t1d.tensor, och * 128 * 1024,
                                      [[1024, 128], [1, 1024]])
                        g.dma_start(out=TS1[och % 2][:], in_=src
                                    ).then_inc(s_ts.h, 16)

                @block.tensor
                def _(t, b=b, tt=tt, tl1=tl1, pe_m1=pe_m1, dv_m1=dv_m1,
                      dv1=dv1, wa=wa, wb=wb, na=na, nb=nb, nEv=nEv):
                    t.wait_ge(s_dv.h, dv1 + 16)
                    for och in range(32):
                        t.wait_ge(s_ts.h, tl1 + 16 * (och + 1))
                        if och > 1:
                            t.wait_ge(s_dv.h, dv_m1 + nEv * (och - 2) + 2)
                        pa, pb = psHs[och % 2]
                        for ich in range(8):
                            stat = TS1[och % 2][:, 128 * ich:128 * (ich + 1)]
                            t.matmul(pa[:, 0:na], stat,
                                     XQT[:, ich * T + wa:ich * T + wa + na],
                                     start=(ich == 0), stop=(ich == 7),
                                     skip_group_check=True)
                            last = t.matmul(pb[:, 0:nb], stat,
                                            XQT[:, ich * T + wb:ich * T + wb + nb],
                                            start=(ich == 0), stop=(ich == 7),
                                            skip_group_check=True)
                        last.then_inc(s_pe.h, 2)

                @block.vector
                def _(v, b=b, tt=tt, pe_m1=pe_m1, dv_m1=dv_m1, sc_m1=sc_m1,
                      na=na, nb=nb, offa=offa, nEv=nEv):
                    cur = [dv_m1]
                    op = mkop(v, s_dv, cur)
                    for och in range(32):
                        v.wait_ge(s_pe.h, pe_m1 + 2 * (och + 1))
                        base = 516 * och
                        pa, pb = psHs[och % 2]
                        op("tensor_copy", HFC[:, base + offa:base + offa + na],
                           pa[:, 0:na])
                        op("tensor_copy", HFC[:, base + 256:base + 256 + nb],
                           pb[:, 0:nb])
                        if tt == 0:
                            op("memset", HFC[:, base:base + 1], 0.0)
                        elif tt == 3:
                            op("memset", HFC[:, base + 513:base + 514], 0.0)
                        op("tensor_scalar", SFM[:, 512 * och:512 * (och + 1)],
                           HFC[:, base:base + 512], CW[:, 3 * och:3 * och + 1],
                           None, ALU.mult)
                        op("scalar_tensor_tensor",
                           SFM[:, 512 * och:512 * (och + 1)],
                           HFC[:, base + 1:base + 513],
                           CW[:, 3 * och + 1:3 * och + 2],
                           SFM[:, 512 * och:512 * (och + 1)], ALU.mult, ALU.add)
                        op("scalar_tensor_tensor",
                           SFM[:, 512 * och:512 * (och + 1)],
                           HFC[:, base + 2:base + 514],
                           CW[:, 3 * och + 2:3 * och + 3],
                           SFM[:, 512 * och:512 * (och + 1)], ALU.mult, ALU.add)

                @block.scalar
                def _(s, b=b, tt=tt, dv_m1=dv_m1, sc_m1=sc_m1, nEv=nEv):
                    for och in range(32):
                        s.wait_ge(s_dv.h, dv_m1 + nEv * (och + 1))
                        s.activation(SFM[:, 512 * och:512 * (och + 1)],
                                     SFM[:, 512 * och:512 * (och + 1)], AF.Silu,
                                     bias=CB[:, och:och + 1]).then_inc(s_sc.h, 1)

                pe_t1 = s_pe.take(16)
                dv_t1 = s_dv.take(16)

                @block.tensor
                def _(t, sc_m1=sc_m1, pe_t1=pe_t1, dv_t1=dv_t1):
                    t.wait_ge(s_sc.h, sc_m1 + 32)
                    k = 0
                    for tb4 in range(4):
                        for part in range(4):
                            if k > 0:
                                t.wait_ge(s_dv.h, dv_t1 + k)
                            for j in range(8):
                                och = part * 8 + j
                                last = t.transpose(
                                    psT[:, 128 * j:128 * (j + 1)],
                                    SFM[:, 512 * och + 128 * tb4:
                                        512 * och + 128 * (tb4 + 1)], IDS[:])
                            last.then_inc(s_pe.h, 1)
                            k += 1

                @block.vector
                def _(v, pe_t1=pe_t1, dv_t1=dv_t1):
                    cur = [dv_t1]
                    op = mkop(v, s_dv, cur)
                    k = 0
                    for tb4 in range(4):
                        for part in range(4):
                            v.wait_ge(s_pe.h, pe_t1 + k + 1)
                            op("tensor_copy",
                               XQ2D[:, 4096 * tb4 + 1024 * part:
                                    4096 * tb4 + 1024 * (part + 1)], psT[:])
                            k += 1

                dv_q2 = s_dv.take(4 * 9)

                @block.vector
                def _(v, dv_q2=dv_q2, dv_t1=dv_t1):
                    cur = [dv_q2]
                    op = mkop(v, s_dv, cur)
                    v.wait_ge(s_dv.h, dv_t1 + 16)
                    for tb4 in range(4):
                        xs = XQ2D[:, 4096 * tb4:4096 * (tb4 + 1)]
                        op("reduce_max", DQ2[:, 0:1], xs, axis=AX.X,
                           apply_absolute_value=True)
                        op("tensor_scalar", DQ2[:, 0:1], DQ2[:, 0:1], 1e-5,
                           None, ALU.max)
                        op("reciprocal", DQ2[:, 1:2], DQ2[:, 0:1])
                        op("tensor_scalar", DQ2[:, 1:2], DQ2[:, 1:2], 127.0,
                           None, ALU.mult)
                        op("tensor_scalar", DQ2[:, 2 + tb4:3 + tb4], DQ2[:, 0:1],
                           1.0 / 127.0, None, ALU.mult)
                        for hf in range(2):
                            xh = XQ2D[:, 4096 * tb4 + 2048 * hf:
                                      4096 * tb4 + 2048 * (hf + 1)]
                            op("tensor_scalar", XRES[:], xh, DQ2[:, 1:2],
                               MAGIC, ALU.mult, ALU.add)
                            op("tensor_scalar", xh, XRES[:], -MAGIC, None,
                               ALU.add)

                pe_t2 = s_pe.take(16)
                dv_t2 = s_dv.take(16)

                @block.tensor
                def _(t, dv_q2=dv_q2, pe_t2=pe_t2, dv_t2=dv_t2):
                    t.wait_ge(s_dv.h, dv_q2 + 36)
                    k = 0
                    for part in range(4):
                        for tb4 in range(4):
                            if k > 0:
                                t.wait_ge(s_dv.h, dv_t2 + k)
                            for j in range(8):
                                hch = part * 8 + j
                                last = t.transpose(
                                    psT[:, 128 * j:128 * (j + 1)],
                                    XQ2D[:, 4096 * tb4 + 128 * hch:
                                        4096 * tb4 + 128 * (hch + 1)], IDS[:])
                            last.then_inc(s_pe.h, 1)
                            k += 1

                @block.vector
                def _(v, pe_t2=pe_t2, dv_t2=dv_t2):
                    cur = [dv_t2]
                    op = mkop(v, s_dv, cur)
                    k = 0
                    for part in range(4):
                        for tb4 in range(4):
                            v.wait_ge(s_pe.h, pe_t2 + k + 1)
                            out_ap = bass.AP(
                                XQ2F[:].tensor,
                                XQ2F[:].offset + 512 * 8 * part + 128 * tb4,
                                [[32 * 512, 128], [512, 8], [1, 128]])
                            op("tensor_copy", out_ap,
                               psT[:].rearrange("p (c f) -> p c f", c=8))
                            k += 1

                for tt2 in range(2):
                    tl2 = s_ts2 = s_t2.take(16 * 32)
                    pe_m2 = s_pe.take(32)
                    x8 = [s_x[0].take(16), s_x[1].take(16)]
                    y8 = [s_y[0].take(16), s_y[1].take(16)]
                    dv8 = s_dv.take(2 * 8)
                    sc8 = s_sc.take(2 * 2)

                    @block.gpsimd
                    def _(g, b=b, tt=tt, tt2=tt2, tl2=tl2, pe_m2=pe_m2):
                        for hch in range(32):
                            if hch >= 2:
                                g.wait_ge(s_pe.h, pe_m2 + hch - 1)
                            src = bass.AP(t2d.tensor, 128 * hch * 1024,
                                          [[1024, 128], [1, 1024]])
                            g.dma_start(out=TS2[hch % 2][:], in_=src
                                        ).then_inc(s_t2.h, 16)

                    @block.tensor
                    def _(t, b=b, tt=tt, tt2=tt2, tl2=tl2, pe_m2=pe_m2,
                          dv_t2=dv_t2, dv8=dv8):
                        t.wait_ge(s_dv.h, dv_t2 + 16)
                        if True:
                            t.wait_ge(s_dv.h, dv8)
                        for hch in range(32):
                            t.wait_ge(s_t2.h, tl2 + 16 * (hch + 1))
                            for tbl in range(2):
                                stat = XQ2F[:, 512 * hch + 256 * tt2 + 128 * tbl:
                                            512 * hch + 256 * tt2 + 128 * (tbl + 1)]
                                t.matmul(psO[tbl][:, 0:512], stat,
                                         TS2[hch % 2][:, 0:512],
                                         start=(hch == 0), stop=(hch == 31))
                                last = t.matmul(psO[tbl][:, 512:1024], stat,
                                                TS2[hch % 2][:, 512:1024],
                                                start=(hch == 0), stop=(hch == 31))
                            last.then_inc(s_pe.h, 1)

                    @block.sync
                    def _(sync, b=b, tt=tt, tt2=tt2, x8=x8, dv8=dv8):
                        for tbl in range(2):
                            ci = 4 * tt + 2 * tt2 + tbl
                            sync.wait_ge(s_dv.h, dv8)
                            src = bass.AP(x3i.tensor,
                                          (b * 16 + ci) * 128 * 1024,
                                          [[1024, 128], [1, 1024]])
                            sync.dma_start(
                                out=XRES[:, 1024 * tbl:1024 * (tbl + 1)],
                                in_=src).then_inc(s_x[tbl].h, 16)

                    @block.vector
                    def _(v, b=b, tt=tt, tt2=tt2, x8=x8, y8=y8, dv8=dv8,
                          sc8=sc8, pe_m2=pe_m2):
                        cur = [dv8]
                        op = mkop(v, s_dv, cur)
                        for tbl in range(2):
                            ci = 4 * tt + 2 * tt2 + tbl
                            tb4 = 2 * tt2 + tbl
                            xs = XRES[:, 1024 * tbl:1024 * (tbl + 1)]
                            v.wait_ge(s_x[tbl].h, x8[tbl] + 16)
                            v.wait_ge(s_pe.h, pe_m2 + 32)
                            op("tensor_tensor", LNS[:, 7:8],
                               DQ2[:, 2 + tb4:3 + tb4], SC[:, 3:4], ALU.mult)
                            op("scalar_tensor_tensor", xs, psO[tbl][:],
                               LNS[:, 7:8], xs, ALU.mult, ALU.add)
                            op("reduce_sum", LNS[:, 0:1], xs, axis=AX.X,
                               negate=True)
                            op("tensor_scalar", LNS[:, 1:2], LNS[:, 0:1],
                               1.0 / D, None, ALU.mult)
                            op("tensor_tensor", LNS[:, 6:7], LNS[:, 1:2],
                               LNS[:, 1:2], ALU.mult)
                            v.wait_ge(s_sc.h, sc8 + 2 * tbl + 1)
                            op("scalar_tensor_tensor", LNS[:, 2:3], LNS[:, 3:4],
                               1.0 / D, LNS[:, 6:7], ALU.mult, ALU.subtract)
                            v.wait_ge(s_sc.h, sc8 + 2 * tbl + 2)
                            op("reciprocal", LNS[:, 5:6], LNS[:, 4:5])
                            ob = XOO[:, 1024 * tbl:1024 * (tbl + 1)]
                            v.wait_ge(s_y[tbl].h, y8[tbl])
                            op("tensor_scalar", ob, xs, LNS[:, 1:2],
                               LNS[:, 5:6], ALU.add, ALU.mult)

                    @block.scalar
                    def _(s, b=b, dv8=dv8, sc8=sc8):
                        for tbl in range(2):
                            xs = XRES[:, 1024 * tbl:1024 * (tbl + 1)]
                            s.wait_ge(s_dv.h, dv8 + 8 * tbl + 2)
                            s.activation(SQS[:], xs, AF.Square,
                                         accum_out=LNS[:, 3:4]).then_inc(s_sc.h, 1)
                            s.wait_ge(s_dv.h, dv8 + 8 * tbl + 6)
                            s.activation(LNS[:, 4:5], LNS[:, 2:3], AF.Sqrt,
                                         bias=EPS[:, 0:1]).then_inc(s_sc.h, 1)

                    @block.gpsimd
                    def _(g, b=b, tt=tt, tt2=tt2, dv8=dv8, y8=y8):
                        for tbl in range(2):
                            ci = 4 * tt + 2 * tt2 + tbl
                            g.wait_ge(s_dv.h, dv8 + 8 * (tbl + 1))
                            dst = bass.AP(out.tensor, (b * 16 + ci) * 128 * 1024,
                                          [[1024, 128], [1, 1024]])
                            g.dma_start(
                                out=dst,
                                in_=XOO[:, 1024 * tbl:1024 * (tbl + 1)]
                            ).then_inc(s_y[tbl].h, 16)

        @block.gpsimd
        def _(g):
            g.wait_ge(s_y[0].h, s_y[0].v)
            g.wait_ge(s_y[1].h, s_y[1].v)

    return nc



def build_cde():
    nc = bass.Bass("TRN2", target_bir_lowering=False, debug=False)
    x2d = nc.dram_tensor("x2d", [NT, D], F32, kind="Internal").ap()
    x3d = nc.dram_tensor("x3d", [NT, D], F32, kind="Internal").ap()

    with ExitStack() as ctx:
        En = ctx.enter_context
        sh = {"nc": nc, "ctx": ctx, "x2d": x2d, "x3d": x3d}
        sh["B1"] = En(nc.sbuf_tensor("B1", [128, 16384], BF16))[:]
        sh["B2"] = En(nc.sbuf_tensor("B2", [128, 16384], BF16))[:]
        sh["B3"] = En(nc.sbuf_tensor("B3", [128, 16512], BF16))[:]
        sh["B4"] = En(nc.sbuf_tensor("B4", [128, 16384], BF16))[:]
        sh["B5"] = En(nc.sbuf_tensor("B5", [128, 16384], BF16))[:]
        sh["PS"] = [En(nc.psum_tensor(f"PS{i}", [128, 1024], F32))[:]
                    for i in range(4)]
        sh["XRES"] = En(nc.sbuf_tensor("XRES", [128, 2048], F32))[:]
        sh["XOUT"] = En(nc.sbuf_tensor("XOUT", [128, 2048], F32))[:]
        sh["SQS"] = En(nc.sbuf_tensor("SQS", [128, 1024], BF16))[:]
        sh["LNS"] = En(nc.sbuf_tensor("LNS", [128, 16], F32))[:]
        sh["IDS"] = En(nc.sbuf_tensor("IDS", [128, 128], BF16))[:]
        sh["EPS"] = En(nc.sbuf_tensor("EPS", [128, 1], F32))[:]
        for nm in ["s_ld", "s_gd", "s_pe", "s_sc", "s_dv", "s_wa", "s_wg2",
                   "s_wp", "s_gs", "s_go", "s_w", "s_wo", "s_w1", "s_w2",
                   "s_ts", "s_t2"]:
            sh[nm] = Ctr(ctx, nc, nm)
        sh["s_x"] = [Ctr(ctx, nc, "s_x0"), Ctr(ctx, nc, "s_x1")]
        sh["s_y"] = [Ctr(ctx, nc, "s_y0"), Ctr(ctx, nc, "s_y1")]
        sh["s_q"] = [Ctr(ctx, nc, "s_q0"), Ctr(ctx, nc, "s_q1")]
        sh["block"] = En(nc.Block())

        build_moe(sh)
        build_l1attn(sh)
        build_l1ffn(sh)

    return nc



_NC_CACHE = {}


def _get_nc(name, builder):
    if name not in _NC_CACHE:
        _NC_CACHE[name] = builder()
    return _NC_CACHE[name]


def kernel(**inputs):
    cores = list(range(NCORES))
    x0_full = np.ascontiguousarray(
        inputs["L0_attn_x"].reshape(16 * T, D).astype(np.float32, copy=False))
    wg = np.ascontiguousarray(inputs["L0_ffn_w_gate"].astype(np.float32, copy=False))
    cfc = np.ascontiguousarray(inputs["L0_ffn_c_fc"].astype(np.float32, copy=False))
    cpj = np.ascontiguousarray(inputs["L0_ffn_c_proj"].astype(np.float32, copy=False))
    wqT = np.ascontiguousarray(inputs["L1_attn_Wq"].astype(np.float32, copy=False).T)
    wkT = np.ascontiguousarray(inputs["L1_attn_Wk"].astype(np.float32, copy=False).T)
    wvT = np.ascontiguousarray(inputs["L1_attn_Wv"].astype(np.float32, copy=False).T)
    woT = np.ascontiguousarray(inputs["L1_attn_Wo"].astype(np.float32, copy=False).T)
    w1T = np.ascontiguousarray(inputs["L1_ffn_w1"].astype(np.float32, copy=False).T)
    w2T = np.ascontiguousarray(inputs["L1_ffn_w2"].astype(np.float32, copy=False).T)
    cw = inputs["L1_ffn_conv_w"].astype(np.float32, copy=False).reshape(4096, 3)
    cwL = np.ascontiguousarray(cw.reshape(32, 128, 3).transpose(1, 0, 2).reshape(128, 96))
    cbL = np.ascontiguousarray(
        inputs["L1_ffn_conv_b"].astype(np.float32, copy=False).reshape(32, 128).T)

    ncA = _get_nc("A", build_l0attn)
    resA = run_bass_kernel_spmd(
        ncA, [{"x0": np.ascontiguousarray(x0_full[c * NT:(c + 1) * NT]),
               "wg": wg} for c in cores], core_ids=cores)
    metaF = np.concatenate([resA.results[c]["meta"] for c in cores], axis=0)
    metaF = np.ascontiguousarray(metaF)
    x1d = [np.ascontiguousarray(resA.results[c]["x1"]) for c in cores]

    ncCDE = _get_nc("CDE", build_cde)
    resCDE = run_bass_kernel_spmd(
        ncCDE, [{"x1o": x1d[c], "metaF": metaF, "cfc": cfc, "cpj": cpj,
                 "wqT": wqT, "wkT": wkT, "wvT": wvT, "woT": woT,
                 "w1T": w1T, "w2T": w2T, "cwL": cwL, "cbL": cbL}
                for c in cores], core_ids=cores)
    out = np.concatenate([resCDE.results[c]["out"] for c in cores], axis=0)
    return out.reshape(16, T, D).astype(np.float32, copy=False)
